# version 21
# baseline (speedup 1.0000x reference)
import numpy as np
import ml_dtypes

import concourse.bass as bass
import concourse.mybir as mybir
from concourse.bass_utils import run_bass_kernel_spmd

F32 = mybir.dt.float32
BF16 = mybir.dt.bfloat16
I32 = mybir.dt.int32
AF = mybir.ActivationFunctionType
ALU = mybir.AluOpType
AX = mybir.AxisListType
BFNP = ml_dtypes.bfloat16

D_MODEL = 1024
D_INNER = 2048
D_STATE = 16
DT_RANK = 64
D_CONV = 4
NDHI = D_INNER // 128
NFC = D_MODEL // 128
NHID = 4 * D_MODEL
EPS_LN = 1e-5
EPS_Q = 1e-5


class Buf:
    __slots__ = ("name", "h", "sem", "last_w", "readers", "dma_count",
                 "is_dram", "guard")

    def __init__(self, name, h, sem=None, is_dram=False, guard=None):
        self.name = name
        self.h = h
        self.sem = sem
        self.last_w = None
        self.readers = []
        self.dma_count = 0
        self.is_dram = is_dram
        self.guard = guard


class Op:
    __slots__ = ("idx", "eng", "fn", "deps", "needs_inc", "sem_val", "is_dma",
                 "dma_buf", "dma_val", "tag")

    def __init__(self, idx, eng, fn, tag=""):
        self.idx = idx
        self.eng = eng
        self.fn = fn
        self.deps = []
        self.needs_inc = False
        self.sem_val = None
        self.is_dma = False
        self.dma_buf = None
        self.dma_val = None
        self.tag = tag


class Prog:

    def __init__(self, nc):
        self.nc = nc
        self.ops = []
        self.bufs = []
        self.cur_tag = ""
        self.group_stack = []
        self.arena_join = None
        self.dma_counts = {}

    def sbuf(self, name, shape, dtype, dma_target=False, sem=None):
        if self.group_stack:
            cm = self.nc.sbuf_tensor(name, list(shape), dtype)
            h = cm.__enter__()
            b = Buf(name, h, guard=cm)
            self.group_stack[-1][0].append(cm)
            self.group_stack[-1][1].append(b)
            b.last_w = self.arena_join
        else:
            h = self.nc.alloc_sbuf_tensor(name, list(shape), dtype)
            b = Buf(name, h)
        if dma_target:
            b.sem = sem or name
        self.bufs.append(b)
        return b

    def open_group(self):
        self.group_stack.append(([], []))

    def close_group(self):
        guards, gbufs = self.group_stack.pop()
        deps = set()
        for b in gbufs:
            if b.last_w is not None:
                deps.add(b.last_w)
            for r in b.readers:
                deps.add(r)
        join = Op(len(self.ops), "v", None, self.cur_tag + "_join")
        join.fn = lambda e, s: e.nop()
        join.deps = list(deps)
        self.ops.append(join)
        self.arena_join = join
        for cm in reversed(guards):
            cm.__exit__(None, None, None)

    def psum(self, name, shape, dtype=F32):
        h = self.nc.alloc_psum_tensor(name, list(shape), dtype)
        b = Buf(name, h)
        self.bufs.append(b)
        return b

    def dram(self, name, shape, dtype, kind="Internal"):
        h = self.nc.dram_tensor(name, list(shape), dtype, kind=kind)
        b = Buf(name, h, is_dram=True)
        b.sem = name + "_dsem"
        self.bufs.append(b)
        return b

    def _dep(self, op, on):
        if on is not None and on is not op and on not in op.deps:
            op.deps.append(on)

    def _track(self, op, reads, writes):
        for b in reads:
            self._dep(op, b.last_w)
            b.readers.append(op)
        for b in writes:
            for r in b.readers:
                self._dep(op, r)
            self._dep(op, b.last_w)
            b.last_w = op
            b.readers = []

    def op(self, eng, fn, r=(), w=()):
        o = Op(len(self.ops), eng, fn, self.cur_tag)
        self._track(o, r, w)
        self.ops.append(o)
        return o

    def dma(self, eng, dst, dst_ap_fn, src, src_ap_fn, slow=False):
        assert dst.sem is not None, f"dma target {dst.name} needs a sem"
        o = Op(len(self.ops), eng, None, self.cur_tag)
        o.is_dma = True
        o.dma_buf = dst
        cnt = self.dma_counts.get(dst.sem, 0) + 1
        self.dma_counts[dst.sem] = cnt
        dst.dma_count += 1
        o.dma_val = 16 * cnt

        def fn(e, sems, _d=dst_ap_fn, _s=src_ap_fn, _o=o, _slow=slow):
            e.dma_start(out=_d(), in_=_s(),
                        allow_slow_non_contiguous=_slow
                        ).then_inc(sems[_o.dma_buf.sem], 16)
        o.fn = fn
        self._track(o, [src], [dst])
        self.ops.append(o)
        return o

    def barrier_on(self, eng, bufs):
        o = Op(len(self.ops), eng, lambda e, sems: None, self.cur_tag)
        for x in bufs:
            self._dep(o, x.last_w)
        self.ops.append(o)
        return o

    def finalize(self):
        nc = self.nc
        for o in self.ops:
            for d in o.deps:
                if not d.is_dma:
                    d.needs_inc = True
        eng_count = {}
        for o in self.ops:
            if o.needs_inc and not o.is_dma:
                eng_count[o.eng] = eng_count.get(o.eng, 0) + 1
                o.sem_val = eng_count[o.eng]
        sem_names = [f"E{e}" for e in "tvsyg"]
        sem_names += sorted(self.dma_counts.keys())
        ctxs = []
        sems = {}
        for snm in sem_names:
            cm = nc.semaphore(snm)
            sems[snm] = cm.__enter__()
            ctxs.append(cm)
        streams = {e: [o for o in self.ops if o.eng == e] for e in "tvsyg"}

        has_wait = {}
        for e, ops in streams.items():
            waited = {}
            for o in ops:
                hw = False
                for d in o.deps:
                    if d.is_dma:
                        snm, val = d.dma_buf.sem, d.dma_val
                    else:
                        snm, val = "E" + d.eng, d.sem_val
                    if waited.get(snm, -1) < val:
                        waited[snm] = val
                        hw = True
                has_wait[o.idx] = hw
        bump = {e: {} for e in "tvsyg"}
        for e, ops in streams.items():
            i = 0
            n = len(ops)
            while i < n:
                o = ops[i]
                if o.is_dma or not o.needs_inc:
                    i += 1
                    continue
                j = i
                vals = [ops[j].sem_val]
                while (j + 1 < n and ops[j + 1].needs_inc
                       and not ops[j + 1].is_dma and not has_wait[ops[j + 1].idx]):
                    j += 1
                    vals.append(ops[j].sem_val)
                end_val = vals[-1]
                for v in vals:
                    bump[e][v] = end_val
                i = j + 1

        blk_cm = nc.Block()
        block = blk_cm.__enter__()

        def make_body(eng_key):
            ops = streams[eng_key]

            def body(eng):
                waited = {}
                for o in ops:
                    for d in o.deps:
                        if d.is_dma:
                            snm, val = d.dma_buf.sem, d.dma_val
                        else:
                            snm = "E" + d.eng
                            val = bump[d.eng].get(d.sem_val, d.sem_val)
                        if waited.get(snm, -1) < val:
                            eng.wait_ge(sems[snm], val)
                            waited[snm] = val
                    inst = o.fn(eng, sems)
                    if o.needs_inc and not o.is_dma:
                        assert inst is not None, f"op {o.idx} [{o.tag}] returned None"
                        inst.then_inc(sems["E" + o.eng], 1)
            return body

        block.tensor(make_body("t"))
        block.vector(make_body("v"))
        block.scalar(make_body("s"))
        block.sync(make_body("y"))
        block.gpsimd(make_body("g"))
        blk_cm.__exit__(None, None, None)
        for cm in reversed(ctxs):
            cm.__exit__(None, None, None)


def build_program(T=4096, SEG=512, CHUNK=1024):
    SEG = min(SEG, T)
    CHUNK = min(CHUNK, T)
    nseg = T // SEG
    nchunk = T // CHUNK
    NTT = SEG // 128
    nc = bass.Bass("TRN2", target_bir_lowering=False)
    P = Prog(nc)

    x_in = P.dram("x", [T, D_MODEL], F32, kind="ExternalInput")
    out_d = P.dram("out", [T, D_MODEL], F32, kind="ExternalOutput")
    wdecl = {}
    for pre in ("fw", "bw"):
        wdecl[pre + "_WinT"] = P.dram(pre + "_WinT", [D_MODEL, 2 * D_INNER], F32, "ExternalInput")
        wdecl[pre + "_WxT"] = P.dram(pre + "_WxT", [D_INNER, DT_RANK + 2 * D_STATE], F32, "ExternalInput")
        wdecl[pre + "_WdtT"] = P.dram(pre + "_WdtT", [DT_RANK, D_INNER], F32, "ExternalInput")
        wdecl[pre + "_Wout_nat"] = P.dram(pre + "_Wout_nat", [D_MODEL, D_INNER], F32, "ExternalInput")
        wdecl[pre + "_convw"] = P.dram(pre + "_convw", [D_INNER, D_CONV], F32, "ExternalInput")
        wdecl[pre + "_convb"] = P.dram(pre + "_convb", [D_INNER], F32, "ExternalInput")
        wdecl[pre + "_bdt"] = P.dram(pre + "_bdt", [D_INNER], F32, "ExternalInput")
        wdecl[pre + "_Alog"] = P.dram(pre + "_Alog", [D_INNER, D_STATE], F32, "ExternalInput")
        wdecl[pre + "_D"] = P.dram(pre + "_D", [D_INNER], F32, "ExternalInput")
    fuseT_d = P.dram("fuseT", [2 * D_MODEL, D_MODEL], F32, "ExternalInput")
    fuseb_d = P.dram("fuse_b", [D_MODEL], F32, "ExternalInput")
    w1T_d = P.dram("w1T", [D_MODEL, NHID], F32, "ExternalInput")
    w2T_d = P.dram("w2T", [NHID, D_MODEL], F32, "ExternalInput")

    ident_d = Buf("ident_d", nc.inline_tensor(np.eye(128).astype(BFNP), "ident_const"),
                  sem="ident_dsem", is_dram=True)
    P.bufs.append(ident_d)

    mid_q = [P.dram(f"mid{q}", [T, 512], F32) for q in range(4)]
    spill = {}
    for pre in ("fw", "bw"):
        spill[pre + "_bc"] = P.dram(f"sp_{pre}_bc", [32, T], BF16)
        spill[pre + "_win"] = P.dram(f"sp_{pre}_win", [NFC, 128, 2 * D_INNER], BF16)
        spill[pre + "_wc"] = P.dram(f"sp_{pre}_wc", [NDHI, 128, D_MODEL], BF16)

    ident = P.sbuf("ident", [128, 128], BF16, dma_target=True)
    Acoef = {p: P.sbuf(f"A_{p}", [128, NDHI, D_STATE], F32, dma_target=True)
             for p in ("fw", "bw")}
    convw_s = {p: P.sbuf(f"convw_{p}", [128, NDHI, D_CONV], F32, dma_target=True) for p in ("fw", "bw")}
    convb_s = {p: P.sbuf(f"convb_{p}", [128, NDHI], F32, dma_target=True) for p in ("fw", "bw")}
    bdt_s = {p: P.sbuf(f"bdt_{p}", [128, NDHI], F32, dma_target=True) for p in ("fw", "bw")}
    Dp_s = {p: P.sbuf(f"Dp_{p}", [128, NDHI], F32, dma_target=True) for p in ("fw", "bw")}
    onesf = P.sbuf("onesf", [1, 128], F32)
    onesb = P.sbuf("onesb", [1, 128], BF16)
    ones_col = P.sbuf("ones_col", [128, 1], F32)
    cols = P.sbuf("colscratch", [128, 48], F32)
    colsB = P.sbuf("colscratchB", [128, 16], F32)
    hstate = P.sbuf("hstate", [128, NDHI, D_STATE], F32)
    lnout = [None, None]
    xt = [P.sbuf(f"xt{i}", [128, D_MODEL], F32, dma_target=True) for i in range(2)]
    epsc = P.sbuf("epsc", [128, 1], F32)

    pA = P.psum("pA", [128, 512], F32)
    pB = P.psum("pB", [128, 512], F32)
    pC1 = P.psum("pC1", [128, 512], F32)
    pC2 = P.psum("pC2", [128, 512], F32)
    pD = P.psum("pD", [128, 512], F32)
    pW = P.psum("pW", [128, 512], F32)
    pRS = P.psum("pRS", [96, 512], F32)
    pT = P.psum("pT", [128, 128], BF16)

    P.cur_tag = "prep"
    P.dma("y", ident, lambda: ident.h[:], ident_d, lambda: ident_d.h.ap())
    P.op("g", lambda e, s: e.memset(onesf.h[:], 1.0), w=[onesf])
    P.op("g", lambda e, s: e.memset(onesb.h[:], 1.0), w=[onesb])
    P.op("g", lambda e, s: e.memset(ones_col.h[:], 1.0), w=[ones_col])
    P.op("g", lambda e, s: e.memset(epsc.h[:], EPS_LN), w=[epsc])
    for pre in ("fw", "bw"):
        P.dma("y", Acoef[pre], lambda pre=pre: Acoef[pre].h[:],
              wdecl[pre + "_Alog"],
              lambda pre=pre: wdecl[pre + "_Alog"].h.ap().rearrange("(a p) s -> p a s", p=128))
        P.op("s", lambda e, s, pre=pre: e.activation(
            Acoef[pre].h[:], Acoef[pre].h[:], AF.Exp),
            r=[Acoef[pre]], w=[Acoef[pre]])
        P.op("v", lambda e, s, pre=pre: e.tensor_single_scalar(
            Acoef[pre].h[:], Acoef[pre].h[:], -1.0, ALU.mult),
            r=[Acoef[pre]], w=[Acoef[pre]])
        P.dma("y", convw_s[pre], lambda pre=pre: convw_s[pre].h[:],
              wdecl[pre + "_convw"],
              lambda pre=pre: wdecl[pre + "_convw"].h.ap().rearrange("(a p) k -> p a k", p=128))
        P.dma("y", convb_s[pre], lambda pre=pre: convb_s[pre].h[:],
              wdecl[pre + "_convb"],
              lambda pre=pre: wdecl[pre + "_convb"].h.ap().rearrange("(a p) -> p a", p=128), slow=True)
        P.dma("y", bdt_s[pre], lambda pre=pre: bdt_s[pre].h[:],
              wdecl[pre + "_bdt"],
              lambda pre=pre: wdecl[pre + "_bdt"].h.ap().rearrange("(a p) -> p a", p=128), slow=True)
        P.dma("y", Dp_s[pre], lambda pre=pre: Dp_s[pre].h[:],
              wdecl[pre + "_D"],
              lambda pre=pre: wdecl[pre + "_D"].h.ap().rearrange("(a p) -> p a", p=128), slow=True)

    def load_weights_cast(stages, dst, dram_buf, rows, ncol, tag):
        P.cur_tag = tag
        if not isinstance(stages, list):
            stages = [stages]
        nchk = rows // 128
        scols = stages[0].h.ap().shape[1]
        i = 0
        for c in range(nchk):
            for j0 in range(0, ncol, scols):
                j1 = min(j0 + scols, ncol)
                stage = stages[i % len(stages)]
                i += 1
                P.dma("y", stage,
                      lambda stage=stage, j0=j0, j1=j1: stage.h[:, 0:j1 - j0],
                      dram_buf, lambda c=c, j0=j0, j1=j1: dram_buf.h.ap()
                      .rearrange("(a p) n -> p a n", p=128)[:, c, j0:j1])
                P.op("v", lambda e, s, stage=stage, c=c, j0=j0, j1=j1: e.tensor_copy(
                    dst.h[:, c, j0:j1], stage.h[:, 0:j1 - j0]), r=[stage], w=[dst])

    lnsq = [P.sbuf("lnsq0", [128, D_MODEL], F32)] * 2

    def ln_dma(seg, tt, tag):
        P.cur_tag = tag
        t0 = seg * SEG
        xtp = xt[tt % 2]
        P.dma("y", xtp, lambda xtp=xtp: xtp.h[:],
              x_in, lambda t0=t0, tt=tt: x_in.h.ap()
              [t0 + tt * 128:t0 + (tt + 1) * 128, :])

    def emit_ln_tile(seg, tt, tag, out=None, skip_dma=False):
        if not skip_dma:
            ln_dma(seg, tt, tag)
        P.cur_tag = tag
        if True:
            p = tt % 2
            c0 = 16 * p
            xtp, sq = xt[p], lnsq[p]
            lo = out if out is not None else lnout[p]
            P.op("v", lambda e, s, xtp=xtp, c0=c0: e.reduce_sum(
                cols.h[:, c0:c0 + 1], xtp.h[:], axis=AX.X), r=[xtp], w=[cols])
            P.op("s", lambda e, s, xtp=xtp, sq=sq: e.activation(
                sq.h[:], xtp.h[:], AF.Square), r=[xtp], w=[sq])
            P.op("v", lambda e, s, sq=sq, c0=c0: e.reduce_sum(
                cols.h[:, c0 + 1:c0 + 2], sq.h[:], axis=AX.X),
                r=[sq], w=[cols])
            P.op("v", lambda e, s, c0=c0: e.tensor_single_scalar(
                cols.h[:, c0 + 2:c0 + 3], cols.h[:, c0:c0 + 1], 1.0 / D_MODEL, ALU.mult),
                r=[cols], w=[cols])
            P.op("v", lambda e, s, c0=c0: e.tensor_mul(
                cols.h[:, c0 + 3:c0 + 4], cols.h[:, c0 + 2:c0 + 3],
                cols.h[:, c0 + 2:c0 + 3]), r=[cols], w=[cols])
            P.op("v", lambda e, s, c0=c0: e.tensor_single_scalar(
                cols.h[:, c0 + 4:c0 + 5], cols.h[:, c0 + 1:c0 + 2], 1.0 / D_MODEL,
                ALU.mult), r=[cols], w=[cols])
            P.op("v", lambda e, s, c0=c0: e.tensor_sub(
                cols.h[:, c0 + 5:c0 + 6], cols.h[:, c0 + 4:c0 + 5],
                cols.h[:, c0 + 3:c0 + 4]), r=[cols], w=[cols])
            P.op("s", lambda e, s, c0=c0: e.activation(
                cols.h[:, c0 + 6:c0 + 7], cols.h[:, c0 + 5:c0 + 6],
                AF.Ln, bias=epsc.h[:], scale=1.0), r=[cols, epsc], w=[cols])
            P.op("s", lambda e, s, c0=c0: e.activation(
                cols.h[:, c0 + 7:c0 + 8], cols.h[:, c0 + 6:c0 + 7],
                AF.Exp, bias=0.0, scale=-0.5), r=[cols], w=[cols])
            P.op("v", lambda e, s, c0=c0: e.tensor_single_scalar(
                cols.h[:, c0 + 8:c0 + 9], cols.h[:, c0 + 2:c0 + 3], -1.0, ALU.mult),
                r=[cols], w=[cols])
            P.op("v", lambda e, s, xtp=xtp, lo=lo, c0=c0: e.tensor_scalar(
                lo.h[:], xtp.h[:], cols.h[:, c0 + 8:c0 + 9], cols.h[:, c0 + 7:c0 + 8],
                ALU.add, ALU.mult), r=[xtp, cols], w=[lo])

    def emit_ln(seg, tag, per_tile):
        for tt in range(NTT):
            emit_ln_tile(seg, tt, tag)
            per_tile(tt)

    NHC = 4
    NG = NDHI // 2

    def sweep_ABC(pre):
        P.open_group()
        P.cur_tag = f"M{pre}_alloc"
        sfx = "_" + pre
        rev = (pre == "bw")
        nc_wx = DT_RANK + 2 * D_STATE

        wxT = P.sbuf("wxT" + sfx, [128, NDHI, nc_wx], BF16)
        wdtT = P.sbuf("wdtT" + sfx, [DT_RANK, D_INNER], BF16)
        fbias = P.sbuf("fbias", [1, D_MODEL], BF16) if rev else None

        P.open_group()
        pstg = [P.sbuf(f"pstg{sfx}{i}", [128, 2048], F32, dma_target=True,
                       sem=f"pstg{i}") for i in range(2)]
        wtmp = [P.sbuf(f"wtmp{sfx}{i}", [128, 2048], BF16) for i in range(2)]
        P.cur_tag = f"M{pre}_wprep"
        k = 0
        for kc in range(NFC):
            for j0 in (0, 2048):
                stg, wt = pstg[k % 2], wtmp[k % 2]
                k += 1
                P.dma("y", stg, lambda stg=stg: stg.h[:],
                      wdecl[pre + "_WinT"], lambda kc=kc, j0=j0, pre=pre:
                      wdecl[pre + "_WinT"].h.ap()
                      .rearrange("(a p) n -> p a n", p=128)[:, kc, j0:j0 + 2048])
                P.op("v", lambda e, s, wt=wt, stg=stg: e.tensor_copy(
                    wt.h[:], stg.h[:]), r=[stg], w=[wt])
                P.dma("g", spill[pre + "_win"],
                      lambda kc=kc, j0=j0, pre=pre:
                      spill[pre + "_win"].h.ap()[kc, :, j0:j0 + 2048],
                      wt, lambda wt=wt: wt.h[:])
        wnat = P.sbuf("wnat" + sfx, [128, NFC, D_INNER], BF16)
        fuT = P.sbuf("fuT" + sfx, [128, NFC, D_MODEL], BF16)
        r0 = 0 if pre == "fw" else D_MODEL
        for mk in range(NFC):
            stg = pstg[mk % 2]
            P.dma("y", stg, lambda stg=stg: stg.h[:],
                  wdecl[pre + "_Wout_nat"], lambda mk=mk, pre=pre:
                  wdecl[pre + "_Wout_nat"].h.ap()
                  .rearrange("(a p) n -> p a n", p=128)[:, mk, :])
            P.op("v", lambda e, s, stg=stg, mk=mk: e.tensor_copy(
                wnat.h[:, mk, :], stg.h[:]), r=[stg], w=[wnat])
            stg2 = pstg[(mk + 1) % 2]
            P.dma("y", stg2, lambda stg2=stg2: stg2.h[:, 0:D_MODEL],
                  fuseT_d, lambda mk=mk, r0=r0: fuseT_d.h.ap()[r0:r0 + D_MODEL, :]
                  .rearrange("(a p) n -> p a n", p=128)[:, mk, :])
            P.op("v", lambda e, s, stg2=stg2, mk=mk: e.tensor_copy(
                fuT.h[:, mk, :], stg2.h[:, 0:D_MODEL]), r=[stg2], w=[fuT])
        wc_t = [P.sbuf(f"wc_t{sfx}{i}", [128, 512], BF16) for i in range(2)]
        for dc in range(NDHI):
            for half in range(2):
                ps = [pA, pB][(dc * 2 + half) % 2]
                for mk in range(NFC):
                    P.op("t", lambda e, s, ps=ps, dc=dc, half=half, mk=mk: e.matmul(
                        ps.h[:, 0:512], wnat.h[:, mk, dc * 128:(dc + 1) * 128],
                        fuT.h[:, mk, half * 512:(half + 1) * 512],
                        start=(mk == 0), stop=(mk == NFC - 1)),
                        r=[wnat, fuT], w=[ps])
                wct = wc_t[half]
                P.op("v", lambda e, s, ps=ps, wct=wct: e.tensor_copy(
                    wct.h[:], ps.h[:, 0:512]), r=[ps], w=[wct])
                P.dma("g", spill[pre + "_wc"],
                      lambda dc=dc, half=half, pre=pre:
                      spill[pre + "_wc"].h.ap()[dc, :, half * 512:(half + 1) * 512],
                      wct, lambda wct=wct: wct.h[:])
        P.cur_tag = f"M{pre}_w"
        for c in range(NDHI):
            stg = pstg[c % 2]
            P.dma("y", stg, lambda stg=stg: stg.h[:, 0:nc_wx],
                  wdecl[pre + "_WxT"], lambda c=c, pre=pre: wdecl[pre + "_WxT"].h.ap()
                  .rearrange("(a p) n -> p a n", p=128)[:, c, :])
            P.op("v", lambda e, s, stg=stg, c=c: e.tensor_copy(
                wxT.h[:, c, :], stg.h[:, 0:nc_wx]), r=[stg], w=[wxT])
        P.dma("y", pstg[0], lambda: pstg[0].h[0:DT_RANK, 0:D_INNER],
              wdecl[pre + "_WdtT"], lambda pre=pre: wdecl[pre + "_WdtT"].h.ap())
        P.op("v", lambda e, s: e.tensor_copy(
            wdtT.h[:], pstg[0].h[0:DT_RANK, 0:D_INNER]), r=[pstg[0]], w=[wdtT])
        if rev:
            P.dma("y", pstg[1], lambda: pstg[1].h[0:1, 0:D_MODEL],
                  fuseb_d, lambda: fuseb_d.h.ap().rearrange("(a n) -> a n", a=1))
            P.op("v", lambda e, s: e.tensor_copy(
                fbias.h[:], pstg[1].h[0:1, 0:D_MODEL]), r=[pstg[1]], w=[fbias])
        P.close_group()

        P.cur_tag = f"M{pre}_alloc"
        winsl = [P.sbuf(f"winsl{sfx}{i}", [128, NFC, 256], BF16, dma_target=True,
                        sem=f"winsl{i}") for i in range(2)]
        wq = [P.sbuf(f"wq{sfx}{i}", [128, NDHI, 256], BF16, dma_target=True,
                     sem=f"wq{i}") for i in range(2)]
        xnT = P.sbuf("xnT" + sfx, [128, NFC, SEG], BF16)
        xnbt = [P.sbuf("xnbt" + sfx, [128, D_MODEL], BF16)] * 2
        xsb = [P.sbuf(f"xsb{sfx}{d}", [128, SEG + 3], BF16) for d in range(NDHI)]
        ub = [P.sbuf(f"ub{sfx}{d}", [128, SEG], BF16) for d in range(NDHI)]
        szb = [P.sbuf(f"szb{sfx}{d}", [128, SEG], BF16) for d in range(NDHI)]
        y2sg = [[P.sbuf(f"y2s{sfx}{pp}_{d}", [128, SEG], BF16)
                 for d in range(NDHI)] for pp in range(2)]
        BrepS = [P.sbuf(f"Brep{sfx}{st}", [128, SEG], BF16, dma_target=True,
                        sem=f"Brep{st}") for st in range(D_STATE)]
        CrepS = [P.sbuf(f"Crep{sfx}{st}", [128, SEG], BF16, dma_target=True,
                        sem=f"Crep{st}") for st in range(D_STATE)]
        dtbc = [P.sbuf(f"dtbc{sfx}{i}", [DT_RANK + 32, SEG], BF16)
                for i in range(2)]
        dtcA = P.sbuf("dtcA" + sfx, [128, SEG], F32)
        dtcB = P.sbuf("dtcB" + sfx, [128, SEG], F32)
        convf = dtcB
        d1g = [P.sbuf(f"d1g{sfx}{i}", [128, 2, SEG], BF16) for i in range(2)]
        dug = [P.sbuf(f"dug{sfx}{i}", [128, 2, SEG], BF16) for i in range(2)]
        a_sl = [P.sbuf(f"a_sl{sfx}{i}", [128, SEG], BF16) for i in range(2)]
        b_sl = [P.sbuf(f"b_sl{sfx}{i}", [128, SEG], BF16) for i in range(2)]
        h_sl = [P.sbuf(f"h_sl{sfx}{i}", [128, SEG], BF16) for i in range(2)]
        hc_sl = [P.sbuf(f"hc_sl{sfx}{i}", [128, SEG], BF16) for i in range(NHC)]
        tq_f32 = dtcA
        ydt = [a_sl[0]] * 2
        mid_ts = [P.sbuf(f"mid_ts{sfx}{i}", [128, 512], F32, dma_target=True,
                         sem=f"mid_ts{i}") for i in range(2)]

        if pre == "fw":
            xs_off = 3
            taps = [(0, 0), (1, 1), (2, 2), (3, 3)]
        else:
            xs_off = 0
            taps = [(3, 0), (2, 1), (1, 2), (0, 3)]
        carry_slc = slice(0, 3) if pre == "fw" else slice(SEG, SEG + 3)
        P.op("g", lambda e, s: e.memset(hstate.h[:], 0.0), w=[hstate])
        segs = list(range(nseg)) if pre == "fw" else list(range(nseg - 1, -1, -1))

        def load_slab(j):
            sl = winsl[j % 2]
            P.dma("y", sl, lambda sl=sl: sl.h[:],
                  spill[pre + "_win"], lambda j=j, pre=pre:
                  spill[pre + "_win"].h.ap()[:, :, 256 * j:256 * j + 256]
                  .rearrange("a p n -> p a n"))

        def ln_chunk(seg, tt):
            def emit():
                p = tt % 2
                xb = xnbt[p]
                emit_ln_tile(seg, tt, f"M{pre}_ln{seg}", out=xb, skip_dma=True)
                if tt + 2 < NTT:
                    ln_dma(seg, tt + 2, f"M{pre}_ln{seg}")
                for fc in range(NFC):
                    P.op("t", lambda e, s, xb=xb, fc=fc: e.transpose(
                        pT.h[:], xb.h[:, fc * 128:(fc + 1) * 128], ident.h[:]),
                        r=[xb, ident], w=[pT])
                    P.op("v", lambda e, s, tt=tt, fc=fc: e.tensor_copy(
                        xnT.h[:, fc, tt * 128:(tt + 1) * 128], pT.h[:]),
                        r=[pT], w=[xnT])
            return emit

        def x_chunk(seg, dhi, first):
            ps = [pA, pB][dhi % 2]
            xb = xsb[dhi]

            def emit_mm():
                P.cur_tag = f"M{pre}_xz{seg}"
                sl = winsl[(dhi // 2) % 2]
                jo = dhi % 2
                for kc in range(NFC):
                    P.op("t", lambda e, s, sl=sl, jo=jo, kc=kc: e.matmul(
                        ps.h[:, 0:SEG], sl.h[:, kc, jo * 128:(jo + 1) * 128],
                        xnT.h[:, kc, :], start=(kc == 0), stop=(kc == NFC - 1)),
                        r=[sl, xnT], w=[ps])
                if not first:
                    if pre == "fw":
                        P.op("v", lambda e, s: e.tensor_copy(
                            xb.h[:, 0:3], xb.h[:, SEG:SEG + 3]), r=[xb], w=[xb])
                    else:
                        P.op("v", lambda e, s: e.tensor_copy(
                            xb.h[:, SEG:SEG + 3], xb.h[:, 0:3]), r=[xb], w=[xb])
                if dhi % 2 == 1 and dhi // 2 + 2 < NFC:
                    load_slab(dhi // 2 + 2)

            def emit_ev():
                P.cur_tag = f"M{pre}_xz{seg}"
                P.op("v", lambda e, s: e.tensor_copy(
                    xb.h[:, xs_off:xs_off + SEG], ps.h[:, 0:SEG]), r=[ps], w=[xb])
            return emit_mm, emit_ev

        def conv_taps(seg, dhi):
            P.cur_tag = f"M{pre}_cv{seg}"
            xb = xsb[dhi]
            k0, o0 = taps[0]
            P.op("v", lambda e, s, xb=xb, dhi=dhi, k0=k0, o0=o0: e.tensor_scalar(
                convf.h[:], xb.h[:, o0:o0 + SEG],
                convw_s[pre].h[:, dhi, k0:k0 + 1], None, ALU.mult),
                r=[xb, convw_s[pre]], w=[convf])
            for kk, off in taps[1:]:
                P.op("v", lambda e, s, xb=xb, dhi=dhi, kk=kk, off=off:
                     e.scalar_tensor_tensor(
                         convf.h[:], xb.h[:, off:off + SEG],
                         convw_s[pre].h[:, dhi, kk:kk + 1], convf.h[:],
                         ALU.mult, ALU.add),
                     r=[xb, convw_s[pre], convf], w=[convf])

        def conv_silu(seg, dhi):
            P.cur_tag = f"M{pre}_cv{seg}"
            u = ub[dhi]
            P.op("s", lambda e, s, u=u, dhi=dhi: e.activation(
                u.h[:], convf.h[:], AF.Silu,
                bias=convb_s[pre].h[:, dhi:dhi + 1], scale=1.0),
                r=[convf, convb_s[pre]], w=[u])

        def xdbl_mm(seg, dhi):
            P.cur_tag = f"M{pre}_cv{seg}"
            u = ub[dhi]
            P.op("t", lambda e, s, u=u, dhi=dhi: e.matmul(
                pRS.h[:, 0:SEG], wxT.h[:, dhi, :], u.h[:],
                start=(dhi == 0), stop=(dhi == NDHI - 1)),
                r=[wxT, u], w=[pRS])

        def cxd_pair(seg, g):
            for dhi in (2 * g, 2 * g + 1):
                conv_taps(seg, dhi)
                conv_silu(seg, dhi)
                xdbl_mm(seg, dhi)

        def z_pair(seg, p, via_wq=False):
            P.cur_tag = f"M{pre}_z{seg}"
            if via_wq:
                sl, ro = wq[(p % 4) // 2], (p % 2) * NFC
            else:
                sl, ro = winsl[p % 2], 0
            for jo in range(2):
                zi = 2 * p + jo
                ps = [pA, pB][jo]
                for kc in range(NFC):
                    P.op("t", lambda e, s, ps=ps, sl=sl, ro=ro, jo=jo, kc=kc:
                         e.matmul(
                             ps.h[:, 0:SEG], sl.h[:, ro + kc, jo * 128:(jo + 1) * 128],
                             xnT.h[:, kc, :], start=(kc == 0), stop=(kc == NFC - 1)),
                         r=[sl, xnT], w=[ps])
                P.op("s", lambda e, s, ps=ps, zi=zi: e.activation(
                    szb[zi].h[:], ps.h[:, 0:SEG], AF.Silu),
                    r=[ps], w=[szb[zi]])

        def z_load_wq(p):
            b, ro = wq[(p % 4) // 2], (p % 2) * NFC
            j = NFC + p
            P.dma("y", b, lambda b=b, ro=ro: b.h[:, ro:ro + NFC, :],
                  spill[pre + "_win"], lambda j=j, pre=pre:
                  spill[pre + "_win"].h.ap()[:, :, 256 * j:256 * j + 256]
                  .rearrange("a p n -> p a n"))

        def chains_wdt(i, g, j):
            P.cur_tag = f"M{pre}_dt"
            dhi = 2 * g + j
            dpb = dtbc[i % 2]
            P.op("t", lambda e, s, dhi=dhi, dpb=dpb: e.matmul(
                pD.h[:, 0:SEG], wdtT.h[:, dhi * 128:(dhi + 1) * 128],
                dpb.h[0:DT_RANK, :], start=True, stop=True),
                r=[wdtT, dpb], w=[pD])

        def chains_sc(i, g, j):
            P.cur_tag = f"M{pre}_dt"
            dhi = 2 * g + j
            P.op("s", lambda e, s, dhi=dhi: e.activation(
                dtcA.h[:], pD.h[:, 0:SEG], AF.Identity,
                bias=bdt_s[pre].h[:, dhi:dhi + 1], scale=1.0),
                r=[pD, bdt_s[pre]], w=[dtcA])
            P.op("s", lambda e, s: e.activation(
                dtcB.h[:], dtcA.h[:], AF.Abs), r=[dtcA], w=[dtcB])
            P.op("s", lambda e, s: e.activation(
                dtcB.h[:], dtcB.h[:], AF.Exp, bias=0.0, scale=-1.0),
                r=[dtcB], w=[dtcB])
            P.op("s", lambda e, s: e.activation(
                dtcB.h[:], dtcB.h[:], AF.Ln, bias=1.0, scale=1.0),
                r=[dtcB], w=[dtcB])
            P.op("s", lambda e, s: e.activation(
                dtcA.h[:], dtcA.h[:], AF.Relu), r=[dtcA], w=[dtcA])

        def chains_v(i, g, j):
            P.cur_tag = f"M{pre}_dt"
            gp = g % 2
            dhi = 2 * g + j
            P.op("v", lambda e, s, gp=gp, j=j: e.tensor_add(
                d1g[gp].h[:, j, :], dtcA.h[:], dtcB.h[:]),
                r=[dtcA, dtcB], w=[d1g[gp]])
            P.op("v", lambda e, s, gp=gp, j=j, dhi=dhi: e.tensor_mul(
                dug[gp].h[:, j, :], d1g[gp].h[:, j, :], ub[dhi].h[:]),
                r=[d1g[gp], ub[dhi]], w=[dug[gp]])

        def chains(i, g):
            for j in range(2):
                chains_wdt(i, g, j)
                chains_sc(i, g, j)
                chains_v(i, g, j)

        def boundary(i, seg):
            P.cur_tag = f"M{pre}_bc{seg}"
            t0 = seg * SEG
            bc = dtbc[i % 2]
            P.op("v", lambda e, s, bc=bc: e.tensor_copy(
                bc.h[:], pRS.h[0:DT_RANK + 32, 0:SEG]), r=[pRS], w=[bc])
            P.dma("g", spill[pre + "_bc"],
                  lambda t0=t0, pre=pre: spill[pre + "_bc"].h.ap()[:, t0:t0 + SEG],
                  bc, lambda bc=bc: bc.h[DT_RANK:DT_RANK + 32, :])
            for st in range(D_STATE):
                P.dma("y", BrepS[st], lambda st=st: BrepS[st].h[:],
                      spill[pre + "_bc"], lambda st=st, t0=t0, pre=pre:
                      spill[pre + "_bc"].h.ap()[st:st + 1, t0:t0 + SEG]
                      .broadcast_to([128, SEG]))
                P.dma("y", CrepS[st], lambda st=st: CrepS[st].h[:],
                      spill[pre + "_bc"], lambda st=st, t0=t0, pre=pre:
                      spill[pre + "_bc"].h.ap()[16 + st:17 + st, t0:t0 + SEG]
                      .broadcast_to([128, SEG]))
            chains(i, 0)

        def y2f(i, g):
            P.cur_tag = f"M{pre}_y2"
            for j, dhi in enumerate((2 * g, 2 * g + 1)):
                pc = [pC1, pC2][j]
                yb = ydt[j]
                y2 = y2sg[i % 2][dhi]
                P.op("s", lambda e, s, yb=yb, dhi=dhi: e.activation(
                    yb.h[:], ub[dhi].h[:], AF.Identity,
                    bias=0.0, scale=Dp_s[pre].h[:, dhi:dhi + 1]),
                    r=[ub[dhi], Dp_s[pre]], w=[yb])
                P.op("v", lambda e, s, pc=pc, yb=yb: e.tensor_add(
                    tq_f32.h[:], pc.h[:, 0:SEG], yb.h[:]), r=[pc, yb], w=[tq_f32])
                P.op("v", lambda e, s, y2=y2, dhi=dhi: e.tensor_mul(
                    y2.h[:], tq_f32.h[:], szb[dhi].h[:]),
                    r=[tq_f32, szb[dhi]], w=[y2])

        def scan_group(i, seg, g, side, nxt):
            P.cur_tag = f"M{pre}_sc{seg}"
            gp = g % 2
            for st in range(D_STATE):
                for j, dhi in enumerate((2 * g, 2 * g + 1)):
                    ip = (2 * st + j) % 2
                    hci = (2 * st + j) % NHC
                    pc = [pC1, pC2][j]
                    asl, bsl, hsl, hcs = a_sl[ip], b_sl[ip], h_sl[ip], hc_sl[hci]
                    P.op("s", lambda e, s, asl=asl, gp=gp, j=j, dhi=dhi, st=st:
                         e.activation(
                             asl.h[:], d1g[gp].h[:, j, :], AF.Exp,
                             bias=0.0, scale=Acoef[pre].h[:, dhi, st:st + 1]),
                         r=[d1g[gp], Acoef[pre]], w=[asl])
                    P.op("v", lambda e, s, bsl=bsl, gp=gp, j=j, st=st: e.tensor_mul(
                        bsl.h[:], dug[gp].h[:, j, :], BrepS[st].h[:]),
                        r=[dug[gp], BrepS[st]], w=[bsl])
                    if not rev:
                        P.op("v", lambda e, s, asl=asl, bsl=bsl, hsl=hsl,
                             dhi=dhi, st=st: e.tensor_tensor_scan(
                                 hsl.h[:], asl.h[:], bsl.h[:],
                                 hstate.h[:, dhi, st:st + 1], ALU.mult, ALU.add),
                             r=[asl, bsl, hstate], w=[hsl])
                        P.op("s", lambda e, s, hsl=hsl, dhi=dhi, st=st: e.activation(
                            hstate.h[:, dhi, st:st + 1], hsl.h[:, SEG - 1:SEG],
                            AF.Identity), r=[hsl], w=[hstate])
                    else:
                        P.op("v", lambda e, s, asl=asl, bsl=bsl, hsl=hsl,
                             dhi=dhi, st=st: e.tensor_tensor_scan(
                                 hsl.h[:, ::-1], asl.h[:, ::-1], bsl.h[:, ::-1],
                                 hstate.h[:, dhi, st:st + 1], ALU.mult, ALU.add),
                             r=[asl, bsl, hstate], w=[hsl])
                        P.op("s", lambda e, s, hsl=hsl, dhi=dhi, st=st: e.activation(
                            hstate.h[:, dhi, st:st + 1], hsl.h[:, 0:1],
                            AF.Identity), r=[hsl], w=[hstate])
                    P.op("v", lambda e, s, hsl=hsl, hcs=hcs, st=st: e.tensor_mul(
                        hcs.h[:], hsl.h[:], CrepS[st].h[:]),
                        r=[hsl, CrepS[st]], w=[hcs])
                    P.op("t", lambda e, s, pc=pc, hcs=hcs, st=st: e.matmul(
                        pc.h[:, 0:SEG], ident.h[:], hcs.h[:],
                        start=(st == 0), stop=(st == D_STATE - 1)),
                        r=[ident, hcs], w=[pc])
                if st == 1:
                    if g + 1 < NG:
                        chains_wdt(i, g + 1, 0)
                    if nxt is not None and g >= 1:
                        conv_silu(nxt, 2 * (g - 1))
                        conv_taps(nxt, 2 * (g - 1) + 1)
                elif st == 2 and nxt is not None and g >= 1:
                    conv_silu(nxt, 2 * (g - 1) + 1)
                    xdbl_mm(nxt, 2 * (g - 1))
                    xdbl_mm(nxt, 2 * (g - 1) + 1)
                elif st == 5 and g + 1 < NG:
                    chains_sc(i, g + 1, 0)
                elif st == 8 and g + 1 < NG:
                    chains_wdt(i, g + 1, 1)
                elif st == 9 and g + 1 < NG:
                    chains_v(i, g + 1, 0)
                elif st == 12 and g + 1 < NG:
                    chains_sc(i, g + 1, 1)
                if (st % 2 == 1 or st in (4, 6, 10)) and side:
                    side.pop(0)()
                P.cur_tag = f"M{pre}_sc{seg}"
            if g + 1 < NG:
                chains_v(i, g + 1, 1)
            y2f(i, g)

        def wcomb_chunks(ip, seg):
            t0p = seg * SEG
            y2p = y2sg[ip % 2]
            out = []
            for h in range(2):
                for tt in range(NTT):
                    def c1(h=h, tt=tt):
                        P.cur_tag = f"M{pre}_wc{seg}"
                        if tt == 0:
                            for q in range(2):
                                wqb = wq[q]
                                for dc in range(NDHI):
                                    P.dma("y", wqb,
                                          lambda wqb=wqb, dc=dc: wqb.h[:, dc, :],
                                          spill[pre + "_wc"],
                                          lambda dc=dc, h=h, q=q, pre=pre:
                                          spill[pre + "_wc"].h.ap()
                                          [dc, :, 512 * h + 256 * q:
                                           512 * h + 256 * q + 256])
                        for dc in range(NDHI):
                            P.op("t", lambda e, s, dc=dc, tt=tt: e.matmul(
                                pW.h[:, 0:256],
                                y2p[dc].h[:, tt * 128:(tt + 1) * 128],
                                wq[0].h[:, dc, :], start=(dc == 0),
                                stop=(dc == NDHI - 1 and not rev)),
                                r=[y2p[dc], wq[0]], w=[pW])
                        if rev:
                            P.op("t", lambda e, s, h=h: e.matmul(
                                pW.h[:, 0:256], onesb.h[:],
                                fbias.h[0:1, 512 * h:512 * h + 256],
                                start=False, stop=True),
                                r=[onesb, fbias], w=[pW])

                    def c2m(h=h, tt=tt):
                        P.cur_tag = f"M{pre}_wc{seg}"
                        for dc in range(NDHI):
                            P.op("t", lambda e, s, dc=dc, tt=tt: e.matmul(
                                pW.h[:, 256:512],
                                y2p[dc].h[:, tt * 128:(tt + 1) * 128],
                                wq[1].h[:, dc, :], start=(dc == 0),
                                stop=(dc == NDHI - 1 and not rev)),
                                r=[y2p[dc], wq[1]], w=[pW])
                        if rev:
                            P.op("t", lambda e, s, h=h: e.matmul(
                                pW.h[:, 256:512], onesb.h[:],
                                fbias.h[0:1, 512 * h + 256:512 * h + 512],
                                start=False, stop=True),
                                r=[onesb, fbias], w=[pW])
                        mts = mid_ts[tt % 2]
                        if pre == "fw":
                            P.dma("y", mts, lambda mts=mts: mts.h[:],
                                  x_in, lambda t0p=t0p, tt=tt, h=h:
                                  x_in.h.ap()[t0p + tt * 128:t0p + (tt + 1) * 128,
                                              512 * h:512 * h + 512])
                        else:
                            mq = mid_q[2 * h + tt % 2]
                            P.dma("y", mts, lambda mts=mts: mts.h[:],
                                  mq, lambda t0p=t0p, tt=tt, mq=mq:
                                  mq.h.ap()
                                  [t0p + tt * 128:t0p + (tt + 1) * 128, :])

                    def c2v(h=h, tt=tt):
                        P.cur_tag = f"M{pre}_wc{seg}"
                        mts = mid_ts[tt % 2]
                        P.op("v", lambda e, s, mts=mts: e.tensor_add(
                            mts.h[:], mts.h[:], pW.h[:, 0:512]),
                            r=[pW, mts], w=[mts])
                        mq = mid_q[2 * h + tt % 2]
                        P.dma("g", mq,
                              lambda t0p=t0p, tt=tt, mq=mq: mq.h.ap()
                              [t0p + tt * 128:t0p + (tt + 1) * 128, :],
                              mts, lambda mts=mts: mts.h[:])
                    out.append(c1)
                    out.append(c2m)
                    out.append(c2v)
            return out

        for i, seg in enumerate(segs):
            nxt = segs[i + 1] if i + 1 < nseg else None
            prv = segs[i - 1] if i >= 1 else None

            if i == 0:
                P.cur_tag = f"M{pre}_pro"
                for dhi in range(NDHI):
                    P.op("g", lambda e, s, dhi=dhi: e.memset(
                        xsb[dhi].h[:, carry_slc], 0.0), w=[xsb[dhi]])
                load_slab(0)
                load_slab(1)
                ln_dma(seg, 0, f"M{pre}_pro")
                ln_dma(seg, 1, f"M{pre}_pro")
                for tt in range(NTT):
                    ln_chunk(seg, tt)()
                for dhi in range(NDHI):
                    mm, ev = x_chunk(seg, dhi, True)
                    mm()
                    ev()
                for g in range(NG):
                    cxd_pair(seg, g)
                for p in range(NG):
                    load_slab(NFC + p)
                    z_pair(seg, p)
                if nxt is not None:
                    ln_dma(nxt, 0, f"M{pre}_pro")
                    ln_dma(nxt, 1, f"M{pre}_pro")
            boundary(i, seg)

            side = []
            if nxt is not None:
                side = [ln_chunk(nxt, tt) for tt in range(NTT)]
                side.append(lambda: (load_slab(0), load_slab(1)))
                pend = []
                for k in range(NDHI):
                    mm, ev = x_chunk(nxt, k, False)
                    side.append(mm)
                    pend.append(ev)
                    if k >= 1:
                        side.append(pend.pop(0))
                side += pend
            w_ = wcomb_chunks(i - 1, prv) if prv is not None else []
            merged = []
            wi = 0
            for idx, p_ in enumerate(side):
                merged.append(p_)
                if idx >= 12 and idx % 3 == 2 and wi < len(w_):
                    merged.append(w_[wi])
                    wi += 1
            side = merged + w_[wi:]

            for g in range(NG):
                scan_group(i, seg, g, side, nxt)
                if nxt is not None:
                    conv_taps(nxt, 2 * g)
                    if g == 5:
                        for p in range(4):
                            z_load_wq(p)
                    elif g == 6:
                        if i + 2 < nseg:
                            ln_dma(segs[i + 2], 0, f"M{pre}_ln")
                            ln_dma(segs[i + 2], 1, f"M{pre}_ln")
                        for p in range(4):
                            z_pair(nxt, p, via_wq=True)
                        for p in range(4, 8):
                            z_load_wq(p)
                    elif g == 7:
                        for p in range(4, 8):
                            z_pair(nxt, p, via_wq=True)
            if nxt is not None:
                conv_silu(nxt, NDHI - 2)
                conv_taps(nxt, NDHI - 1)
                conv_silu(nxt, NDHI - 1)
                xdbl_mm(nxt, NDHI - 2)
                xdbl_mm(nxt, NDHI - 1)
            while side:
                side.pop(0)()
        for c in wcomb_chunks(nseg - 1, segs[-1]):
            c()
        P.close_group()

    def sweep_D():
        P.open_group()
        P.cur_tag = "D_alloc"
        lnout[0] = P.sbuf("lnout0", [128, D_MODEL], F32)
        lnout[1] = P.sbuf("lnout1", [128, D_MODEL], F32)
        w1t = P.sbuf("w1t", [128, NFC, NHID], BF16)
        w2t = P.sbuf("w2t", [128, NHID // 128, D_MODEL], BF16)
        xq1 = [P.sbuf("xq1_0", [128, 512], BF16)] * 2
        xq2 = [P.sbuf(f"xq2_{i}", [128, 512], BF16) for i in range(2)]
        i32q = P.sbuf("i32q", [128, 512], I32)
        rnd_a = P.sbuf("rnd_a", [128, 512], F32)
        rnd_b = P.sbuf("rnd_b", [128, 512], F32)

        def emit_round(src_fn, src_bufs, out_buf, out_fn, n):
            assert n <= 512
            P.op("v", lambda e, s: e.tensor_copy(i32q.h[:, 0:n], src_fn()),
                 r=list(src_bufs), w=[i32q])
            P.op("v", lambda e, s: e.tensor_single_scalar(
                rnd_a.h[:, 0:n], i32q.h[:, 0:n], -1.0, ALU.add),
                r=[i32q], w=[rnd_a])
            P.op("v", lambda e, s: e.tensor_sub(
                rnd_b.h[:, 0:n], src_fn(), rnd_a.h[:, 0:n]),
                r=list(src_bufs) + [rnd_a], w=[rnd_b])
            P.op("v", lambda e, s: e.scalar_tensor_tensor(
                rnd_a.h[:, 0:n], rnd_b.h[:, 0:n], 1.5, rnd_a.h[:, 0:n],
                ALU.is_ge, ALU.add), r=[rnd_a, rnd_b], w=[rnd_a])
            P.op("v", lambda e, s: e.scalar_tensor_tensor(
                out_fn(), rnd_b.h[:, 0:n], 0.5, rnd_a.h[:, 0:n],
                ALU.is_gt, ALU.add), r=[rnd_a, rnd_b], w=[out_buf])

        def ternarize(stages, dst, dram_buf, nrow, ncol, s_dst_col, tag):
            P.cur_tag = tag
            nchk = nrow // 128
            scols = stages[0].h.ap().shape[1]
            P.op("g", lambda e, s: e.memset(colsB.h[:, 0:1], 0.0), w=[colsB])
            i = 0
            for c in range(nchk):
                for j0 in range(0, ncol, scols):
                    j1 = min(j0 + scols, ncol)
                    stage = stages[i % 2]
                    i += 1
                    P.dma("y", stage,
                          lambda stage=stage, j0=j0, j1=j1: stage.h[:, 0:j1 - j0],
                          dram_buf, lambda c=c, j0=j0, j1=j1: dram_buf.h.ap()
                          .rearrange("(a p) n -> p a n", p=128)[:, c, j0:j1])
                    P.op("v", lambda e, s, stage=stage, j0=j0, j1=j1: e.tensor_reduce(
                        colsB.h[:, 1:2], stage.h[:, 0:j1 - j0], axis=AX.X,
                        op=ALU.add, apply_absolute_value=True),
                        r=[stage], w=[colsB])
                    P.op("v", lambda e, s: e.tensor_add(
                        colsB.h[:, 0:1], colsB.h[:, 0:1], colsB.h[:, 1:2]),
                        r=[colsB], w=[colsB])
            P.op("t", lambda e, s: e.matmul(pD.h[0:1, 0:1], colsB.h[:, 0:1],
                                            ones_col.h[:], start=True, stop=True),
                 r=[colsB, ones_col], w=[pD])
            P.op("v", lambda e, s: e.tensor_copy(colsB.h[0:1, 2:3], pD.h[0:1, 0:1]),
                 r=[pD], w=[colsB])
            P.op("t", lambda e, s: e.matmul(pD.h[:, 0:1], onesf.h[:],
                                            colsB.h[0:1, 2:3], start=True, stop=True),
                 r=[onesf, colsB], w=[pD])
            nel = float(nrow) * ncol
            P.op("v", lambda e, s, nel=nel: e.tensor_single_scalar(
                colsB.h[:, 3:4], pD.h[:, 0:1], 0.5 / nel, ALU.mult),
                r=[pD], w=[colsB])
            P.op("v", lambda e, s: e.tensor_single_scalar(
                colsB.h[:, 4:5], colsB.h[:, 3:4], -1.0, ALU.mult), r=[colsB], w=[colsB])
            i = 0
            for c in range(nchk):
                for j0 in range(0, ncol, scols):
                    j1 = min(j0 + scols, ncol)
                    stage = stages[i % 2]
                    i += 1
                    P.dma("y", stage,
                          lambda stage=stage, j0=j0, j1=j1: stage.h[:, 0:j1 - j0],
                          dram_buf, lambda c=c, j0=j0, j1=j1: dram_buf.h.ap()
                          .rearrange("(a p) n -> p a n", p=128)[:, c, j0:j1])
                    for q0 in range(0, j1 - j0, 512):
                        qn = min(512, j1 - j0 - q0)
                        P.op("v", lambda e, s, stage=stage, q0=q0, qn=qn: e.tensor_scalar(
                            rnd_a.h[:, 0:qn], stage.h[:, q0:q0 + qn],
                            colsB.h[:, 3:4], None, ALU.is_ge),
                            r=[stage, colsB], w=[rnd_a])
                        P.op("v", lambda e, s, stage=stage, q0=q0, qn=qn: e.tensor_scalar(
                            rnd_b.h[:, 0:qn], stage.h[:, q0:q0 + qn],
                            colsB.h[:, 4:5], None, ALU.is_le),
                            r=[stage, colsB], w=[rnd_b])
                        P.op("v", lambda e, s, c=c, j0=j0, q0=q0, qn=qn: e.tensor_sub(
                            dst.h[:, c, j0 + q0:j0 + q0 + qn],
                            rnd_a.h[:, 0:qn], rnd_b.h[:, 0:qn]),
                            r=[rnd_a, rnd_b], w=[dst])
            P.op("v", lambda e, s, s_dst_col=s_dst_col: e.tensor_single_scalar(
                cols.h[:, s_dst_col:s_dst_col + 1], colsB.h[:, 3:4], 2.0, ALU.mult),
                r=[colsB], w=[cols])

        P.open_group()
        wstgD = [P.sbuf(f"wstgD{i}", [128, 2048], F32, dma_target=True)
                 for i in range(2)]
        ternarize(wstgD, w1t, w1T_d, D_MODEL, NHID, 9, "D_w1")
        ternarize(wstgD, w2t, w2T_d, NHID, D_MODEL, 10, "D_w2")
        P.close_group()
        xq1T = [P.sbuf(f"xq1T{i}", [128, NFC, 128], BF16) for i in range(2)]
        xq2T = [P.sbuf(f"xq2T{i}", [128, NHID // 128, 128], BF16) for i in range(2)]
        hbuf = P.sbuf("hbuf", [128, NHID], F32)
        out_t = [P.sbuf(f"out_t{i}", [128, 512], F32, dma_target=True)
                 for i in range(4)]
        qv = rnd_b
        gcol = [P.sbuf(f"gcol{i}", [128, 1], F32) for i in range(2)]
        ocol = [P.sbuf(f"ocol{i}", [128, 1], F32) for i in range(2)]
        outw = []
        for wi in range(4):
            b = Buf(f"outw{wi}", out_d.h, sem=f"outw{wi}", is_dram=True)
            P.bufs.append(b)
            outw.append(b)

        NT_ALL = T // 128

        def stage_A(t):
            P.cur_tag = f"D_A{t}"
            dp = t % 2
            x1T = xq1T[dp]
            emit_ln_tile(0, t, "D_ln", skip_dma=True)
            if t + 2 < NT_ALL:
                ln_dma(0, t + 2, "D_ln")
            lo = lnout[dp]
            P.op("v", lambda e, s, lo=lo: e.reduce_max(
                cols.h[:, 43:44], lo.h[:], axis=AX.X,
                apply_absolute_value=True), r=[lo], w=[cols])
            P.op("v", lambda e, s: e.reciprocal(cols.h[:, 44:45], cols.h[:, 43:44]),
                 r=[cols], w=[cols])
            P.op("v", lambda e, s: e.tensor_single_scalar(
                cols.h[:, 44:45], cols.h[:, 44:45], 127.0, ALU.mult),
                r=[cols], w=[cols])
            P.op("v", lambda e, s: e.reciprocal(cols.h[:, 45:46], cols.h[:, 44:45]),
                 r=[cols], w=[cols])
            for qi in range(2):
                q0 = qi * 512
                xq1b = xq1[qi]
                P.op("v", lambda e, s, lo=lo, q0=q0: e.tensor_scalar(
                    qv.h[:], lo.h[:, q0:q0 + 512], cols.h[:, 44:45],
                    None, ALU.mult), r=[lo, cols], w=[qv])
                emit_round(lambda: qv.h[:], [qv], xq1b,
                           lambda xq1b=xq1b: xq1b.h[:], 512)
                for j in range(4):
                    fc = qi * 4 + j
                    P.op("t", lambda e, s, xq1b=xq1b, j=j: e.transpose(
                        pT.h[:], xq1b.h[:, j * 128:(j + 1) * 128], ident.h[:]),
                        r=[xq1b, ident], w=[pT])
                    P.op("v", lambda e, s, x1T=x1T, fc=fc: e.tensor_copy(
                        x1T.h[:, fc, :], pT.h[:]), r=[pT], w=[x1T])
            gc = gcol[dp]
            P.op("v", lambda e, s, gc=gc: e.tensor_mul(
                gc.h[:], cols.h[:, 9:10], cols.h[:, 45:46]),
                r=[cols], w=[gc])

        def stage_L(t):
            P.cur_tag = f"D_L{t}"
            for h in range(2):
                ot = out_t[2 * h + t % 2]
                mq = mid_q[2 * h + t % 2]
                P.dma("y", ot, lambda ot=ot: ot.h[:],
                      mq, lambda t=t, mq=mq: mq.h.ap()
                      [t * 128:(t + 1) * 128, :])

        def stage_B(t):
            P.cur_tag = f"D_B{t}"
            x1T = xq1T[t % 2]
            for hc in range(NHID // 512):
                ps = [pA, pB][hc % 2]
                for fc in range(NFC):
                    P.op("t", lambda e, s, ps=ps, x1T=x1T, hc=hc, fc=fc: e.matmul(
                        ps.h[:, 0:512], x1T.h[:, fc, :],
                        w1t.h[:, fc, hc * 512:(hc + 1) * 512],
                        start=(fc == 0), stop=(fc == NFC - 1)),
                        r=[x1T, w1t], w=[ps])
                gc = gcol[t % 2]
                P.op("s", lambda e, s, ps=ps, hc=hc, gc=gc: e.activation(
                    hbuf.h[:, hc * 512:(hc + 1) * 512], ps.h[:, 0:512],
                    AF.Gelu, bias=0.0, scale=gc.h[:]),
                    r=[ps, gc], w=[hbuf])

        def stage_C(t):
            P.cur_tag = f"D_C{t}"
            x2T = xq2T[t % 2]
            P.op("v", lambda e, s: e.reduce_max(
                cols.h[:, 43:44], hbuf.h[:], axis=AX.X,
                apply_absolute_value=True), r=[hbuf], w=[cols])
            P.op("v", lambda e, s: e.tensor_single_scalar(
                cols.h[:, 43:44], cols.h[:, 43:44], EPS_Q, ALU.max),
                r=[cols], w=[cols])
            P.op("v", lambda e, s: e.reciprocal(cols.h[:, 44:45], cols.h[:, 43:44]),
                 r=[cols], w=[cols])
            P.op("v", lambda e, s: e.tensor_single_scalar(
                cols.h[:, 44:45], cols.h[:, 44:45], 127.0, ALU.mult),
                r=[cols], w=[cols])
            P.op("v", lambda e, s: e.reciprocal(cols.h[:, 45:46], cols.h[:, 44:45]),
                 r=[cols], w=[cols])
            for qi in range(8):
                q0 = qi * 512
                xq2b = xq2[qi % 2]
                P.op("v", lambda e, s, q0=q0: e.tensor_scalar(
                    qv.h[:], hbuf.h[:, q0:q0 + 512],
                    cols.h[:, 44:45], None, ALU.mult),
                    r=[hbuf, cols], w=[qv])
                emit_round(lambda: qv.h[:], [qv], xq2b,
                           lambda xq2b=xq2b: xq2b.h[:], 512)
                for j in range(4):
                    k = qi * 4 + j
                    P.op("t", lambda e, s, xq2b=xq2b, j=j: e.transpose(
                        pT.h[:], xq2b.h[:, j * 128:(j + 1) * 128], ident.h[:]),
                        r=[xq2b, ident], w=[pT])
                    P.op("v", lambda e, s, x2T=x2T, k=k: e.tensor_copy(
                        x2T.h[:, k, :], pT.h[:]), r=[pT], w=[x2T])
            oc = ocol[t % 2]
            P.op("v", lambda e, s, oc=oc: e.tensor_mul(
                oc.h[:], cols.h[:, 10:11], cols.h[:, 45:46]),
                r=[cols], w=[oc])

        def stage_Dm(t):
            P.cur_tag = f"D_D{t}"
            x2T = xq2T[t % 2]
            for half in range(2):
                pc = [pC1, pC2][half]
                for k in range(NHID // 128):
                    P.op("t", lambda e, s, pc=pc, x2T=x2T, k=k, half=half: e.matmul(
                        pc.h[:], x2T.h[:, k, :],
                        w2t.h[:, k, half * 512:(half + 1) * 512],
                        start=(k == 0), stop=(k == NHID // 128 - 1)),
                        r=[x2T, w2t], w=[pc])

        def stage_E(t):
            P.cur_tag = f"D_E{t}"
            for half in range(2):
                pc = [pC1, pC2][half]
                ot = out_t[2 * half + t % 2]
                oc = ocol[t % 2]
                P.op("v", lambda e, s, pc=pc, ot=ot, oc=oc: e.scalar_tensor_tensor(
                    ot.h[:], pc.h[:], oc.h[:], ot.h[:],
                    ALU.mult, ALU.add),
                    r=[pc, oc, ot], w=[ot])
                ow = outw[half * 2 + t % 2]
                P.dma("gy"[half], ow,
                      lambda t=t, half=half: out_d.h.ap()
                      [t * 128:(t + 1) * 128, half * 512:(half + 1) * 512],
                      ot, lambda ot=ot: ot.h[:])

        ln_dma(0, 0, "D_ln")
        ln_dma(0, 1, "D_ln")
        stage_A(0)
        if NT_ALL > 1:
            stage_A(1)
        stage_B(0)
        if NT_ALL > 2:
            stage_A(2)
        stage_C(0)
        if NT_ALL > 1:
            stage_B(1)
        stage_L(0)
        for t in range(NT_ALL):
            stage_Dm(t)
            if t + 1 < NT_ALL:
                stage_L(t + 1)
            if t + 3 < NT_ALL:
                stage_A(t + 3)
            if t + 1 < NT_ALL:
                stage_C(t + 1)
            if t + 2 < NT_ALL:
                stage_B(t + 2)
            stage_E(t)
        P.close_group()

    sweep_ABC("fw")
    sweep_ABC("bw")
    sweep_D()

    P.cur_tag = "final"
    P.barrier_on("y", [out_d])
    P.finalize()
    return nc



def kernel(L0_attn_x, L0_attn_fw_Win, L0_attn_fw_convw, L0_attn_fw_convb,
           L0_attn_fw_Wx, L0_attn_fw_Wdt, L0_attn_fw_bdt, L0_attn_fw_Alog,
           L0_attn_fw_D, L0_attn_fw_Wout, L0_attn_bw_Win, L0_attn_bw_convw,
           L0_attn_bw_convb, L0_attn_bw_Wx, L0_attn_bw_Wdt, L0_attn_bw_bdt,
           L0_attn_bw_Alog, L0_attn_bw_D, L0_attn_bw_Wout, L0_attn_fuse_W,
           L0_attn_fuse_b, L0_ffn_w1, L0_ffn_w2):
    x = np.ascontiguousarray(np.asarray(L0_attn_x, dtype=np.float32))
    B, T, D = x.shape
    nc = build_program(T=T)

    def tr(a):
        return np.ascontiguousarray(np.asarray(a, dtype=np.float32).T)

    common = {
        "fw_WinT": tr(L0_attn_fw_Win), "bw_WinT": tr(L0_attn_bw_Win),
        "fw_WxT": tr(L0_attn_fw_Wx), "bw_WxT": tr(L0_attn_bw_Wx),
        "fw_WdtT": tr(L0_attn_fw_Wdt), "bw_WdtT": tr(L0_attn_bw_Wdt),
        "fw_Wout_nat": np.ascontiguousarray(np.asarray(L0_attn_fw_Wout, np.float32)),
        "bw_Wout_nat": np.ascontiguousarray(np.asarray(L0_attn_bw_Wout, np.float32)),
        "fw_convw": np.ascontiguousarray(np.asarray(L0_attn_fw_convw, np.float32).reshape(D_INNER, D_CONV)),
        "bw_convw": np.ascontiguousarray(np.asarray(L0_attn_bw_convw, np.float32).reshape(D_INNER, D_CONV)),
        "fw_convb": np.ascontiguousarray(np.asarray(L0_attn_fw_convb, np.float32)),
        "bw_convb": np.ascontiguousarray(np.asarray(L0_attn_bw_convb, np.float32)),
        "fw_bdt": np.ascontiguousarray(np.asarray(L0_attn_fw_bdt, np.float32)),
        "bw_bdt": np.ascontiguousarray(np.asarray(L0_attn_bw_bdt, np.float32)),
        "fw_Alog": np.ascontiguousarray(np.asarray(L0_attn_fw_Alog, np.float32)),
        "bw_Alog": np.ascontiguousarray(np.asarray(L0_attn_bw_Alog, np.float32)),
        "fw_D": np.ascontiguousarray(np.asarray(L0_attn_fw_D, np.float32)),
        "bw_D": np.ascontiguousarray(np.asarray(L0_attn_bw_D, np.float32)),
        "fuseT": tr(L0_attn_fuse_W),
        "fuse_b": np.ascontiguousarray(np.asarray(L0_attn_fuse_b, np.float32)),
        "w1T": tr(L0_ffn_w1),
        "w2T": tr(L0_ffn_w2),
    }
    in_maps = [dict(common, x=np.ascontiguousarray(x[b])) for b in range(B)]
    res = run_bass_kernel_spmd(nc, in_maps, core_ids=list(range(B)))
    return np.stack([r["out"] for r in res.results], axis=0)


# version 22
# speedup vs baseline: 1.0859x; 1.0859x over previous
import numpy as np
import ml_dtypes

import concourse.bass as bass
import concourse.mybir as mybir
from concourse.bass_utils import run_bass_kernel_spmd

F32 = mybir.dt.float32
BF16 = mybir.dt.bfloat16
I32 = mybir.dt.int32
AF = mybir.ActivationFunctionType
ALU = mybir.AluOpType
AX = mybir.AxisListType
BFNP = ml_dtypes.bfloat16

D_MODEL = 1024
D_INNER = 2048
D_STATE = 16
DT_RANK = 64
D_CONV = 4
NDHI = D_INNER // 128
NFC = D_MODEL // 128
NHID = 4 * D_MODEL
EPS_LN = 1e-5
EPS_Q = 1e-5


class Buf:
    __slots__ = ("name", "h", "sem", "last_w", "readers", "dma_count",
                 "is_dram", "guard")

    def __init__(self, name, h, sem=None, is_dram=False, guard=None):
        self.name = name
        self.h = h
        self.sem = sem
        self.last_w = None
        self.readers = []
        self.dma_count = 0
        self.is_dram = is_dram
        self.guard = guard


class Op:
    __slots__ = ("idx", "eng", "fn", "deps", "needs_inc", "sem_val", "is_dma",
                 "dma_buf", "dma_val", "tag")

    def __init__(self, idx, eng, fn, tag=""):
        self.idx = idx
        self.eng = eng
        self.fn = fn
        self.deps = []
        self.needs_inc = False
        self.sem_val = None
        self.is_dma = False
        self.dma_buf = None
        self.dma_val = None
        self.tag = tag


class Prog:

    def __init__(self, nc):
        self.nc = nc
        self.ops = []
        self.bufs = []
        self.cur_tag = ""
        self.group_stack = []
        self.arena_join = None
        self.dma_counts = {}

    def sbuf(self, name, shape, dtype, dma_target=False, sem=None):
        if self.group_stack:
            cm = self.nc.sbuf_tensor(name, list(shape), dtype)
            h = cm.__enter__()
            b = Buf(name, h, guard=cm)
            self.group_stack[-1][0].append(cm)
            self.group_stack[-1][1].append(b)
            b.last_w = self.arena_join
        else:
            h = self.nc.alloc_sbuf_tensor(name, list(shape), dtype)
            b = Buf(name, h)
        if dma_target:
            b.sem = sem or name
        self.bufs.append(b)
        return b

    def open_group(self):
        self.group_stack.append(([], []))

    def close_group(self):
        guards, gbufs = self.group_stack.pop()
        deps = set()
        for b in gbufs:
            if b.last_w is not None:
                deps.add(b.last_w)
            for r in b.readers:
                deps.add(r)
        join = Op(len(self.ops), "v", None, self.cur_tag + "_join")
        join.fn = lambda e, s: e.nop()
        join.deps = list(deps)
        self.ops.append(join)
        self.arena_join = join
        for cm in reversed(guards):
            cm.__exit__(None, None, None)

    def psum(self, name, shape, dtype=F32):
        h = self.nc.alloc_psum_tensor(name, list(shape), dtype)
        b = Buf(name, h)
        self.bufs.append(b)
        return b

    def dram(self, name, shape, dtype, kind="Internal"):
        h = self.nc.dram_tensor(name, list(shape), dtype, kind=kind)
        b = Buf(name, h, is_dram=True)
        b.sem = name + "_dsem"
        self.bufs.append(b)
        return b

    def _dep(self, op, on):
        if on is not None and on is not op and on not in op.deps:
            op.deps.append(on)

    def _track(self, op, reads, writes):
        for b in reads:
            self._dep(op, b.last_w)
            b.readers.append(op)
        for b in writes:
            for r in b.readers:
                self._dep(op, r)
            self._dep(op, b.last_w)
            b.last_w = op
            b.readers = []

    def op(self, eng, fn, r=(), w=()):
        o = Op(len(self.ops), eng, fn, self.cur_tag)
        self._track(o, r, w)
        self.ops.append(o)
        return o

    def dma(self, eng, dst, dst_ap_fn, src, src_ap_fn, slow=False):
        assert dst.sem is not None, f"dma target {dst.name} needs a sem"
        o = Op(len(self.ops), eng, None, self.cur_tag)
        o.is_dma = True
        o.dma_buf = dst
        cnt = self.dma_counts.get(dst.sem, 0) + 1
        self.dma_counts[dst.sem] = cnt
        dst.dma_count += 1
        o.dma_val = 16 * cnt

        def fn(e, sems, _d=dst_ap_fn, _s=src_ap_fn, _o=o, _slow=slow):
            e.dma_start(out=_d(), in_=_s(),
                        allow_slow_non_contiguous=_slow
                        ).then_inc(sems[_o.dma_buf.sem], 16)
        o.fn = fn
        self._track(o, [src], [dst])
        self.ops.append(o)
        return o

    def barrier_on(self, eng, bufs):
        o = Op(len(self.ops), eng, lambda e, sems: None, self.cur_tag)
        for x in bufs:
            self._dep(o, x.last_w)
        self.ops.append(o)
        return o

    def finalize(self):
        nc = self.nc
        for o in self.ops:
            for d in o.deps:
                if not d.is_dma:
                    d.needs_inc = True
        eng_count = {}
        for o in self.ops:
            if o.needs_inc and not o.is_dma:
                eng_count[o.eng] = eng_count.get(o.eng, 0) + 1
                o.sem_val = eng_count[o.eng]
        sem_names = [f"E{e}" for e in "tvsyg"]
        sem_names += sorted(self.dma_counts.keys())
        ctxs = []
        sems = {}
        for snm in sem_names:
            cm = nc.semaphore(snm)
            sems[snm] = cm.__enter__()
            ctxs.append(cm)
        streams = {e: [o for o in self.ops if o.eng == e] for e in "tvsyg"}

        has_wait = {}
        for e, ops in streams.items():
            waited = {}
            for o in ops:
                hw = False
                for d in o.deps:
                    if d.is_dma:
                        snm, val = d.dma_buf.sem, d.dma_val
                    else:
                        snm, val = "E" + d.eng, d.sem_val
                    if waited.get(snm, -1) < val:
                        waited[snm] = val
                        hw = True
                has_wait[o.idx] = hw
        bump = {e: {} for e in "tvsyg"}
        for e, ops in streams.items():
            i = 0
            n = len(ops)
            while i < n:
                o = ops[i]
                if o.is_dma or not o.needs_inc:
                    i += 1
                    continue
                j = i
                vals = [ops[j].sem_val]
                while (j + 1 < n and ops[j + 1].needs_inc
                       and not ops[j + 1].is_dma and not has_wait[ops[j + 1].idx]):
                    j += 1
                    vals.append(ops[j].sem_val)
                end_val = vals[-1]
                for v in vals:
                    bump[e][v] = end_val
                i = j + 1

        blk_cm = nc.Block()
        block = blk_cm.__enter__()

        def make_body(eng_key):
            ops = streams[eng_key]

            def body(eng):
                waited = {}
                for o in ops:
                    for d in o.deps:
                        if d.is_dma:
                            snm, val = d.dma_buf.sem, d.dma_val
                        else:
                            snm = "E" + d.eng
                            val = bump[d.eng].get(d.sem_val, d.sem_val)
                        if waited.get(snm, -1) < val:
                            eng.wait_ge(sems[snm], val)
                            waited[snm] = val
                    inst = o.fn(eng, sems)
                    if o.needs_inc and not o.is_dma:
                        assert inst is not None, f"op {o.idx} [{o.tag}] returned None"
                        inst.then_inc(sems["E" + o.eng], 1)
            return body

        block.tensor(make_body("t"))
        block.vector(make_body("v"))
        block.scalar(make_body("s"))
        block.sync(make_body("y"))
        block.gpsimd(make_body("g"))
        blk_cm.__exit__(None, None, None)
        for cm in reversed(ctxs):
            cm.__exit__(None, None, None)


def build_program(T=4096, SEG=512, CHUNK=1024):
    SEG = min(SEG, T)
    CHUNK = min(CHUNK, T)
    nseg = T // SEG
    nchunk = T // CHUNK
    NTT = SEG // 128
    nc = bass.Bass("TRN2", target_bir_lowering=False)
    P = Prog(nc)

    x_in = P.dram("x", [T, D_MODEL], F32, kind="ExternalInput")
    out_d = P.dram("out", [T, D_MODEL], F32, kind="ExternalOutput")
    wdecl = {}
    for pre in ("fw", "bw"):
        wdecl[pre + "_WinT"] = P.dram(pre + "_WinT", [D_MODEL, 2 * D_INNER], F32, "ExternalInput")
        wdecl[pre + "_WxT"] = P.dram(pre + "_WxT", [D_INNER, DT_RANK + 2 * D_STATE], F32, "ExternalInput")
        wdecl[pre + "_WdtT"] = P.dram(pre + "_WdtT", [DT_RANK, D_INNER], F32, "ExternalInput")
        wdecl[pre + "_Wout_nat"] = P.dram(pre + "_Wout_nat", [D_MODEL, D_INNER], F32, "ExternalInput")
        wdecl[pre + "_convw"] = P.dram(pre + "_convw", [D_INNER, D_CONV], F32, "ExternalInput")
        wdecl[pre + "_convb"] = P.dram(pre + "_convb", [D_INNER], F32, "ExternalInput")
        wdecl[pre + "_bdt"] = P.dram(pre + "_bdt", [D_INNER], F32, "ExternalInput")
        wdecl[pre + "_Alog"] = P.dram(pre + "_Alog", [D_INNER, D_STATE], F32, "ExternalInput")
        wdecl[pre + "_D"] = P.dram(pre + "_D", [D_INNER], F32, "ExternalInput")
    fuseT_d = P.dram("fuseT", [2 * D_MODEL, D_MODEL], F32, "ExternalInput")
    fuseb_d = P.dram("fuse_b", [D_MODEL], F32, "ExternalInput")
    w1T_d = P.dram("w1T", [D_MODEL, NHID], F32, "ExternalInput")
    w2T_d = P.dram("w2T", [NHID, D_MODEL], F32, "ExternalInput")

    ident_d = Buf("ident_d", nc.inline_tensor(np.eye(128).astype(BFNP), "ident_const"),
                  sem="ident_dsem", is_dram=True)
    P.bufs.append(ident_d)

    mid_q = [P.dram(f"mid{q}", [T, 512], F32) for q in range(4)]
    spill = {}
    for pre in ("fw", "bw"):
        spill[pre + "_bc"] = P.dram(f"sp_{pre}_bc", [32, T], BF16)
        spill[pre + "_win"] = P.dram(f"sp_{pre}_win", [NFC, 128, 2 * D_INNER], BF16)
        spill[pre + "_wc"] = P.dram(f"sp_{pre}_wc", [NDHI, 128, D_MODEL], BF16)

    ident = P.sbuf("ident", [128, 128], BF16, dma_target=True)
    Acoef = {p: P.sbuf(f"A_{p}", [128, NDHI, D_STATE], F32, dma_target=True)
             for p in ("fw", "bw")}
    convw_s = {p: P.sbuf(f"convw_{p}", [128, NDHI, D_CONV], F32, dma_target=True) for p in ("fw", "bw")}
    convb_s = {p: P.sbuf(f"convb_{p}", [128, NDHI], F32, dma_target=True) for p in ("fw", "bw")}
    bdt_s = {p: P.sbuf(f"bdt_{p}", [128, NDHI], F32, dma_target=True) for p in ("fw", "bw")}
    Dp_s = {p: P.sbuf(f"Dp_{p}", [128, NDHI], F32, dma_target=True) for p in ("fw", "bw")}
    onesf = P.sbuf("onesf", [1, 128], F32)
    onesb = P.sbuf("onesb", [1, 128], BF16)
    ones_col = P.sbuf("ones_col", [128, 1], F32)
    cols = P.sbuf("colscratch", [128, 48], F32)
    colsB = P.sbuf("colscratchB", [128, 16], F32)
    hstate = P.sbuf("hstate", [128, NDHI, D_STATE], F32)
    lnout = [None, None]
    xt = [P.sbuf(f"xt{i}", [128, D_MODEL], F32, dma_target=True) for i in range(2)]
    epsc = P.sbuf("epsc", [128, 1], F32)

    pA = P.psum("pA", [128, 512], F32)
    pB = P.psum("pB", [128, 512], F32)
    pC1 = P.psum("pC1", [128, 512], F32)
    pC2 = P.psum("pC2", [128, 512], F32)
    pD = P.psum("pD", [128, 512], F32)
    pW = P.psum("pW", [128, 512], F32)
    pRS = P.psum("pRS", [96, 512], F32)
    pT = P.psum("pT", [128, 128], BF16)

    P.cur_tag = "prep"
    P.dma("y", ident, lambda: ident.h[:], ident_d, lambda: ident_d.h.ap())
    P.op("g", lambda e, s: e.memset(onesf.h[:], 1.0), w=[onesf])
    P.op("g", lambda e, s: e.memset(onesb.h[:], 1.0), w=[onesb])
    P.op("g", lambda e, s: e.memset(ones_col.h[:], 1.0), w=[ones_col])
    P.op("g", lambda e, s: e.memset(epsc.h[:], EPS_LN), w=[epsc])
    for pre in ("fw", "bw"):
        P.dma("y", Acoef[pre], lambda pre=pre: Acoef[pre].h[:],
              wdecl[pre + "_Alog"],
              lambda pre=pre: wdecl[pre + "_Alog"].h.ap().rearrange("(a p) s -> p a s", p=128))
        P.op("s", lambda e, s, pre=pre: e.activation(
            Acoef[pre].h[:], Acoef[pre].h[:], AF.Exp),
            r=[Acoef[pre]], w=[Acoef[pre]])
        P.op("v", lambda e, s, pre=pre: e.tensor_single_scalar(
            Acoef[pre].h[:], Acoef[pre].h[:], -1.0, ALU.mult),
            r=[Acoef[pre]], w=[Acoef[pre]])
        P.dma("y", convw_s[pre], lambda pre=pre: convw_s[pre].h[:],
              wdecl[pre + "_convw"],
              lambda pre=pre: wdecl[pre + "_convw"].h.ap().rearrange("(a p) k -> p a k", p=128))
        P.dma("y", convb_s[pre], lambda pre=pre: convb_s[pre].h[:],
              wdecl[pre + "_convb"],
              lambda pre=pre: wdecl[pre + "_convb"].h.ap().rearrange("(a p) -> p a", p=128), slow=True)
        P.dma("y", bdt_s[pre], lambda pre=pre: bdt_s[pre].h[:],
              wdecl[pre + "_bdt"],
              lambda pre=pre: wdecl[pre + "_bdt"].h.ap().rearrange("(a p) -> p a", p=128), slow=True)
        P.dma("y", Dp_s[pre], lambda pre=pre: Dp_s[pre].h[:],
              wdecl[pre + "_D"],
              lambda pre=pre: wdecl[pre + "_D"].h.ap().rearrange("(a p) -> p a", p=128), slow=True)

    def load_weights_cast(stages, dst, dram_buf, rows, ncol, tag):
        P.cur_tag = tag
        if not isinstance(stages, list):
            stages = [stages]
        nchk = rows // 128
        scols = stages[0].h.ap().shape[1]
        i = 0
        for c in range(nchk):
            for j0 in range(0, ncol, scols):
                j1 = min(j0 + scols, ncol)
                stage = stages[i % len(stages)]
                i += 1
                P.dma("y", stage,
                      lambda stage=stage, j0=j0, j1=j1: stage.h[:, 0:j1 - j0],
                      dram_buf, lambda c=c, j0=j0, j1=j1: dram_buf.h.ap()
                      .rearrange("(a p) n -> p a n", p=128)[:, c, j0:j1])
                P.op("v", lambda e, s, stage=stage, c=c, j0=j0, j1=j1: e.tensor_copy(
                    dst.h[:, c, j0:j1], stage.h[:, 0:j1 - j0]), r=[stage], w=[dst])

    lnsq = [P.sbuf("lnsq0", [128, D_MODEL], F32)] * 2

    def ln_dma(seg, tt, tag):
        P.cur_tag = tag
        t0 = seg * SEG
        xtp = xt[tt % 2]
        P.dma("y", xtp, lambda xtp=xtp: xtp.h[:],
              x_in, lambda t0=t0, tt=tt: x_in.h.ap()
              [t0 + tt * 128:t0 + (tt + 1) * 128, :])

    def emit_ln_tile(seg, tt, tag, out=None, skip_dma=False):
        if not skip_dma:
            ln_dma(seg, tt, tag)
        P.cur_tag = tag
        if True:
            p = tt % 2
            c0 = 16 * p
            xtp, sq = xt[p], lnsq[p]
            lo = out if out is not None else lnout[p]
            P.op("v", lambda e, s, xtp=xtp, c0=c0: e.reduce_sum(
                cols.h[:, c0:c0 + 1], xtp.h[:], axis=AX.X), r=[xtp], w=[cols])
            P.op("s", lambda e, s, xtp=xtp, sq=sq: e.activation(
                sq.h[:], xtp.h[:], AF.Square), r=[xtp], w=[sq])
            P.op("v", lambda e, s, sq=sq, c0=c0: e.reduce_sum(
                cols.h[:, c0 + 1:c0 + 2], sq.h[:], axis=AX.X),
                r=[sq], w=[cols])
            P.op("v", lambda e, s, c0=c0: e.tensor_single_scalar(
                cols.h[:, c0 + 2:c0 + 3], cols.h[:, c0:c0 + 1], 1.0 / D_MODEL, ALU.mult),
                r=[cols], w=[cols])
            P.op("v", lambda e, s, c0=c0: e.tensor_mul(
                cols.h[:, c0 + 3:c0 + 4], cols.h[:, c0 + 2:c0 + 3],
                cols.h[:, c0 + 2:c0 + 3]), r=[cols], w=[cols])
            P.op("v", lambda e, s, c0=c0: e.tensor_single_scalar(
                cols.h[:, c0 + 4:c0 + 5], cols.h[:, c0 + 1:c0 + 2], 1.0 / D_MODEL,
                ALU.mult), r=[cols], w=[cols])
            P.op("v", lambda e, s, c0=c0: e.tensor_sub(
                cols.h[:, c0 + 5:c0 + 6], cols.h[:, c0 + 4:c0 + 5],
                cols.h[:, c0 + 3:c0 + 4]), r=[cols], w=[cols])
            P.op("s", lambda e, s, c0=c0: e.activation(
                cols.h[:, c0 + 6:c0 + 7], cols.h[:, c0 + 5:c0 + 6],
                AF.Ln, bias=epsc.h[:], scale=1.0), r=[cols, epsc], w=[cols])
            P.op("s", lambda e, s, c0=c0: e.activation(
                cols.h[:, c0 + 7:c0 + 8], cols.h[:, c0 + 6:c0 + 7],
                AF.Exp, bias=0.0, scale=-0.5), r=[cols], w=[cols])
            P.op("v", lambda e, s, c0=c0: e.tensor_single_scalar(
                cols.h[:, c0 + 8:c0 + 9], cols.h[:, c0 + 2:c0 + 3], -1.0, ALU.mult),
                r=[cols], w=[cols])
            P.op("v", lambda e, s, xtp=xtp, lo=lo, c0=c0: e.tensor_scalar(
                lo.h[:], xtp.h[:], cols.h[:, c0 + 8:c0 + 9], cols.h[:, c0 + 7:c0 + 8],
                ALU.add, ALU.mult), r=[xtp, cols], w=[lo])

    def emit_ln(seg, tag, per_tile):
        for tt in range(NTT):
            emit_ln_tile(seg, tt, tag)
            per_tile(tt)

    NHC = 4
    NG = NDHI // 2

    def sweep_ABC(pre):
        P.open_group()
        P.cur_tag = f"M{pre}_alloc"
        sfx = "_" + pre
        rev = (pre == "bw")
        nc_wx = DT_RANK + 2 * D_STATE

        wxT = P.sbuf("wxT" + sfx, [128, NDHI, nc_wx], BF16)
        wdtT = P.sbuf("wdtT" + sfx, [DT_RANK, D_INNER], BF16)
        fbias = P.sbuf("fbias", [1, D_MODEL], BF16) if rev else None

        P.open_group()
        pstg = [P.sbuf(f"pstg{sfx}{i}", [128, 2048], F32, dma_target=True,
                       sem=f"pstg{i}") for i in range(2)]
        wtmp = [P.sbuf(f"wtmp{sfx}{i}", [128, 2048], BF16) for i in range(2)]
        P.cur_tag = f"M{pre}_wprep"
        k = 0
        for kc in range(NFC):
            for j0 in (0, 2048):
                stg, wt = pstg[k % 2], wtmp[k % 2]
                k += 1
                P.dma("y", stg, lambda stg=stg: stg.h[:],
                      wdecl[pre + "_WinT"], lambda kc=kc, j0=j0, pre=pre:
                      wdecl[pre + "_WinT"].h.ap()
                      .rearrange("(a p) n -> p a n", p=128)[:, kc, j0:j0 + 2048])
                P.op("v", lambda e, s, wt=wt, stg=stg: e.tensor_copy(
                    wt.h[:], stg.h[:]), r=[stg], w=[wt])
                P.dma("g", spill[pre + "_win"],
                      lambda kc=kc, j0=j0, pre=pre:
                      spill[pre + "_win"].h.ap()[kc, :, j0:j0 + 2048],
                      wt, lambda wt=wt: wt.h[:])
        wnat = P.sbuf("wnat" + sfx, [128, NFC, D_INNER], BF16)
        fuT = P.sbuf("fuT" + sfx, [128, NFC, D_MODEL], BF16)
        r0 = 0 if pre == "fw" else D_MODEL
        for mk in range(NFC):
            stg = pstg[mk % 2]
            P.dma("y", stg, lambda stg=stg: stg.h[:],
                  wdecl[pre + "_Wout_nat"], lambda mk=mk, pre=pre:
                  wdecl[pre + "_Wout_nat"].h.ap()
                  .rearrange("(a p) n -> p a n", p=128)[:, mk, :])
            P.op("v", lambda e, s, stg=stg, mk=mk: e.tensor_copy(
                wnat.h[:, mk, :], stg.h[:]), r=[stg], w=[wnat])
            stg2 = pstg[(mk + 1) % 2]
            P.dma("y", stg2, lambda stg2=stg2: stg2.h[:, 0:D_MODEL],
                  fuseT_d, lambda mk=mk, r0=r0: fuseT_d.h.ap()[r0:r0 + D_MODEL, :]
                  .rearrange("(a p) n -> p a n", p=128)[:, mk, :])
            P.op("v", lambda e, s, stg2=stg2, mk=mk: e.tensor_copy(
                fuT.h[:, mk, :], stg2.h[:, 0:D_MODEL]), r=[stg2], w=[fuT])
        wc_t = [P.sbuf(f"wc_t{sfx}{i}", [128, 512], BF16) for i in range(2)]
        for dc in range(NDHI):
            for half in range(2):
                ps = [pA, pB][(dc * 2 + half) % 2]
                for mk in range(NFC):
                    P.op("t", lambda e, s, ps=ps, dc=dc, half=half, mk=mk: e.matmul(
                        ps.h[:, 0:512], wnat.h[:, mk, dc * 128:(dc + 1) * 128],
                        fuT.h[:, mk, half * 512:(half + 1) * 512],
                        start=(mk == 0), stop=(mk == NFC - 1)),
                        r=[wnat, fuT], w=[ps])
                wct = wc_t[half]
                P.op("v", lambda e, s, ps=ps, wct=wct: e.tensor_copy(
                    wct.h[:], ps.h[:, 0:512]), r=[ps], w=[wct])
                P.dma("g", spill[pre + "_wc"],
                      lambda dc=dc, half=half, pre=pre:
                      spill[pre + "_wc"].h.ap()[dc, :, half * 512:(half + 1) * 512],
                      wct, lambda wct=wct: wct.h[:])
        P.cur_tag = f"M{pre}_w"
        for c in range(NDHI):
            stg = pstg[c % 2]
            P.dma("y", stg, lambda stg=stg: stg.h[:, 0:nc_wx],
                  wdecl[pre + "_WxT"], lambda c=c, pre=pre: wdecl[pre + "_WxT"].h.ap()
                  .rearrange("(a p) n -> p a n", p=128)[:, c, :])
            P.op("v", lambda e, s, stg=stg, c=c: e.tensor_copy(
                wxT.h[:, c, :], stg.h[:, 0:nc_wx]), r=[stg], w=[wxT])
        P.dma("y", pstg[0], lambda: pstg[0].h[0:DT_RANK, 0:D_INNER],
              wdecl[pre + "_WdtT"], lambda pre=pre: wdecl[pre + "_WdtT"].h.ap())
        P.op("v", lambda e, s: e.tensor_copy(
            wdtT.h[:], pstg[0].h[0:DT_RANK, 0:D_INNER]), r=[pstg[0]], w=[wdtT])
        if rev:
            P.dma("y", pstg[1], lambda: pstg[1].h[0:1, 0:D_MODEL],
                  fuseb_d, lambda: fuseb_d.h.ap().rearrange("(a n) -> a n", a=1))
            P.op("v", lambda e, s: e.tensor_copy(
                fbias.h[:], pstg[1].h[0:1, 0:D_MODEL]), r=[pstg[1]], w=[fbias])
        P.close_group()

        P.cur_tag = f"M{pre}_alloc"
        winsl = [P.sbuf(f"winsl{sfx}{i}", [128, NFC, 256], BF16, dma_target=True,
                        sem=f"winsl{i}") for i in range(2)]
        wq = [P.sbuf(f"wq{sfx}{i}", [128, NDHI, 256], BF16, dma_target=True,
                     sem=f"wq{i}") for i in range(2)]
        xnT = P.sbuf("xnT" + sfx, [128, NFC, SEG], BF16)
        xnbt = [P.sbuf("xnbt" + sfx, [128, D_MODEL], BF16)] * 2
        xsb = [P.sbuf(f"xsb{sfx}{d}", [128, SEG + 3], BF16) for d in range(NDHI)]
        ub = [P.sbuf(f"ub{sfx}{d}", [128, SEG], BF16) for d in range(NDHI)]
        szb = [P.sbuf(f"szb{sfx}{d}", [128, SEG], BF16) for d in range(NDHI)]
        y2sg = [[P.sbuf(f"y2s{sfx}{pp}_{d}", [128, SEG], BF16)
                 for d in range(NDHI)] for pp in range(2)]
        BrepS = [P.sbuf(f"Brep{sfx}{st}", [128, SEG], BF16, dma_target=True,
                        sem=f"Brep{st}") for st in range(D_STATE)]
        CrepS = [P.sbuf(f"Crep{sfx}{st}", [128, SEG], BF16, dma_target=True,
                        sem=f"Crep{st}") for st in range(D_STATE)]
        dtbc = [P.sbuf(f"dtbc{sfx}{i}", [DT_RANK + 32, SEG], BF16)
                for i in range(2)]
        dtcA = P.sbuf("dtcA" + sfx, [128, SEG], F32)
        dtcB = P.sbuf("dtcB" + sfx, [128, SEG], F32)
        convf = dtcB
        d1g = [P.sbuf(f"d1g{sfx}{i}", [128, 2, SEG], BF16) for i in range(2)]
        dug = [P.sbuf(f"dug{sfx}{i}", [128, 2, SEG], BF16) for i in range(2)]
        a_sl = [P.sbuf(f"a_sl{sfx}{i}", [128, SEG], BF16) for i in range(2)]
        b_sl = [P.sbuf(f"b_sl{sfx}{i}", [128, SEG], BF16) for i in range(2)]
        h_sl = [P.sbuf(f"h_sl{sfx}{i}", [128, SEG], BF16) for i in range(2)]
        hc_sl = [P.sbuf(f"hc_sl{sfx}{i}", [128, SEG], BF16) for i in range(NHC)]
        tq_f32 = dtcA
        ydt = [a_sl[0]] * 2
        mid_ts = [P.sbuf(f"mid_ts{sfx}{i}", [128, 512], F32, dma_target=True,
                         sem=f"mid_ts{i}") for i in range(2)]

        if pre == "fw":
            xs_off = 3
            taps = [(0, 0), (1, 1), (2, 2), (3, 3)]
        else:
            xs_off = 0
            taps = [(3, 0), (2, 1), (1, 2), (0, 3)]
        carry_slc = slice(0, 3) if pre == "fw" else slice(SEG, SEG + 3)
        P.op("g", lambda e, s: e.memset(hstate.h[:], 0.0), w=[hstate])
        segs = list(range(nseg)) if pre == "fw" else list(range(nseg - 1, -1, -1))

        def load_slab(j):
            sl = winsl[j % 2]
            P.dma("y", sl, lambda sl=sl: sl.h[:],
                  spill[pre + "_win"], lambda j=j, pre=pre:
                  spill[pre + "_win"].h.ap()[:, :, 256 * j:256 * j + 256]
                  .rearrange("a p n -> p a n"))

        def ln_chunk(seg, tt):
            def emit():
                p = tt % 2
                xb = xnbt[p]
                emit_ln_tile(seg, tt, f"M{pre}_ln{seg}", out=xb, skip_dma=True)
                if tt + 2 < NTT:
                    ln_dma(seg, tt + 2, f"M{pre}_ln{seg}")
                for fc in range(NFC):
                    P.op("t", lambda e, s, xb=xb, fc=fc: e.transpose(
                        pT.h[:], xb.h[:, fc * 128:(fc + 1) * 128], ident.h[:]),
                        r=[xb, ident], w=[pT])
                    P.op("v", lambda e, s, tt=tt, fc=fc: e.tensor_copy(
                        xnT.h[:, fc, tt * 128:(tt + 1) * 128], pT.h[:]),
                        r=[pT], w=[xnT])
            return emit

        def x_chunk(seg, dhi, first):
            ps = [pA, pB][dhi % 2]
            xb = xsb[dhi]

            def emit_mm():
                P.cur_tag = f"M{pre}_xz{seg}"
                sl = winsl[(dhi // 2) % 2]
                jo = dhi % 2
                for kc in range(NFC):
                    P.op("t", lambda e, s, sl=sl, jo=jo, kc=kc: e.matmul(
                        ps.h[:, 0:SEG], sl.h[:, kc, jo * 128:(jo + 1) * 128],
                        xnT.h[:, kc, :], start=(kc == 0), stop=(kc == NFC - 1)),
                        r=[sl, xnT], w=[ps])
                if not first:
                    if pre == "fw":
                        P.op("v", lambda e, s: e.tensor_copy(
                            xb.h[:, 0:3], xb.h[:, SEG:SEG + 3]), r=[xb], w=[xb])
                    else:
                        P.op("v", lambda e, s: e.tensor_copy(
                            xb.h[:, SEG:SEG + 3], xb.h[:, 0:3]), r=[xb], w=[xb])
                if dhi % 2 == 1 and dhi // 2 + 2 < NFC:
                    load_slab(dhi // 2 + 2)

            def emit_ev():
                P.cur_tag = f"M{pre}_xz{seg}"
                P.op("v", lambda e, s: e.tensor_copy(
                    xb.h[:, xs_off:xs_off + SEG], ps.h[:, 0:SEG]), r=[ps], w=[xb])
            return emit_mm, emit_ev

        def conv_taps(seg, dhi):
            P.cur_tag = f"M{pre}_cv{seg}"
            xb = xsb[dhi]
            k0, o0 = taps[0]
            P.op("v", lambda e, s, xb=xb, dhi=dhi, k0=k0, o0=o0: e.tensor_scalar(
                convf.h[:], xb.h[:, o0:o0 + SEG],
                convw_s[pre].h[:, dhi, k0:k0 + 1], None, ALU.mult),
                r=[xb, convw_s[pre]], w=[convf])
            for kk, off in taps[1:]:
                P.op("v", lambda e, s, xb=xb, dhi=dhi, kk=kk, off=off:
                     e.scalar_tensor_tensor(
                         convf.h[:], xb.h[:, off:off + SEG],
                         convw_s[pre].h[:, dhi, kk:kk + 1], convf.h[:],
                         ALU.mult, ALU.add),
                     r=[xb, convw_s[pre], convf], w=[convf])

        def conv_silu(seg, dhi):
            P.cur_tag = f"M{pre}_cv{seg}"
            u = ub[dhi]
            P.op("s", lambda e, s, u=u, dhi=dhi: e.activation(
                u.h[:], convf.h[:], AF.Silu,
                bias=convb_s[pre].h[:, dhi:dhi + 1], scale=1.0),
                r=[convf, convb_s[pre]], w=[u])

        def xdbl_mm(seg, dhi):
            P.cur_tag = f"M{pre}_cv{seg}"
            u = ub[dhi]
            P.op("t", lambda e, s, u=u, dhi=dhi: e.matmul(
                pRS.h[:, 0:SEG], wxT.h[:, dhi, :], u.h[:],
                start=(dhi == 0), stop=(dhi == NDHI - 1)),
                r=[wxT, u], w=[pRS])

        def cxd_pair(seg, g):
            for dhi in (2 * g, 2 * g + 1):
                conv_taps(seg, dhi)
                conv_silu(seg, dhi)
                xdbl_mm(seg, dhi)

        def z_pair(seg, p, via_wq=False):
            P.cur_tag = f"M{pre}_z{seg}"
            if via_wq:
                sl, ro = wq[(p % 4) // 2], (p % 2) * NFC
            else:
                sl, ro = winsl[p % 2], 0
            for jo in range(2):
                zi = 2 * p + jo
                ps = [pA, pB][jo]
                for kc in range(NFC):
                    P.op("t", lambda e, s, ps=ps, sl=sl, ro=ro, jo=jo, kc=kc:
                         e.matmul(
                             ps.h[:, 0:SEG], sl.h[:, ro + kc, jo * 128:(jo + 1) * 128],
                             xnT.h[:, kc, :], start=(kc == 0), stop=(kc == NFC - 1)),
                         r=[sl, xnT], w=[ps])
                P.op("s", lambda e, s, ps=ps, zi=zi: e.activation(
                    szb[zi].h[:], ps.h[:, 0:SEG], AF.Silu),
                    r=[ps], w=[szb[zi]])

        def z_load_wq(p):
            b, ro = wq[(p % 4) // 2], (p % 2) * NFC
            j = NFC + p
            P.dma("y", b, lambda b=b, ro=ro: b.h[:, ro:ro + NFC, :],
                  spill[pre + "_win"], lambda j=j, pre=pre:
                  spill[pre + "_win"].h.ap()[:, :, 256 * j:256 * j + 256]
                  .rearrange("a p n -> p a n"))

        def chains_wdt(i, g, j):
            P.cur_tag = f"M{pre}_dt"
            dhi = 2 * g + j
            dpb = dtbc[i % 2]
            P.op("t", lambda e, s, dhi=dhi, dpb=dpb: e.matmul(
                pD.h[:, 0:SEG], wdtT.h[:, dhi * 128:(dhi + 1) * 128],
                dpb.h[0:DT_RANK, :], start=True, stop=True),
                r=[wdtT, dpb], w=[pD])

        def chains_sc(i, g, j):
            P.cur_tag = f"M{pre}_dt"
            dhi = 2 * g + j
            P.op("s", lambda e, s, dhi=dhi: e.activation(
                dtcA.h[:], pD.h[:, 0:SEG], AF.Identity,
                bias=bdt_s[pre].h[:, dhi:dhi + 1], scale=1.0),
                r=[pD, bdt_s[pre]], w=[dtcA])
            P.op("s", lambda e, s: e.activation(
                dtcB.h[:], dtcA.h[:], AF.Abs), r=[dtcA], w=[dtcB])
            P.op("s", lambda e, s: e.activation(
                dtcB.h[:], dtcB.h[:], AF.Exp, bias=0.0, scale=-1.0),
                r=[dtcB], w=[dtcB])
            P.op("s", lambda e, s: e.activation(
                dtcB.h[:], dtcB.h[:], AF.Ln, bias=1.0, scale=1.0),
                r=[dtcB], w=[dtcB])
            P.op("s", lambda e, s: e.activation(
                dtcA.h[:], dtcA.h[:], AF.Relu), r=[dtcA], w=[dtcA])

        def chains_v(i, g, j):
            P.cur_tag = f"M{pre}_dt"
            gp = g % 2
            dhi = 2 * g + j
            P.op("v", lambda e, s, gp=gp, j=j: e.tensor_add(
                d1g[gp].h[:, j, :], dtcA.h[:], dtcB.h[:]),
                r=[dtcA, dtcB], w=[d1g[gp]])
            P.op("v", lambda e, s, gp=gp, j=j, dhi=dhi: e.tensor_mul(
                dug[gp].h[:, j, :], d1g[gp].h[:, j, :], ub[dhi].h[:]),
                r=[d1g[gp], ub[dhi]], w=[dug[gp]])

        def chains(i, g):
            for j in range(2):
                chains_wdt(i, g, j)
                chains_sc(i, g, j)
                chains_v(i, g, j)

        def boundary(i, seg):
            P.cur_tag = f"M{pre}_bc{seg}"
            t0 = seg * SEG
            bc = dtbc[i % 2]
            P.op("v", lambda e, s, bc=bc: e.tensor_copy(
                bc.h[:], pRS.h[0:DT_RANK + 32, 0:SEG]), r=[pRS], w=[bc])
            P.dma("g", spill[pre + "_bc"],
                  lambda t0=t0, pre=pre: spill[pre + "_bc"].h.ap()[:, t0:t0 + SEG],
                  bc, lambda bc=bc: bc.h[DT_RANK:DT_RANK + 32, :])
            for st in range(D_STATE):
                P.dma("y", BrepS[st], lambda st=st: BrepS[st].h[:],
                      spill[pre + "_bc"], lambda st=st, t0=t0, pre=pre:
                      spill[pre + "_bc"].h.ap()[st:st + 1, t0:t0 + SEG]
                      .broadcast_to([128, SEG]))
                P.dma("y", CrepS[st], lambda st=st: CrepS[st].h[:],
                      spill[pre + "_bc"], lambda st=st, t0=t0, pre=pre:
                      spill[pre + "_bc"].h.ap()[16 + st:17 + st, t0:t0 + SEG]
                      .broadcast_to([128, SEG]))
            chains(i, 0)

        def y2f(i, g):
            P.cur_tag = f"M{pre}_y2"
            for j, dhi in enumerate((2 * g, 2 * g + 1)):
                pc = [pC1, pC2][j]
                yb = ydt[j]
                y2 = y2sg[i % 2][dhi]
                P.op("s", lambda e, s, yb=yb, dhi=dhi: e.activation(
                    yb.h[:], ub[dhi].h[:], AF.Identity,
                    bias=0.0, scale=Dp_s[pre].h[:, dhi:dhi + 1]),
                    r=[ub[dhi], Dp_s[pre]], w=[yb])
                P.op("v", lambda e, s, pc=pc, yb=yb: e.tensor_add(
                    tq_f32.h[:], pc.h[:, 0:SEG], yb.h[:]), r=[pc, yb], w=[tq_f32])
                P.op("v", lambda e, s, y2=y2, dhi=dhi: e.tensor_mul(
                    y2.h[:], tq_f32.h[:], szb[dhi].h[:]),
                    r=[tq_f32, szb[dhi]], w=[y2])

        def scan_group(i, seg, g, side, nxt):
            P.cur_tag = f"M{pre}_sc{seg}"
            gp = g % 2
            for st in range(D_STATE):
                for j, dhi in enumerate((2 * g, 2 * g + 1)):
                    ip = (2 * st + j) % 2
                    hci = (2 * st + j) % NHC
                    pc = [pC1, pC2][j]
                    asl, bsl, hsl, hcs = a_sl[ip], b_sl[ip], h_sl[ip], hc_sl[hci]
                    P.op("s", lambda e, s, asl=asl, gp=gp, j=j, dhi=dhi, st=st:
                         e.activation(
                             asl.h[:], d1g[gp].h[:, j, :], AF.Exp,
                             bias=0.0, scale=Acoef[pre].h[:, dhi, st:st + 1]),
                         r=[d1g[gp], Acoef[pre]], w=[asl])
                    P.op("v", lambda e, s, bsl=bsl, gp=gp, j=j, st=st: e.tensor_mul(
                        bsl.h[:], dug[gp].h[:, j, :], BrepS[st].h[:]),
                        r=[dug[gp], BrepS[st]], w=[bsl])
                    if not rev:
                        P.op("v", lambda e, s, asl=asl, bsl=bsl, hsl=hsl,
                             dhi=dhi, st=st: e.tensor_tensor_scan(
                                 hsl.h[:], asl.h[:], bsl.h[:],
                                 hstate.h[:, dhi, st:st + 1], ALU.mult, ALU.add),
                             r=[asl, bsl, hstate], w=[hsl])
                        P.op("v", lambda e, s, hsl=hsl, dhi=dhi, st=st: e.tensor_copy(
                            hstate.h[:, dhi, st:st + 1], hsl.h[:, SEG - 1:SEG]),
                            r=[hsl], w=[hstate])
                    else:
                        P.op("v", lambda e, s, asl=asl, bsl=bsl, hsl=hsl,
                             dhi=dhi, st=st: e.tensor_tensor_scan(
                                 hsl.h[:, ::-1], asl.h[:, ::-1], bsl.h[:, ::-1],
                                 hstate.h[:, dhi, st:st + 1], ALU.mult, ALU.add),
                             r=[asl, bsl, hstate], w=[hsl])
                        P.op("v", lambda e, s, hsl=hsl, dhi=dhi, st=st: e.tensor_copy(
                            hstate.h[:, dhi, st:st + 1], hsl.h[:, 0:1]),
                            r=[hsl], w=[hstate])
                    P.op("v", lambda e, s, hsl=hsl, hcs=hcs, st=st: e.tensor_mul(
                        hcs.h[:], hsl.h[:], CrepS[st].h[:]),
                        r=[hsl, CrepS[st]], w=[hcs])
                    P.op("t", lambda e, s, pc=pc, hcs=hcs, st=st: e.matmul(
                        pc.h[:, 0:SEG], ident.h[:], hcs.h[:],
                        start=(st == 0), stop=(st == D_STATE - 1)),
                        r=[ident, hcs], w=[pc])
                if st == 1:
                    if g + 1 < NG:
                        chains_wdt(i, g + 1, 0)
                    if nxt is not None and g >= 1:
                        conv_silu(nxt, 2 * (g - 1))
                        conv_taps(nxt, 2 * (g - 1) + 1)
                elif st == 2 and nxt is not None and g >= 1:
                    conv_silu(nxt, 2 * (g - 1) + 1)
                    xdbl_mm(nxt, 2 * (g - 1))
                    xdbl_mm(nxt, 2 * (g - 1) + 1)
                elif st == 5 and g + 1 < NG:
                    chains_sc(i, g + 1, 0)
                elif st == 8 and g + 1 < NG:
                    chains_wdt(i, g + 1, 1)
                elif st == 9 and g + 1 < NG:
                    chains_v(i, g + 1, 0)
                elif st == 12 and g + 1 < NG:
                    chains_sc(i, g + 1, 1)
                if (st % 2 == 1 or st in (4, 6, 10)) and side:
                    side.pop(0)()
                P.cur_tag = f"M{pre}_sc{seg}"
            if g + 1 < NG:
                chains_v(i, g + 1, 1)
            y2f(i, g)

        def wcomb_chunks(ip, seg):
            t0p = seg * SEG
            y2p = y2sg[ip % 2]
            out = []
            for h in range(2):
                for tt in range(NTT):
                    def c1(h=h, tt=tt):
                        P.cur_tag = f"M{pre}_wc{seg}"
                        if tt == 0:
                            for q in range(2):
                                wqb = wq[q]
                                for dc in range(NDHI):
                                    P.dma("y", wqb,
                                          lambda wqb=wqb, dc=dc: wqb.h[:, dc, :],
                                          spill[pre + "_wc"],
                                          lambda dc=dc, h=h, q=q, pre=pre:
                                          spill[pre + "_wc"].h.ap()
                                          [dc, :, 512 * h + 256 * q:
                                           512 * h + 256 * q + 256])
                        for dc in range(NDHI):
                            P.op("t", lambda e, s, dc=dc, tt=tt: e.matmul(
                                pW.h[:, 0:256],
                                y2p[dc].h[:, tt * 128:(tt + 1) * 128],
                                wq[0].h[:, dc, :], start=(dc == 0),
                                stop=(dc == NDHI - 1 and not rev)),
                                r=[y2p[dc], wq[0]], w=[pW])
                        if rev:
                            P.op("t", lambda e, s, h=h: e.matmul(
                                pW.h[:, 0:256], onesb.h[:],
                                fbias.h[0:1, 512 * h:512 * h + 256],
                                start=False, stop=True),
                                r=[onesb, fbias], w=[pW])

                    def c2m(h=h, tt=tt):
                        P.cur_tag = f"M{pre}_wc{seg}"
                        for dc in range(NDHI):
                            P.op("t", lambda e, s, dc=dc, tt=tt: e.matmul(
                                pW.h[:, 256:512],
                                y2p[dc].h[:, tt * 128:(tt + 1) * 128],
                                wq[1].h[:, dc, :], start=(dc == 0),
                                stop=(dc == NDHI - 1 and not rev)),
                                r=[y2p[dc], wq[1]], w=[pW])
                        if rev:
                            P.op("t", lambda e, s, h=h: e.matmul(
                                pW.h[:, 256:512], onesb.h[:],
                                fbias.h[0:1, 512 * h + 256:512 * h + 512],
                                start=False, stop=True),
                                r=[onesb, fbias], w=[pW])
                        mts = mid_ts[tt % 2]
                        if pre == "fw":
                            P.dma("y", mts, lambda mts=mts: mts.h[:],
                                  x_in, lambda t0p=t0p, tt=tt, h=h:
                                  x_in.h.ap()[t0p + tt * 128:t0p + (tt + 1) * 128,
                                              512 * h:512 * h + 512])
                        else:
                            mq = mid_q[2 * h + tt % 2]
                            P.dma("y", mts, lambda mts=mts: mts.h[:],
                                  mq, lambda t0p=t0p, tt=tt, mq=mq:
                                  mq.h.ap()
                                  [t0p + tt * 128:t0p + (tt + 1) * 128, :])

                    def c2v(h=h, tt=tt):
                        P.cur_tag = f"M{pre}_wc{seg}"
                        mts = mid_ts[tt % 2]
                        P.op("v", lambda e, s, mts=mts: e.tensor_add(
                            mts.h[:], mts.h[:], pW.h[:, 0:512]),
                            r=[pW, mts], w=[mts])
                        mq = mid_q[2 * h + tt % 2]
                        P.dma("g", mq,
                              lambda t0p=t0p, tt=tt, mq=mq: mq.h.ap()
                              [t0p + tt * 128:t0p + (tt + 1) * 128, :],
                              mts, lambda mts=mts: mts.h[:])
                    out.append(c1)
                    out.append(c2m)
                    out.append(c2v)
            return out

        for i, seg in enumerate(segs):
            nxt = segs[i + 1] if i + 1 < nseg else None
            prv = segs[i - 1] if i >= 1 else None

            if i == 0:
                P.cur_tag = f"M{pre}_pro"
                for dhi in range(NDHI):
                    P.op("g", lambda e, s, dhi=dhi: e.memset(
                        xsb[dhi].h[:, carry_slc], 0.0), w=[xsb[dhi]])
                load_slab(0)
                load_slab(1)
                ln_dma(seg, 0, f"M{pre}_pro")
                ln_dma(seg, 1, f"M{pre}_pro")
                for tt in range(NTT):
                    ln_chunk(seg, tt)()
                for dhi in range(NDHI):
                    mm, ev = x_chunk(seg, dhi, True)
                    mm()
                    ev()
                for g in range(NG):
                    cxd_pair(seg, g)
                for p in range(NG):
                    load_slab(NFC + p)
                    z_pair(seg, p)
                if nxt is not None:
                    ln_dma(nxt, 0, f"M{pre}_pro")
                    ln_dma(nxt, 1, f"M{pre}_pro")
            boundary(i, seg)

            side = []
            if nxt is not None:
                side = [ln_chunk(nxt, tt) for tt in range(NTT)]
                side.append(lambda: (load_slab(0), load_slab(1)))
                pend = []
                for k in range(NDHI):
                    mm, ev = x_chunk(nxt, k, False)
                    side.append(mm)
                    pend.append(ev)
                    if k >= 1:
                        side.append(pend.pop(0))
                side += pend
            w_ = wcomb_chunks(i - 1, prv) if prv is not None else []
            merged = []
            wi = 0
            for idx, p_ in enumerate(side):
                merged.append(p_)
                if idx >= 12 and idx % 3 == 2 and wi < len(w_):
                    merged.append(w_[wi])
                    wi += 1
            side = merged + w_[wi:]

            for g in range(NG):
                scan_group(i, seg, g, side, nxt)
                if nxt is not None:
                    conv_taps(nxt, 2 * g)
                    if g == 5:
                        for p in range(4):
                            z_load_wq(p)
                    elif g == 6:
                        if i + 2 < nseg:
                            ln_dma(segs[i + 2], 0, f"M{pre}_ln")
                            ln_dma(segs[i + 2], 1, f"M{pre}_ln")
                        for p in range(4):
                            z_pair(nxt, p, via_wq=True)
                        for p in range(4, 8):
                            z_load_wq(p)
                    elif g == 7:
                        for p in range(4, 8):
                            z_pair(nxt, p, via_wq=True)
            if nxt is not None:
                conv_silu(nxt, NDHI - 2)
                conv_taps(nxt, NDHI - 1)
                conv_silu(nxt, NDHI - 1)
                xdbl_mm(nxt, NDHI - 2)
                xdbl_mm(nxt, NDHI - 1)
            while side:
                side.pop(0)()
        for c in wcomb_chunks(nseg - 1, segs[-1]):
            c()
        P.close_group()

    def sweep_D():
        P.open_group()
        P.cur_tag = "D_alloc"
        lnout[0] = P.sbuf("lnout0", [128, D_MODEL], F32)
        lnout[1] = P.sbuf("lnout1", [128, D_MODEL], F32)
        w1t = P.sbuf("w1t", [128, NFC, NHID], BF16)
        w2t = P.sbuf("w2t", [128, NHID // 128, D_MODEL], BF16)
        xq1 = [P.sbuf("xq1_0", [128, 512], BF16)] * 2
        xq2 = [P.sbuf(f"xq2_{i}", [128, 512], BF16) for i in range(2)]
        i32q = P.sbuf("i32q", [128, 512], I32)
        rnd_a = P.sbuf("rnd_a", [128, 512], F32)
        rnd_b = P.sbuf("rnd_b", [128, 512], F32)

        def emit_round(src_fn, src_bufs, out_buf, out_fn, n):
            assert n <= 512
            P.op("v", lambda e, s: e.tensor_copy(i32q.h[:, 0:n], src_fn()),
                 r=list(src_bufs), w=[i32q])
            P.op("v", lambda e, s: e.tensor_single_scalar(
                rnd_a.h[:, 0:n], i32q.h[:, 0:n], -1.0, ALU.add),
                r=[i32q], w=[rnd_a])
            P.op("v", lambda e, s: e.tensor_sub(
                rnd_b.h[:, 0:n], src_fn(), rnd_a.h[:, 0:n]),
                r=list(src_bufs) + [rnd_a], w=[rnd_b])
            P.op("v", lambda e, s: e.scalar_tensor_tensor(
                rnd_a.h[:, 0:n], rnd_b.h[:, 0:n], 1.5, rnd_a.h[:, 0:n],
                ALU.is_ge, ALU.add), r=[rnd_a, rnd_b], w=[rnd_a])
            P.op("v", lambda e, s: e.scalar_tensor_tensor(
                out_fn(), rnd_b.h[:, 0:n], 0.5, rnd_a.h[:, 0:n],
                ALU.is_gt, ALU.add), r=[rnd_a, rnd_b], w=[out_buf])

        def ternarize(stages, dst, dram_buf, nrow, ncol, s_dst_col, tag):
            P.cur_tag = tag
            nchk = nrow // 128
            scols = stages[0].h.ap().shape[1]
            P.op("g", lambda e, s: e.memset(colsB.h[:, 0:1], 0.0), w=[colsB])
            i = 0
            for c in range(nchk):
                for j0 in range(0, ncol, scols):
                    j1 = min(j0 + scols, ncol)
                    stage = stages[i % 2]
                    i += 1
                    P.dma("y", stage,
                          lambda stage=stage, j0=j0, j1=j1: stage.h[:, 0:j1 - j0],
                          dram_buf, lambda c=c, j0=j0, j1=j1: dram_buf.h.ap()
                          .rearrange("(a p) n -> p a n", p=128)[:, c, j0:j1])
                    P.op("v", lambda e, s, stage=stage, j0=j0, j1=j1: e.tensor_reduce(
                        colsB.h[:, 1:2], stage.h[:, 0:j1 - j0], axis=AX.X,
                        op=ALU.add, apply_absolute_value=True),
                        r=[stage], w=[colsB])
                    P.op("v", lambda e, s: e.tensor_add(
                        colsB.h[:, 0:1], colsB.h[:, 0:1], colsB.h[:, 1:2]),
                        r=[colsB], w=[colsB])
            P.op("t", lambda e, s: e.matmul(pD.h[0:1, 0:1], colsB.h[:, 0:1],
                                            ones_col.h[:], start=True, stop=True),
                 r=[colsB, ones_col], w=[pD])
            P.op("v", lambda e, s: e.tensor_copy(colsB.h[0:1, 2:3], pD.h[0:1, 0:1]),
                 r=[pD], w=[colsB])
            P.op("t", lambda e, s: e.matmul(pD.h[:, 0:1], onesf.h[:],
                                            colsB.h[0:1, 2:3], start=True, stop=True),
                 r=[onesf, colsB], w=[pD])
            nel = float(nrow) * ncol
            P.op("v", lambda e, s, nel=nel: e.tensor_single_scalar(
                colsB.h[:, 3:4], pD.h[:, 0:1], 0.5 / nel, ALU.mult),
                r=[pD], w=[colsB])
            P.op("v", lambda e, s: e.tensor_single_scalar(
                colsB.h[:, 4:5], colsB.h[:, 3:4], -1.0, ALU.mult), r=[colsB], w=[colsB])
            i = 0
            for c in range(nchk):
                for j0 in range(0, ncol, scols):
                    j1 = min(j0 + scols, ncol)
                    stage = stages[i % 2]
                    i += 1
                    P.dma("y", stage,
                          lambda stage=stage, j0=j0, j1=j1: stage.h[:, 0:j1 - j0],
                          dram_buf, lambda c=c, j0=j0, j1=j1: dram_buf.h.ap()
                          .rearrange("(a p) n -> p a n", p=128)[:, c, j0:j1])
                    for q0 in range(0, j1 - j0, 512):
                        qn = min(512, j1 - j0 - q0)
                        P.op("v", lambda e, s, stage=stage, q0=q0, qn=qn: e.tensor_scalar(
                            rnd_a.h[:, 0:qn], stage.h[:, q0:q0 + qn],
                            colsB.h[:, 3:4], None, ALU.is_ge),
                            r=[stage, colsB], w=[rnd_a])
                        P.op("v", lambda e, s, stage=stage, q0=q0, qn=qn: e.tensor_scalar(
                            rnd_b.h[:, 0:qn], stage.h[:, q0:q0 + qn],
                            colsB.h[:, 4:5], None, ALU.is_le),
                            r=[stage, colsB], w=[rnd_b])
                        P.op("v", lambda e, s, c=c, j0=j0, q0=q0, qn=qn: e.tensor_sub(
                            dst.h[:, c, j0 + q0:j0 + q0 + qn],
                            rnd_a.h[:, 0:qn], rnd_b.h[:, 0:qn]),
                            r=[rnd_a, rnd_b], w=[dst])
            P.op("v", lambda e, s, s_dst_col=s_dst_col: e.tensor_single_scalar(
                cols.h[:, s_dst_col:s_dst_col + 1], colsB.h[:, 3:4], 2.0, ALU.mult),
                r=[colsB], w=[cols])

        P.open_group()
        wstgD = [P.sbuf(f"wstgD{i}", [128, 2048], F32, dma_target=True)
                 for i in range(2)]
        ternarize(wstgD, w1t, w1T_d, D_MODEL, NHID, 9, "D_w1")
        ternarize(wstgD, w2t, w2T_d, NHID, D_MODEL, 10, "D_w2")
        P.close_group()
        xq1T = [P.sbuf(f"xq1T{i}", [128, NFC, 128], BF16) for i in range(2)]
        xq2T = [P.sbuf(f"xq2T{i}", [128, NHID // 128, 128], BF16) for i in range(2)]
        hbuf = P.sbuf("hbuf", [128, NHID], F32)
        out_t = [P.sbuf(f"out_t{i}", [128, 512], F32, dma_target=True)
                 for i in range(4)]
        qv = rnd_b
        gcol = [P.sbuf(f"gcol{i}", [128, 1], F32) for i in range(2)]
        ocol = [P.sbuf(f"ocol{i}", [128, 1], F32) for i in range(2)]
        outw = []
        for wi in range(4):
            b = Buf(f"outw{wi}", out_d.h, sem=f"outw{wi}", is_dram=True)
            P.bufs.append(b)
            outw.append(b)

        NT_ALL = T // 128

        def stage_A(t):
            P.cur_tag = f"D_A{t}"
            dp = t % 2
            x1T = xq1T[dp]
            emit_ln_tile(0, t, "D_ln", skip_dma=True)
            if t + 2 < NT_ALL:
                ln_dma(0, t + 2, "D_ln")
            lo = lnout[dp]
            P.op("v", lambda e, s, lo=lo: e.reduce_max(
                cols.h[:, 43:44], lo.h[:], axis=AX.X,
                apply_absolute_value=True), r=[lo], w=[cols])
            P.op("v", lambda e, s: e.reciprocal(cols.h[:, 44:45], cols.h[:, 43:44]),
                 r=[cols], w=[cols])
            P.op("v", lambda e, s: e.tensor_single_scalar(
                cols.h[:, 44:45], cols.h[:, 44:45], 127.0, ALU.mult),
                r=[cols], w=[cols])
            P.op("v", lambda e, s: e.reciprocal(cols.h[:, 45:46], cols.h[:, 44:45]),
                 r=[cols], w=[cols])
            for qi in range(2):
                q0 = qi * 512
                xq1b = xq1[qi]
                P.op("v", lambda e, s, lo=lo, q0=q0: e.tensor_scalar(
                    qv.h[:], lo.h[:, q0:q0 + 512], cols.h[:, 44:45],
                    None, ALU.mult), r=[lo, cols], w=[qv])
                emit_round(lambda: qv.h[:], [qv], xq1b,
                           lambda xq1b=xq1b: xq1b.h[:], 512)
                for j in range(4):
                    fc = qi * 4 + j
                    P.op("t", lambda e, s, xq1b=xq1b, j=j: e.transpose(
                        pT.h[:], xq1b.h[:, j * 128:(j + 1) * 128], ident.h[:]),
                        r=[xq1b, ident], w=[pT])
                    P.op("v", lambda e, s, x1T=x1T, fc=fc: e.tensor_copy(
                        x1T.h[:, fc, :], pT.h[:]), r=[pT], w=[x1T])
            gc = gcol[dp]
            P.op("v", lambda e, s, gc=gc: e.tensor_mul(
                gc.h[:], cols.h[:, 9:10], cols.h[:, 45:46]),
                r=[cols], w=[gc])

        def stage_L(t):
            P.cur_tag = f"D_L{t}"
            for h in range(2):
                ot = out_t[2 * h + t % 2]
                mq = mid_q[2 * h + t % 2]
                P.dma("y", ot, lambda ot=ot: ot.h[:],
                      mq, lambda t=t, mq=mq: mq.h.ap()
                      [t * 128:(t + 1) * 128, :])

        def stage_B(t):
            P.cur_tag = f"D_B{t}"
            x1T = xq1T[t % 2]
            for hc in range(NHID // 512):
                ps = [pA, pB][hc % 2]
                for fc in range(NFC):
                    P.op("t", lambda e, s, ps=ps, x1T=x1T, hc=hc, fc=fc: e.matmul(
                        ps.h[:, 0:512], x1T.h[:, fc, :],
                        w1t.h[:, fc, hc * 512:(hc + 1) * 512],
                        start=(fc == 0), stop=(fc == NFC - 1)),
                        r=[x1T, w1t], w=[ps])
                gc = gcol[t % 2]
                P.op("s", lambda e, s, ps=ps, hc=hc, gc=gc: e.activation(
                    hbuf.h[:, hc * 512:(hc + 1) * 512], ps.h[:, 0:512],
                    AF.Gelu, bias=0.0, scale=gc.h[:]),
                    r=[ps, gc], w=[hbuf])

        def stage_C(t):
            P.cur_tag = f"D_C{t}"
            x2T = xq2T[t % 2]
            P.op("v", lambda e, s: e.reduce_max(
                cols.h[:, 43:44], hbuf.h[:], axis=AX.X,
                apply_absolute_value=True), r=[hbuf], w=[cols])
            P.op("v", lambda e, s: e.tensor_single_scalar(
                cols.h[:, 43:44], cols.h[:, 43:44], EPS_Q, ALU.max),
                r=[cols], w=[cols])
            P.op("v", lambda e, s: e.reciprocal(cols.h[:, 44:45], cols.h[:, 43:44]),
                 r=[cols], w=[cols])
            P.op("v", lambda e, s: e.tensor_single_scalar(
                cols.h[:, 44:45], cols.h[:, 44:45], 127.0, ALU.mult),
                r=[cols], w=[cols])
            P.op("v", lambda e, s: e.reciprocal(cols.h[:, 45:46], cols.h[:, 44:45]),
                 r=[cols], w=[cols])
            for qi in range(8):
                q0 = qi * 512
                xq2b = xq2[qi % 2]
                P.op("v", lambda e, s, q0=q0: e.tensor_scalar(
                    qv.h[:], hbuf.h[:, q0:q0 + 512],
                    cols.h[:, 44:45], None, ALU.mult),
                    r=[hbuf, cols], w=[qv])
                emit_round(lambda: qv.h[:], [qv], xq2b,
                           lambda xq2b=xq2b: xq2b.h[:], 512)
                for j in range(4):
                    k = qi * 4 + j
                    P.op("t", lambda e, s, xq2b=xq2b, j=j: e.transpose(
                        pT.h[:], xq2b.h[:, j * 128:(j + 1) * 128], ident.h[:]),
                        r=[xq2b, ident], w=[pT])
                    P.op("v", lambda e, s, x2T=x2T, k=k: e.tensor_copy(
                        x2T.h[:, k, :], pT.h[:]), r=[pT], w=[x2T])
            oc = ocol[t % 2]
            P.op("v", lambda e, s, oc=oc: e.tensor_mul(
                oc.h[:], cols.h[:, 10:11], cols.h[:, 45:46]),
                r=[cols], w=[oc])

        def stage_Dm(t):
            P.cur_tag = f"D_D{t}"
            x2T = xq2T[t % 2]
            for half in range(2):
                pc = [pC1, pC2][half]
                for k in range(NHID // 128):
                    P.op("t", lambda e, s, pc=pc, x2T=x2T, k=k, half=half: e.matmul(
                        pc.h[:], x2T.h[:, k, :],
                        w2t.h[:, k, half * 512:(half + 1) * 512],
                        start=(k == 0), stop=(k == NHID // 128 - 1)),
                        r=[x2T, w2t], w=[pc])

        def stage_E(t):
            P.cur_tag = f"D_E{t}"
            for half in range(2):
                pc = [pC1, pC2][half]
                ot = out_t[2 * half + t % 2]
                oc = ocol[t % 2]
                P.op("v", lambda e, s, pc=pc, ot=ot, oc=oc: e.scalar_tensor_tensor(
                    ot.h[:], pc.h[:], oc.h[:], ot.h[:],
                    ALU.mult, ALU.add),
                    r=[pc, oc, ot], w=[ot])
                ow = outw[half * 2 + t % 2]
                P.dma("gy"[half], ow,
                      lambda t=t, half=half: out_d.h.ap()
                      [t * 128:(t + 1) * 128, half * 512:(half + 1) * 512],
                      ot, lambda ot=ot: ot.h[:])

        ln_dma(0, 0, "D_ln")
        ln_dma(0, 1, "D_ln")
        stage_A(0)
        if NT_ALL > 1:
            stage_A(1)
        stage_B(0)
        if NT_ALL > 2:
            stage_A(2)
        stage_C(0)
        if NT_ALL > 1:
            stage_B(1)
        stage_L(0)
        for t in range(NT_ALL):
            stage_Dm(t)
            if t + 1 < NT_ALL:
                stage_L(t + 1)
            if t + 3 < NT_ALL:
                stage_A(t + 3)
            if t + 1 < NT_ALL:
                stage_C(t + 1)
            if t + 2 < NT_ALL:
                stage_B(t + 2)
            stage_E(t)
        P.close_group()

    sweep_ABC("fw")
    sweep_ABC("bw")
    sweep_D()

    P.cur_tag = "final"
    P.barrier_on("y", [out_d])
    P.finalize()
    return nc



def kernel(L0_attn_x, L0_attn_fw_Win, L0_attn_fw_convw, L0_attn_fw_convb,
           L0_attn_fw_Wx, L0_attn_fw_Wdt, L0_attn_fw_bdt, L0_attn_fw_Alog,
           L0_attn_fw_D, L0_attn_fw_Wout, L0_attn_bw_Win, L0_attn_bw_convw,
           L0_attn_bw_convb, L0_attn_bw_Wx, L0_attn_bw_Wdt, L0_attn_bw_bdt,
           L0_attn_bw_Alog, L0_attn_bw_D, L0_attn_bw_Wout, L0_attn_fuse_W,
           L0_attn_fuse_b, L0_ffn_w1, L0_ffn_w2):
    x = np.ascontiguousarray(np.asarray(L0_attn_x, dtype=np.float32))
    B, T, D = x.shape
    nc = build_program(T=T)

    def tr(a):
        return np.ascontiguousarray(np.asarray(a, dtype=np.float32).T)

    common = {
        "fw_WinT": tr(L0_attn_fw_Win), "bw_WinT": tr(L0_attn_bw_Win),
        "fw_WxT": tr(L0_attn_fw_Wx), "bw_WxT": tr(L0_attn_bw_Wx),
        "fw_WdtT": tr(L0_attn_fw_Wdt), "bw_WdtT": tr(L0_attn_bw_Wdt),
        "fw_Wout_nat": np.ascontiguousarray(np.asarray(L0_attn_fw_Wout, np.float32)),
        "bw_Wout_nat": np.ascontiguousarray(np.asarray(L0_attn_bw_Wout, np.float32)),
        "fw_convw": np.ascontiguousarray(np.asarray(L0_attn_fw_convw, np.float32).reshape(D_INNER, D_CONV)),
        "bw_convw": np.ascontiguousarray(np.asarray(L0_attn_bw_convw, np.float32).reshape(D_INNER, D_CONV)),
        "fw_convb": np.ascontiguousarray(np.asarray(L0_attn_fw_convb, np.float32)),
        "bw_convb": np.ascontiguousarray(np.asarray(L0_attn_bw_convb, np.float32)),
        "fw_bdt": np.ascontiguousarray(np.asarray(L0_attn_fw_bdt, np.float32)),
        "bw_bdt": np.ascontiguousarray(np.asarray(L0_attn_bw_bdt, np.float32)),
        "fw_Alog": np.ascontiguousarray(np.asarray(L0_attn_fw_Alog, np.float32)),
        "bw_Alog": np.ascontiguousarray(np.asarray(L0_attn_bw_Alog, np.float32)),
        "fw_D": np.ascontiguousarray(np.asarray(L0_attn_fw_D, np.float32)),
        "bw_D": np.ascontiguousarray(np.asarray(L0_attn_bw_D, np.float32)),
        "fuseT": tr(L0_attn_fuse_W),
        "fuse_b": np.ascontiguousarray(np.asarray(L0_attn_fuse_b, np.float32)),
        "w1T": tr(L0_ffn_w1),
        "w2T": tr(L0_ffn_w2),
    }
    in_maps = [dict(common, x=np.ascontiguousarray(x[b])) for b in range(B)]
    res = run_bass_kernel_spmd(nc, in_maps, core_ids=list(range(B)))
    return np.stack([r["out"] for r in res.results], axis=0)


# version 23 (final)
# speedup vs baseline: 1.1558x; 1.0645x over previous
import numpy as np
import ml_dtypes

import concourse.bass as bass
import concourse.mybir as mybir
from concourse.bass_utils import run_bass_kernel_spmd

F32 = mybir.dt.float32
BF16 = mybir.dt.bfloat16
I32 = mybir.dt.int32
AF = mybir.ActivationFunctionType
ALU = mybir.AluOpType
AX = mybir.AxisListType
BFNP = ml_dtypes.bfloat16

D_MODEL = 1024
D_INNER = 2048
D_STATE = 16
DT_RANK = 64
D_CONV = 4
NDHI = D_INNER // 128
NFC = D_MODEL // 128
NHID = 4 * D_MODEL
EPS_LN = 1e-5
EPS_Q = 1e-5


class Buf:
    __slots__ = ("name", "h", "sem", "last_w", "readers", "dma_count",
                 "is_dram", "guard")

    def __init__(self, name, h, sem=None, is_dram=False, guard=None):
        self.name = name
        self.h = h
        self.sem = sem
        self.last_w = None
        self.readers = []
        self.dma_count = 0
        self.is_dram = is_dram
        self.guard = guard


class Op:
    __slots__ = ("idx", "eng", "fn", "deps", "needs_inc", "sem_val", "is_dma",
                 "dma_buf", "dma_val", "tag")

    def __init__(self, idx, eng, fn, tag=""):
        self.idx = idx
        self.eng = eng
        self.fn = fn
        self.deps = []
        self.needs_inc = False
        self.sem_val = None
        self.is_dma = False
        self.dma_buf = None
        self.dma_val = None
        self.tag = tag


class Prog:

    def __init__(self, nc):
        self.nc = nc
        self.ops = []
        self.bufs = []
        self.cur_tag = ""
        self.group_stack = []
        self.arena_join = None
        self.dma_counts = {}

    def sbuf(self, name, shape, dtype, dma_target=False, sem=None):
        if self.group_stack:
            cm = self.nc.sbuf_tensor(name, list(shape), dtype)
            h = cm.__enter__()
            b = Buf(name, h, guard=cm)
            self.group_stack[-1][0].append(cm)
            self.group_stack[-1][1].append(b)
            b.last_w = self.arena_join
        else:
            h = self.nc.alloc_sbuf_tensor(name, list(shape), dtype)
            b = Buf(name, h)
        if dma_target:
            b.sem = sem or name
        self.bufs.append(b)
        return b

    def open_group(self):
        self.group_stack.append(([], []))

    def close_group(self):
        guards, gbufs = self.group_stack.pop()
        deps = set()
        for b in gbufs:
            if b.last_w is not None:
                deps.add(b.last_w)
            for r in b.readers:
                deps.add(r)
        join = Op(len(self.ops), "v", None, self.cur_tag + "_join")
        join.fn = lambda e, s: e.nop()
        join.deps = list(deps)
        self.ops.append(join)
        self.arena_join = join
        for cm in reversed(guards):
            cm.__exit__(None, None, None)

    def psum(self, name, shape, dtype=F32):
        h = self.nc.alloc_psum_tensor(name, list(shape), dtype)
        b = Buf(name, h)
        self.bufs.append(b)
        return b

    def dram(self, name, shape, dtype, kind="Internal"):
        h = self.nc.dram_tensor(name, list(shape), dtype, kind=kind)
        b = Buf(name, h, is_dram=True)
        b.sem = name + "_dsem"
        self.bufs.append(b)
        return b

    def _dep(self, op, on):
        if on is not None and on is not op and on not in op.deps:
            op.deps.append(on)

    def _track(self, op, reads, writes):
        for b in reads:
            self._dep(op, b.last_w)
            b.readers.append(op)
        for b in writes:
            for r in b.readers:
                self._dep(op, r)
            self._dep(op, b.last_w)
            b.last_w = op
            b.readers = []

    def op(self, eng, fn, r=(), w=()):
        o = Op(len(self.ops), eng, fn, self.cur_tag)
        self._track(o, r, w)
        self.ops.append(o)
        return o

    def dma(self, eng, dst, dst_ap_fn, src, src_ap_fn, slow=False):
        assert dst.sem is not None, f"dma target {dst.name} needs a sem"
        o = Op(len(self.ops), eng, None, self.cur_tag)
        o.is_dma = True
        o.dma_buf = dst
        cnt = self.dma_counts.get(dst.sem, 0) + 1
        self.dma_counts[dst.sem] = cnt
        dst.dma_count += 1
        o.dma_val = 16 * cnt

        def fn(e, sems, _d=dst_ap_fn, _s=src_ap_fn, _o=o, _slow=slow):
            e.dma_start(out=_d(), in_=_s(),
                        allow_slow_non_contiguous=_slow
                        ).then_inc(sems[_o.dma_buf.sem], 16)
        o.fn = fn
        self._track(o, [src], [dst])
        self.ops.append(o)
        return o

    def barrier_on(self, eng, bufs):
        o = Op(len(self.ops), eng, lambda e, sems: None, self.cur_tag)
        for x in bufs:
            self._dep(o, x.last_w)
        self.ops.append(o)
        return o

    def finalize(self):
        nc = self.nc
        for o in self.ops:
            for d in o.deps:
                if not d.is_dma:
                    d.needs_inc = True
        eng_count = {}
        for o in self.ops:
            if o.needs_inc and not o.is_dma:
                eng_count[o.eng] = eng_count.get(o.eng, 0) + 1
                o.sem_val = eng_count[o.eng]
        sem_names = [f"E{e}" for e in "tvsyg"]
        sem_names += sorted(self.dma_counts.keys())
        ctxs = []
        sems = {}
        for snm in sem_names:
            cm = nc.semaphore(snm)
            sems[snm] = cm.__enter__()
            ctxs.append(cm)
        streams = {e: [o for o in self.ops if o.eng == e] for e in "tvsyg"}

        has_wait = {}
        for e, ops in streams.items():
            waited = {}
            for o in ops:
                hw = False
                for d in o.deps:
                    if d.is_dma:
                        snm, val = d.dma_buf.sem, d.dma_val
                    else:
                        snm, val = "E" + d.eng, d.sem_val
                    if waited.get(snm, -1) < val:
                        waited[snm] = val
                        hw = True
                has_wait[o.idx] = hw
        bump = {e: {} for e in "tvsyg"}
        for e, ops in streams.items():
            i = 0
            n = len(ops)
            while i < n:
                o = ops[i]
                if o.is_dma or not o.needs_inc:
                    i += 1
                    continue
                j = i
                vals = [ops[j].sem_val]
                while (j + 1 < n and ops[j + 1].needs_inc
                       and not ops[j + 1].is_dma and not has_wait[ops[j + 1].idx]):
                    j += 1
                    vals.append(ops[j].sem_val)
                end_val = vals[-1]
                for v in vals:
                    bump[e][v] = end_val
                i = j + 1

        blk_cm = nc.Block()
        block = blk_cm.__enter__()

        def make_body(eng_key):
            ops = streams[eng_key]

            def body(eng):
                waited = {}
                for o in ops:
                    for d in o.deps:
                        if d.is_dma:
                            snm, val = d.dma_buf.sem, d.dma_val
                        else:
                            snm = "E" + d.eng
                            val = bump[d.eng].get(d.sem_val, d.sem_val)
                        if waited.get(snm, -1) < val:
                            eng.wait_ge(sems[snm], val)
                            waited[snm] = val
                    inst = o.fn(eng, sems)
                    if o.needs_inc and not o.is_dma:
                        assert inst is not None, f"op {o.idx} [{o.tag}] returned None"
                        inst.then_inc(sems["E" + o.eng], 1)
            return body

        block.tensor(make_body("t"))
        block.vector(make_body("v"))
        block.scalar(make_body("s"))
        block.sync(make_body("y"))
        block.gpsimd(make_body("g"))
        blk_cm.__exit__(None, None, None)
        for cm in reversed(ctxs):
            cm.__exit__(None, None, None)


def build_program(T=4096, SEG=512, CHUNK=1024):
    SEG = min(SEG, T)
    CHUNK = min(CHUNK, T)
    nseg = T // SEG
    nchunk = T // CHUNK
    NTT = SEG // 128
    nc = bass.Bass("TRN2", target_bir_lowering=False)
    P = Prog(nc)

    x_in = P.dram("x", [T, D_MODEL], F32, kind="ExternalInput")
    out_d = P.dram("out", [T, D_MODEL], F32, kind="ExternalOutput")
    wdecl = {}
    for pre in ("fw", "bw"):
        wdecl[pre + "_WinT"] = P.dram(pre + "_WinT", [D_MODEL, 2 * D_INNER], F32, "ExternalInput")
        wdecl[pre + "_WxT"] = P.dram(pre + "_WxT", [D_INNER, DT_RANK + 2 * D_STATE], F32, "ExternalInput")
        wdecl[pre + "_WdtT"] = P.dram(pre + "_WdtT", [DT_RANK, D_INNER], F32, "ExternalInput")
        wdecl[pre + "_Wout_nat"] = P.dram(pre + "_Wout_nat", [D_MODEL, D_INNER], F32, "ExternalInput")
        wdecl[pre + "_convw"] = P.dram(pre + "_convw", [D_INNER, D_CONV], F32, "ExternalInput")
        wdecl[pre + "_convb"] = P.dram(pre + "_convb", [D_INNER], F32, "ExternalInput")
        wdecl[pre + "_bdt"] = P.dram(pre + "_bdt", [D_INNER], F32, "ExternalInput")
        wdecl[pre + "_Alog"] = P.dram(pre + "_Alog", [D_INNER, D_STATE], F32, "ExternalInput")
        wdecl[pre + "_D"] = P.dram(pre + "_D", [D_INNER], F32, "ExternalInput")
    fuseT_d = P.dram("fuseT", [2 * D_MODEL, D_MODEL], F32, "ExternalInput")
    fuseb_d = P.dram("fuse_b", [D_MODEL], F32, "ExternalInput")
    w1T_d = P.dram("w1T", [D_MODEL, NHID], F32, "ExternalInput")
    w2T_d = P.dram("w2T", [NHID, D_MODEL], F32, "ExternalInput")

    ident_d = Buf("ident_d", nc.inline_tensor(np.eye(128).astype(BFNP), "ident_const"),
                  sem="ident_dsem", is_dram=True)
    P.bufs.append(ident_d)

    mid_q = [P.dram(f"mid{q}", [T, 512], F32) for q in range(4)]
    spill = {}
    for pre in ("fw", "bw"):
        spill[pre + "_bc"] = P.dram(f"sp_{pre}_bc", [32, T], BF16)
        spill[pre + "_win"] = P.dram(f"sp_{pre}_win", [NFC, 128, 2 * D_INNER], BF16)
        spill[pre + "_wc"] = P.dram(f"sp_{pre}_wc", [NDHI, 128, D_MODEL], BF16)

    ident = P.sbuf("ident", [128, 128], BF16, dma_target=True)
    Acoef = {p: P.sbuf(f"A_{p}", [128, NDHI, D_STATE], F32, dma_target=True)
             for p in ("fw", "bw")}
    convw_s = {p: P.sbuf(f"convw_{p}", [128, NDHI, D_CONV], F32, dma_target=True) for p in ("fw", "bw")}
    convb_s = {p: P.sbuf(f"convb_{p}", [128, NDHI], F32, dma_target=True) for p in ("fw", "bw")}
    bdt_s = {p: P.sbuf(f"bdt_{p}", [128, NDHI], F32, dma_target=True) for p in ("fw", "bw")}
    Dp_s = {p: P.sbuf(f"Dp_{p}", [128, NDHI], F32, dma_target=True) for p in ("fw", "bw")}
    onesf = P.sbuf("onesf", [1, 128], F32)
    onesb = P.sbuf("onesb", [1, 128], BF16)
    ones_col = P.sbuf("ones_col", [128, 1], F32)
    cols = P.sbuf("colscratch", [128, 48], F32)
    colsB = P.sbuf("colscratchB", [128, 16], F32)
    hstate = P.sbuf("hstate", [128, NDHI, D_STATE], F32)
    lnout = [None, None]
    xt = [P.sbuf(f"xt{i}", [128, D_MODEL], F32, dma_target=True) for i in range(2)]
    epsc = P.sbuf("epsc", [128, 1], F32)

    pA = P.psum("pA", [128, 512], F32)
    pB = P.psum("pB", [128, 512], F32)
    pC1 = P.psum("pC1", [128, 512], F32)
    pC2 = P.psum("pC2", [128, 512], F32)
    pD = P.psum("pD", [128, 512], F32)
    pW = P.psum("pW", [128, 512], F32)
    pRS = P.psum("pRS", [96, 512], F32)
    pT = P.psum("pT", [128, 128], BF16)

    P.cur_tag = "prep"
    P.dma("y", ident, lambda: ident.h[:], ident_d, lambda: ident_d.h.ap())
    P.op("g", lambda e, s: e.memset(onesf.h[:], 1.0), w=[onesf])
    P.op("g", lambda e, s: e.memset(onesb.h[:], 1.0), w=[onesb])
    P.op("g", lambda e, s: e.memset(ones_col.h[:], 1.0), w=[ones_col])
    P.op("g", lambda e, s: e.memset(epsc.h[:], EPS_LN), w=[epsc])
    for pre in ("fw", "bw"):
        P.dma("y", Acoef[pre], lambda pre=pre: Acoef[pre].h[:],
              wdecl[pre + "_Alog"],
              lambda pre=pre: wdecl[pre + "_Alog"].h.ap().rearrange("(a p) s -> p a s", p=128))
        P.op("s", lambda e, s, pre=pre: e.activation(
            Acoef[pre].h[:], Acoef[pre].h[:], AF.Exp),
            r=[Acoef[pre]], w=[Acoef[pre]])
        P.op("v", lambda e, s, pre=pre: e.tensor_single_scalar(
            Acoef[pre].h[:], Acoef[pre].h[:], -1.0, ALU.mult),
            r=[Acoef[pre]], w=[Acoef[pre]])
        P.dma("y", convw_s[pre], lambda pre=pre: convw_s[pre].h[:],
              wdecl[pre + "_convw"],
              lambda pre=pre: wdecl[pre + "_convw"].h.ap().rearrange("(a p) k -> p a k", p=128))
        P.dma("y", convb_s[pre], lambda pre=pre: convb_s[pre].h[:],
              wdecl[pre + "_convb"],
              lambda pre=pre: wdecl[pre + "_convb"].h.ap().rearrange("(a p) -> p a", p=128), slow=True)
        P.dma("y", bdt_s[pre], lambda pre=pre: bdt_s[pre].h[:],
              wdecl[pre + "_bdt"],
              lambda pre=pre: wdecl[pre + "_bdt"].h.ap().rearrange("(a p) -> p a", p=128), slow=True)
        P.dma("y", Dp_s[pre], lambda pre=pre: Dp_s[pre].h[:],
              wdecl[pre + "_D"],
              lambda pre=pre: wdecl[pre + "_D"].h.ap().rearrange("(a p) -> p a", p=128), slow=True)

    def load_weights_cast(stages, dst, dram_buf, rows, ncol, tag):
        P.cur_tag = tag
        if not isinstance(stages, list):
            stages = [stages]
        nchk = rows // 128
        scols = stages[0].h.ap().shape[1]
        i = 0
        for c in range(nchk):
            for j0 in range(0, ncol, scols):
                j1 = min(j0 + scols, ncol)
                stage = stages[i % len(stages)]
                i += 1
                P.dma("y", stage,
                      lambda stage=stage, j0=j0, j1=j1: stage.h[:, 0:j1 - j0],
                      dram_buf, lambda c=c, j0=j0, j1=j1: dram_buf.h.ap()
                      .rearrange("(a p) n -> p a n", p=128)[:, c, j0:j1])
                P.op("v", lambda e, s, stage=stage, c=c, j0=j0, j1=j1: e.tensor_copy(
                    dst.h[:, c, j0:j1], stage.h[:, 0:j1 - j0]), r=[stage], w=[dst])

    lnsq = [P.sbuf("lnsq0", [128, D_MODEL], F32)] * 2

    def ln_dma(seg, tt, tag):
        P.cur_tag = tag
        t0 = seg * SEG
        xtp = xt[tt % 2]
        P.dma("y", xtp, lambda xtp=xtp: xtp.h[:],
              x_in, lambda t0=t0, tt=tt: x_in.h.ap()
              [t0 + tt * 128:t0 + (tt + 1) * 128, :])

    def emit_ln_tile(seg, tt, tag, out=None, skip_dma=False):
        if not skip_dma:
            ln_dma(seg, tt, tag)
        P.cur_tag = tag
        if True:
            p = tt % 2
            c0 = 16 * p
            xtp, sq = xt[p], lnsq[p]
            lo = out if out is not None else lnout[p]
            P.op("v", lambda e, s, xtp=xtp, c0=c0: e.reduce_sum(
                cols.h[:, c0:c0 + 1], xtp.h[:], axis=AX.X), r=[xtp], w=[cols])
            P.op("s", lambda e, s, xtp=xtp, sq=sq: e.activation(
                sq.h[:], xtp.h[:], AF.Square), r=[xtp], w=[sq])
            P.op("v", lambda e, s, sq=sq, c0=c0: e.reduce_sum(
                cols.h[:, c0 + 1:c0 + 2], sq.h[:], axis=AX.X),
                r=[sq], w=[cols])
            P.op("v", lambda e, s, c0=c0: e.tensor_single_scalar(
                cols.h[:, c0 + 2:c0 + 3], cols.h[:, c0:c0 + 1], 1.0 / D_MODEL, ALU.mult),
                r=[cols], w=[cols])
            P.op("v", lambda e, s, c0=c0: e.tensor_mul(
                cols.h[:, c0 + 3:c0 + 4], cols.h[:, c0 + 2:c0 + 3],
                cols.h[:, c0 + 2:c0 + 3]), r=[cols], w=[cols])
            P.op("v", lambda e, s, c0=c0: e.tensor_single_scalar(
                cols.h[:, c0 + 4:c0 + 5], cols.h[:, c0 + 1:c0 + 2], 1.0 / D_MODEL,
                ALU.mult), r=[cols], w=[cols])
            P.op("v", lambda e, s, c0=c0: e.tensor_sub(
                cols.h[:, c0 + 5:c0 + 6], cols.h[:, c0 + 4:c0 + 5],
                cols.h[:, c0 + 3:c0 + 4]), r=[cols], w=[cols])
            P.op("s", lambda e, s, c0=c0: e.activation(
                cols.h[:, c0 + 6:c0 + 7], cols.h[:, c0 + 5:c0 + 6],
                AF.Ln, bias=epsc.h[:], scale=1.0), r=[cols, epsc], w=[cols])
            P.op("s", lambda e, s, c0=c0: e.activation(
                cols.h[:, c0 + 7:c0 + 8], cols.h[:, c0 + 6:c0 + 7],
                AF.Exp, bias=0.0, scale=-0.5), r=[cols], w=[cols])
            P.op("v", lambda e, s, c0=c0: e.tensor_single_scalar(
                cols.h[:, c0 + 8:c0 + 9], cols.h[:, c0 + 2:c0 + 3], -1.0, ALU.mult),
                r=[cols], w=[cols])
            P.op("v", lambda e, s, xtp=xtp, lo=lo, c0=c0: e.tensor_scalar(
                lo.h[:], xtp.h[:], cols.h[:, c0 + 8:c0 + 9], cols.h[:, c0 + 7:c0 + 8],
                ALU.add, ALU.mult), r=[xtp, cols], w=[lo])

    def emit_ln(seg, tag, per_tile):
        for tt in range(NTT):
            emit_ln_tile(seg, tt, tag)
            per_tile(tt)

    NHC = 4
    NG = NDHI // 2

    def sweep_ABC(pre):
        P.open_group()
        P.cur_tag = f"M{pre}_alloc"
        sfx = "_" + pre
        rev = (pre == "bw")
        nc_wx = DT_RANK + 2 * D_STATE

        wxT = P.sbuf("wxT" + sfx, [128, NDHI, nc_wx], BF16)
        wdtT = P.sbuf("wdtT" + sfx, [DT_RANK, D_INNER], BF16)
        fbias = P.sbuf("fbias", [1, D_MODEL], BF16) if rev else None

        P.open_group()
        pstg = [P.sbuf(f"pstg{sfx}{i}", [128, 2048], F32, dma_target=True,
                       sem=f"pstg{i}") for i in range(2)]
        wtmp = [P.sbuf(f"wtmp{sfx}{i}", [128, 2048], BF16) for i in range(2)]
        P.cur_tag = f"M{pre}_wprep"
        k = 0
        for kc in range(NFC):
            for j0 in (0, 2048):
                stg, wt = pstg[k % 2], wtmp[k % 2]
                k += 1
                P.dma("y", stg, lambda stg=stg: stg.h[:],
                      wdecl[pre + "_WinT"], lambda kc=kc, j0=j0, pre=pre:
                      wdecl[pre + "_WinT"].h.ap()
                      .rearrange("(a p) n -> p a n", p=128)[:, kc, j0:j0 + 2048])
                P.op("v", lambda e, s, wt=wt, stg=stg: e.tensor_copy(
                    wt.h[:], stg.h[:]), r=[stg], w=[wt])
                P.dma("g", spill[pre + "_win"],
                      lambda kc=kc, j0=j0, pre=pre:
                      spill[pre + "_win"].h.ap()[kc, :, j0:j0 + 2048],
                      wt, lambda wt=wt: wt.h[:])
        wnat = P.sbuf("wnat" + sfx, [128, NFC, D_INNER], BF16)
        fuT = P.sbuf("fuT" + sfx, [128, NFC, D_MODEL], BF16)
        r0 = 0 if pre == "fw" else D_MODEL
        for mk in range(NFC):
            stg = pstg[mk % 2]
            P.dma("y", stg, lambda stg=stg: stg.h[:],
                  wdecl[pre + "_Wout_nat"], lambda mk=mk, pre=pre:
                  wdecl[pre + "_Wout_nat"].h.ap()
                  .rearrange("(a p) n -> p a n", p=128)[:, mk, :])
            P.op("v", lambda e, s, stg=stg, mk=mk: e.tensor_copy(
                wnat.h[:, mk, :], stg.h[:]), r=[stg], w=[wnat])
            stg2 = pstg[(mk + 1) % 2]
            P.dma("y", stg2, lambda stg2=stg2: stg2.h[:, 0:D_MODEL],
                  fuseT_d, lambda mk=mk, r0=r0: fuseT_d.h.ap()[r0:r0 + D_MODEL, :]
                  .rearrange("(a p) n -> p a n", p=128)[:, mk, :])
            P.op("v", lambda e, s, stg2=stg2, mk=mk: e.tensor_copy(
                fuT.h[:, mk, :], stg2.h[:, 0:D_MODEL]), r=[stg2], w=[fuT])
        wc_t = [P.sbuf(f"wc_t{sfx}{i}", [128, 512], BF16) for i in range(2)]
        for dc in range(NDHI):
            for half in range(2):
                ps = [pA, pB][(dc * 2 + half) % 2]
                for mk in range(NFC):
                    P.op("t", lambda e, s, ps=ps, dc=dc, half=half, mk=mk: e.matmul(
                        ps.h[:, 0:512], wnat.h[:, mk, dc * 128:(dc + 1) * 128],
                        fuT.h[:, mk, half * 512:(half + 1) * 512],
                        start=(mk == 0), stop=(mk == NFC - 1)),
                        r=[wnat, fuT], w=[ps])
                wct = wc_t[half]
                P.op("v", lambda e, s, ps=ps, wct=wct: e.tensor_copy(
                    wct.h[:], ps.h[:, 0:512]), r=[ps], w=[wct])
                P.dma("g", spill[pre + "_wc"],
                      lambda dc=dc, half=half, pre=pre:
                      spill[pre + "_wc"].h.ap()[dc, :, half * 512:(half + 1) * 512],
                      wct, lambda wct=wct: wct.h[:])
        P.cur_tag = f"M{pre}_w"
        for c in range(NDHI):
            stg = pstg[c % 2]
            P.dma("y", stg, lambda stg=stg: stg.h[:, 0:nc_wx],
                  wdecl[pre + "_WxT"], lambda c=c, pre=pre: wdecl[pre + "_WxT"].h.ap()
                  .rearrange("(a p) n -> p a n", p=128)[:, c, :])
            P.op("v", lambda e, s, stg=stg, c=c: e.tensor_copy(
                wxT.h[:, c, :], stg.h[:, 0:nc_wx]), r=[stg], w=[wxT])
        P.dma("y", pstg[0], lambda: pstg[0].h[0:DT_RANK, 0:D_INNER],
              wdecl[pre + "_WdtT"], lambda pre=pre: wdecl[pre + "_WdtT"].h.ap())
        P.op("v", lambda e, s: e.tensor_copy(
            wdtT.h[:], pstg[0].h[0:DT_RANK, 0:D_INNER]), r=[pstg[0]], w=[wdtT])
        if rev:
            P.dma("y", pstg[1], lambda: pstg[1].h[0:1, 0:D_MODEL],
                  fuseb_d, lambda: fuseb_d.h.ap().rearrange("(a n) -> a n", a=1))
            P.op("v", lambda e, s: e.tensor_copy(
                fbias.h[:], pstg[1].h[0:1, 0:D_MODEL]), r=[pstg[1]], w=[fbias])
        P.close_group()

        P.cur_tag = f"M{pre}_alloc"
        winsl = [P.sbuf(f"winsl{sfx}{i}", [128, NFC, 256], BF16, dma_target=True,
                        sem=f"winsl{i}") for i in range(2)]
        wq = [P.sbuf(f"wq{sfx}{i}", [128, NDHI, 256], BF16, dma_target=True,
                     sem=f"wq{i}") for i in range(2)]
        xnT = P.sbuf("xnT" + sfx, [128, NFC, SEG], BF16)
        xnbt = [P.sbuf("xnbt" + sfx, [128, D_MODEL], BF16)] * 2
        xsb = [P.sbuf(f"xsb{sfx}{d}", [128, SEG + 3], BF16) for d in range(NDHI)]
        ub = [P.sbuf(f"ub{sfx}{d}", [128, SEG], BF16) for d in range(NDHI)]
        szb = [P.sbuf(f"szb{sfx}{d}", [128, SEG], BF16) for d in range(NDHI)]
        y2sg = [[P.sbuf(f"y2s{sfx}{pp}_{d}", [128, SEG], BF16)
                 for d in range(NDHI)] for pp in range(2)]
        BrepS = [P.sbuf(f"Brep{sfx}{st}", [128, SEG], BF16, dma_target=True,
                        sem=f"Brep{st}") for st in range(D_STATE)]
        CrepS = [P.sbuf(f"Crep{sfx}{st}", [128, SEG], BF16, dma_target=True,
                        sem=f"Crep{st}") for st in range(D_STATE)]
        dtbc = [P.sbuf(f"dtbc{sfx}{i}", [DT_RANK + 32, SEG], BF16)
                for i in range(2)]
        dtcA = P.sbuf("dtcA" + sfx, [128, SEG], F32)
        dtcB = P.sbuf("dtcB" + sfx, [128, SEG], F32)
        convf = dtcB
        d1g = [P.sbuf(f"d1g{sfx}{i}", [128, 2, SEG], BF16) for i in range(2)]
        dug = [P.sbuf(f"dug{sfx}{i}", [128, 2, SEG], BF16) for i in range(2)]
        a_sl = [P.sbuf(f"a_sl{sfx}{i}", [128, SEG], BF16) for i in range(2)]
        b_sl = [P.sbuf(f"b_sl{sfx}{i}", [128, SEG], BF16) for i in range(2)]
        h_sl = [P.sbuf(f"h_sl{sfx}{i}", [128, SEG], BF16) for i in range(2)]
        hc_sl = [P.sbuf(f"hc_sl{sfx}{i}", [128, SEG], BF16) for i in range(NHC)]
        tq_f32 = dtcA
        ydt = [a_sl[0]] * 2
        mid_ts = [P.sbuf(f"mid_ts{sfx}{i}", [128, 512], F32, dma_target=True,
                         sem=f"mid_ts{i}") for i in range(2)]

        if pre == "fw":
            xs_off = 3
            taps = [(0, 0), (1, 1), (2, 2), (3, 3)]
        else:
            xs_off = 0
            taps = [(3, 0), (2, 1), (1, 2), (0, 3)]
        carry_slc = slice(0, 3) if pre == "fw" else slice(SEG, SEG + 3)
        P.op("g", lambda e, s: e.memset(hstate.h[:], 0.0), w=[hstate])
        segs = list(range(nseg)) if pre == "fw" else list(range(nseg - 1, -1, -1))

        def load_slab(j):
            sl = winsl[j % 2]
            P.dma("y", sl, lambda sl=sl: sl.h[:],
                  spill[pre + "_win"], lambda j=j, pre=pre:
                  spill[pre + "_win"].h.ap()[:, :, 256 * j:256 * j + 256]
                  .rearrange("a p n -> p a n"))

        def ln_chunk(seg, tt):
            def emit():
                p = tt % 2
                xb = xnbt[p]
                emit_ln_tile(seg, tt, f"M{pre}_ln{seg}", out=xb, skip_dma=True)
                if tt + 2 < NTT:
                    ln_dma(seg, tt + 2, f"M{pre}_ln{seg}")
                for fc in range(NFC):
                    P.op("t", lambda e, s, xb=xb, fc=fc: e.transpose(
                        pT.h[:], xb.h[:, fc * 128:(fc + 1) * 128], ident.h[:]),
                        r=[xb, ident], w=[pT])
                    P.op("v", lambda e, s, tt=tt, fc=fc: e.tensor_copy(
                        xnT.h[:, fc, tt * 128:(tt + 1) * 128], pT.h[:]),
                        r=[pT], w=[xnT])
            return emit

        def x_chunk(seg, dhi, first):
            ps = [pA, pB][dhi % 2]
            xb = xsb[dhi]

            def emit_mm():
                P.cur_tag = f"M{pre}_xz{seg}"
                sl = winsl[(dhi // 2) % 2]
                jo = dhi % 2
                for kc in range(NFC):
                    P.op("t", lambda e, s, sl=sl, jo=jo, kc=kc: e.matmul(
                        ps.h[:, 0:SEG], sl.h[:, kc, jo * 128:(jo + 1) * 128],
                        xnT.h[:, kc, :], start=(kc == 0), stop=(kc == NFC - 1)),
                        r=[sl, xnT], w=[ps])
                if not first:
                    if pre == "fw":
                        P.op("v", lambda e, s: e.tensor_copy(
                            xb.h[:, 0:3], xb.h[:, SEG:SEG + 3]), r=[xb], w=[xb])
                    else:
                        P.op("v", lambda e, s: e.tensor_copy(
                            xb.h[:, SEG:SEG + 3], xb.h[:, 0:3]), r=[xb], w=[xb])
                if dhi % 2 == 1 and dhi // 2 + 2 < NFC:
                    load_slab(dhi // 2 + 2)

            def emit_ev():
                P.cur_tag = f"M{pre}_xz{seg}"
                P.op("v", lambda e, s: e.tensor_copy(
                    xb.h[:, xs_off:xs_off + SEG], ps.h[:, 0:SEG]), r=[ps], w=[xb])
            return emit_mm, emit_ev

        def conv_taps(seg, dhi):
            P.cur_tag = f"M{pre}_cv{seg}"
            xb = xsb[dhi]
            k0, o0 = taps[0]
            P.op("v", lambda e, s, xb=xb, dhi=dhi, k0=k0, o0=o0: e.tensor_scalar(
                convf.h[:], xb.h[:, o0:o0 + SEG],
                convw_s[pre].h[:, dhi, k0:k0 + 1], None, ALU.mult),
                r=[xb, convw_s[pre]], w=[convf])
            for kk, off in taps[1:]:
                P.op("v", lambda e, s, xb=xb, dhi=dhi, kk=kk, off=off:
                     e.scalar_tensor_tensor(
                         convf.h[:], xb.h[:, off:off + SEG],
                         convw_s[pre].h[:, dhi, kk:kk + 1], convf.h[:],
                         ALU.mult, ALU.add),
                     r=[xb, convw_s[pre], convf], w=[convf])

        def conv_silu(seg, dhi):
            P.cur_tag = f"M{pre}_cv{seg}"
            u = ub[dhi]
            P.op("s", lambda e, s, u=u, dhi=dhi: e.activation(
                u.h[:], convf.h[:], AF.Silu,
                bias=convb_s[pre].h[:, dhi:dhi + 1], scale=1.0),
                r=[convf, convb_s[pre]], w=[u])

        def xdbl_mm(seg, dhi):
            P.cur_tag = f"M{pre}_cv{seg}"
            u = ub[dhi]
            P.op("t", lambda e, s, u=u, dhi=dhi: e.matmul(
                pRS.h[:, 0:SEG], wxT.h[:, dhi, :], u.h[:],
                start=(dhi == 0), stop=(dhi == NDHI - 1)),
                r=[wxT, u], w=[pRS])

        def cxd_pair(seg, g):
            for dhi in (2 * g, 2 * g + 1):
                conv_taps(seg, dhi)
                conv_silu(seg, dhi)
                xdbl_mm(seg, dhi)

        def z_pair(seg, p, via_wq=False):
            P.cur_tag = f"M{pre}_z{seg}"
            if via_wq:
                sl, ro = wq[(p % 4) // 2], (p % 2) * NFC
            else:
                sl, ro = winsl[p % 2], 0
            for jo in range(2):
                zi = 2 * p + jo
                ps = [pA, pB][jo]
                for kc in range(NFC):
                    P.op("t", lambda e, s, ps=ps, sl=sl, ro=ro, jo=jo, kc=kc:
                         e.matmul(
                             ps.h[:, 0:SEG], sl.h[:, ro + kc, jo * 128:(jo + 1) * 128],
                             xnT.h[:, kc, :], start=(kc == 0), stop=(kc == NFC - 1)),
                         r=[sl, xnT], w=[ps])
                P.op("s", lambda e, s, ps=ps, zi=zi: e.activation(
                    szb[zi].h[:], ps.h[:, 0:SEG], AF.Silu),
                    r=[ps], w=[szb[zi]])

        def z_load_wq(p):
            b, ro = wq[(p % 4) // 2], (p % 2) * NFC
            j = NFC + p
            P.dma("y", b, lambda b=b, ro=ro: b.h[:, ro:ro + NFC, :],
                  spill[pre + "_win"], lambda j=j, pre=pre:
                  spill[pre + "_win"].h.ap()[:, :, 256 * j:256 * j + 256]
                  .rearrange("a p n -> p a n"))

        def chains_wdt(i, g, j):
            P.cur_tag = f"M{pre}_dt"
            dhi = 2 * g + j
            dpb = dtbc[i % 2]
            P.op("t", lambda e, s, dhi=dhi, dpb=dpb: e.matmul(
                pD.h[:, 0:SEG], wdtT.h[:, dhi * 128:(dhi + 1) * 128],
                dpb.h[0:DT_RANK, :], start=True, stop=True),
                r=[wdtT, dpb], w=[pD])

        def chains_sc(i, g, j):
            P.cur_tag = f"M{pre}_dt"
            dhi = 2 * g + j
            P.op("s", lambda e, s, dhi=dhi: e.activation(
                dtcA.h[:], pD.h[:, 0:SEG], AF.Identity,
                bias=bdt_s[pre].h[:, dhi:dhi + 1], scale=1.0),
                r=[pD, bdt_s[pre]], w=[dtcA])
            P.op("s", lambda e, s: e.activation(
                dtcB.h[:], dtcA.h[:], AF.Abs), r=[dtcA], w=[dtcB])
            P.op("s", lambda e, s: e.activation(
                dtcB.h[:], dtcB.h[:], AF.Exp, bias=0.0, scale=-1.0),
                r=[dtcB], w=[dtcB])
            P.op("s", lambda e, s: e.activation(
                dtcB.h[:], dtcB.h[:], AF.Ln, bias=1.0, scale=1.0),
                r=[dtcB], w=[dtcB])
            P.op("s", lambda e, s: e.activation(
                dtcA.h[:], dtcA.h[:], AF.Relu), r=[dtcA], w=[dtcA])

        def chains_v(i, g, j):
            P.cur_tag = f"M{pre}_dt"
            gp = g % 2
            dhi = 2 * g + j
            P.op("v", lambda e, s, gp=gp, j=j: e.tensor_add(
                d1g[gp].h[:, j, :], dtcA.h[:], dtcB.h[:]),
                r=[dtcA, dtcB], w=[d1g[gp]])
            P.op("v", lambda e, s, gp=gp, j=j, dhi=dhi: e.tensor_mul(
                dug[gp].h[:, j, :], d1g[gp].h[:, j, :], ub[dhi].h[:]),
                r=[d1g[gp], ub[dhi]], w=[dug[gp]])

        def chains(i, g):
            for j in range(2):
                chains_wdt(i, g, j)
                chains_sc(i, g, j)
                chains_v(i, g, j)

        def boundary(i, seg):
            P.cur_tag = f"M{pre}_bc{seg}"
            t0 = seg * SEG
            bc = dtbc[i % 2]
            P.op("v", lambda e, s, bc=bc: e.tensor_copy(
                bc.h[:], pRS.h[0:DT_RANK + 32, 0:SEG]), r=[pRS], w=[bc])
            P.dma("g", spill[pre + "_bc"],
                  lambda t0=t0, pre=pre: spill[pre + "_bc"].h.ap()[:, t0:t0 + SEG],
                  bc, lambda bc=bc: bc.h[DT_RANK:DT_RANK + 32, :])
            for st in range(D_STATE):
                P.dma("y", BrepS[st], lambda st=st: BrepS[st].h[:],
                      spill[pre + "_bc"], lambda st=st, t0=t0, pre=pre:
                      spill[pre + "_bc"].h.ap()[st:st + 1, t0:t0 + SEG]
                      .broadcast_to([128, SEG]))
                P.dma("y", CrepS[st], lambda st=st: CrepS[st].h[:],
                      spill[pre + "_bc"], lambda st=st, t0=t0, pre=pre:
                      spill[pre + "_bc"].h.ap()[16 + st:17 + st, t0:t0 + SEG]
                      .broadcast_to([128, SEG]))
            chains(i, 0)

        def y2f(i, g):
            P.cur_tag = f"M{pre}_y2"
            for j, dhi in enumerate((2 * g, 2 * g + 1)):
                pc = [pC1, pC2][j]
                yb = ydt[j]
                y2 = y2sg[i % 2][dhi]
                P.op("s", lambda e, s, yb=yb, dhi=dhi: e.activation(
                    yb.h[:], ub[dhi].h[:], AF.Identity,
                    bias=0.0, scale=Dp_s[pre].h[:, dhi:dhi + 1]),
                    r=[ub[dhi], Dp_s[pre]], w=[yb])
                P.op("v", lambda e, s, pc=pc, yb=yb: e.tensor_add(
                    tq_f32.h[:], pc.h[:, 0:SEG], yb.h[:]), r=[pc, yb], w=[tq_f32])
                P.op("v", lambda e, s, y2=y2, dhi=dhi: e.tensor_mul(
                    y2.h[:], tq_f32.h[:], szb[dhi].h[:]),
                    r=[tq_f32, szb[dhi]], w=[y2])

        def scan_group(i, seg, g, side, nxt):
            P.cur_tag = f"M{pre}_sc{seg}"
            gp = g % 2
            for st in range(D_STATE):
                for j, dhi in enumerate((2 * g, 2 * g + 1)):
                    ip = (2 * st + j) % 2
                    hci = (2 * st + j) % NHC
                    pc = [pC1, pC2][j]
                    asl, bsl, hsl, hcs = a_sl[ip], b_sl[ip], h_sl[ip], hc_sl[hci]
                    P.op("s", lambda e, s, asl=asl, gp=gp, j=j, dhi=dhi, st=st:
                         e.activation(
                             asl.h[:], d1g[gp].h[:, j, :], AF.Exp,
                             bias=0.0, scale=Acoef[pre].h[:, dhi, st:st + 1]),
                         r=[d1g[gp], Acoef[pre]], w=[asl])
                    P.op("v", lambda e, s, bsl=bsl, gp=gp, j=j, st=st: e.tensor_mul(
                        bsl.h[:], dug[gp].h[:, j, :], BrepS[st].h[:]),
                        r=[dug[gp], BrepS[st]], w=[bsl])
                    if not rev:
                        P.op("v", lambda e, s, asl=asl, bsl=bsl, hsl=hsl,
                             dhi=dhi, st=st: e.tensor_tensor_scan(
                                 hsl.h[:], asl.h[:], bsl.h[:],
                                 hstate.h[:, dhi, st:st + 1], ALU.mult, ALU.add),
                             r=[asl, bsl, hstate], w=[hsl])
                        P.op("v", lambda e, s, hsl=hsl, dhi=dhi, st=st: e.tensor_copy(
                            hstate.h[:, dhi, st:st + 1], hsl.h[:, SEG - 1:SEG]),
                            r=[hsl], w=[hstate])
                    else:
                        P.op("v", lambda e, s, asl=asl, bsl=bsl, hsl=hsl,
                             dhi=dhi, st=st: e.tensor_tensor_scan(
                                 hsl.h[:, ::-1], asl.h[:, ::-1], bsl.h[:, ::-1],
                                 hstate.h[:, dhi, st:st + 1], ALU.mult, ALU.add),
                             r=[asl, bsl, hstate], w=[hsl])
                        P.op("v", lambda e, s, hsl=hsl, dhi=dhi, st=st: e.tensor_copy(
                            hstate.h[:, dhi, st:st + 1], hsl.h[:, 0:1]),
                            r=[hsl], w=[hstate])
                    P.op("v", lambda e, s, hsl=hsl, hcs=hcs, st=st: e.tensor_mul(
                        hcs.h[:], hsl.h[:], CrepS[st].h[:]),
                        r=[hsl, CrepS[st]], w=[hcs])
                    P.op("t", lambda e, s, pc=pc, hcs=hcs, st=st: e.matmul(
                        pc.h[:, 0:SEG], ident.h[:], hcs.h[:],
                        start=(st == 0), stop=(st == D_STATE - 1)),
                        r=[ident, hcs], w=[pc])
                if st == 1:
                    if g + 1 < NG:
                        chains_wdt(i, g + 1, 0)
                    if nxt is not None and g >= 1:
                        conv_silu(nxt, 2 * (g - 1))
                        conv_taps(nxt, 2 * (g - 1) + 1)
                elif st == 2 and nxt is not None and g >= 1:
                    conv_silu(nxt, 2 * (g - 1) + 1)
                    xdbl_mm(nxt, 2 * (g - 1))
                    xdbl_mm(nxt, 2 * (g - 1) + 1)
                elif st == 5 and g + 1 < NG:
                    chains_sc(i, g + 1, 0)
                elif st == 8 and g + 1 < NG:
                    chains_wdt(i, g + 1, 1)
                elif st == 9 and g + 1 < NG:
                    chains_v(i, g + 1, 0)
                elif st == 12 and g + 1 < NG:
                    chains_sc(i, g + 1, 1)
                if (st % 2 == 1 or st in (4, 6, 10)) and side:
                    side.pop(0)()
                P.cur_tag = f"M{pre}_sc{seg}"
            if g + 1 < NG:
                chains_v(i, g + 1, 1)
            y2f(i, g)

        def wcomb_chunks(ip, seg):
            t0p = seg * SEG
            y2p = y2sg[ip % 2]
            out = []
            for h in range(2):
                for tt in range(NTT):
                    def c1(h=h, tt=tt):
                        P.cur_tag = f"M{pre}_wc{seg}"
                        if tt == 0:
                            for q in range(2):
                                wqb = wq[q]
                                P.dma("y", wqb, lambda wqb=wqb: wqb.h[:],
                                      spill[pre + "_wc"],
                                      lambda h=h, q=q, pre=pre:
                                      spill[pre + "_wc"].h.ap()
                                      [:, :, 512 * h + 256 * q:
                                       512 * h + 256 * q + 256]
                                      .rearrange("a p n -> p a n"))
                        for dc in range(NDHI):
                            P.op("t", lambda e, s, dc=dc, tt=tt: e.matmul(
                                pW.h[:, 0:256],
                                y2p[dc].h[:, tt * 128:(tt + 1) * 128],
                                wq[0].h[:, dc, :], start=(dc == 0),
                                stop=(dc == NDHI - 1 and not rev)),
                                r=[y2p[dc], wq[0]], w=[pW])
                        if rev:
                            P.op("t", lambda e, s, h=h: e.matmul(
                                pW.h[:, 0:256], onesb.h[:],
                                fbias.h[0:1, 512 * h:512 * h + 256],
                                start=False, stop=True),
                                r=[onesb, fbias], w=[pW])

                    def c2m(h=h, tt=tt):
                        P.cur_tag = f"M{pre}_wc{seg}"
                        for dc in range(NDHI):
                            P.op("t", lambda e, s, dc=dc, tt=tt: e.matmul(
                                pW.h[:, 256:512],
                                y2p[dc].h[:, tt * 128:(tt + 1) * 128],
                                wq[1].h[:, dc, :], start=(dc == 0),
                                stop=(dc == NDHI - 1 and not rev)),
                                r=[y2p[dc], wq[1]], w=[pW])
                        if rev:
                            P.op("t", lambda e, s, h=h: e.matmul(
                                pW.h[:, 256:512], onesb.h[:],
                                fbias.h[0:1, 512 * h + 256:512 * h + 512],
                                start=False, stop=True),
                                r=[onesb, fbias], w=[pW])
                        mts = mid_ts[tt % 2]
                        if pre == "fw":
                            P.dma("y", mts, lambda mts=mts: mts.h[:],
                                  x_in, lambda t0p=t0p, tt=tt, h=h:
                                  x_in.h.ap()[t0p + tt * 128:t0p + (tt + 1) * 128,
                                              512 * h:512 * h + 512])
                        else:
                            mq = mid_q[2 * h + tt % 2]
                            P.dma("y", mts, lambda mts=mts: mts.h[:],
                                  mq, lambda t0p=t0p, tt=tt, mq=mq:
                                  mq.h.ap()
                                  [t0p + tt * 128:t0p + (tt + 1) * 128, :])

                    def c2v(h=h, tt=tt):
                        P.cur_tag = f"M{pre}_wc{seg}"
                        mts = mid_ts[tt % 2]
                        P.op("v", lambda e, s, mts=mts: e.tensor_add(
                            mts.h[:], mts.h[:], pW.h[:, 0:512]),
                            r=[pW, mts], w=[mts])
                        mq = mid_q[2 * h + tt % 2]
                        P.dma("g", mq,
                              lambda t0p=t0p, tt=tt, mq=mq: mq.h.ap()
                              [t0p + tt * 128:t0p + (tt + 1) * 128, :],
                              mts, lambda mts=mts: mts.h[:])
                    out.append(c1)
                    out.append(c2m)
                    out.append(c2v)
            return out

        for i, seg in enumerate(segs):
            nxt = segs[i + 1] if i + 1 < nseg else None
            prv = segs[i - 1] if i >= 1 else None

            if i == 0:
                P.cur_tag = f"M{pre}_pro"
                for dhi in range(NDHI):
                    P.op("g", lambda e, s, dhi=dhi: e.memset(
                        xsb[dhi].h[:, carry_slc], 0.0), w=[xsb[dhi]])
                load_slab(0)
                load_slab(1)
                ln_dma(seg, 0, f"M{pre}_pro")
                ln_dma(seg, 1, f"M{pre}_pro")
                for tt in range(NTT):
                    ln_chunk(seg, tt)()
                for dhi in range(NDHI):
                    mm, ev = x_chunk(seg, dhi, True)
                    mm()
                    ev()
                for g in range(NG):
                    cxd_pair(seg, g)
                for p in range(NG):
                    load_slab(NFC + p)
                    z_pair(seg, p)
                if nxt is not None:
                    ln_dma(nxt, 0, f"M{pre}_pro")
                    ln_dma(nxt, 1, f"M{pre}_pro")
            boundary(i, seg)

            side = []
            if nxt is not None:
                side = [ln_chunk(nxt, tt) for tt in range(NTT)]
                side.append(lambda: (load_slab(0), load_slab(1)))
                pend = []
                for k in range(NDHI):
                    mm, ev = x_chunk(nxt, k, False)
                    side.append(mm)
                    pend.append(ev)
                    if k >= 1:
                        side.append(pend.pop(0))
                side += pend
            w_ = wcomb_chunks(i - 1, prv) if prv is not None else []
            merged = []
            wi = 0
            for idx, p_ in enumerate(side):
                merged.append(p_)
                if idx >= 12 and idx % 3 == 2 and wi < len(w_):
                    merged.append(w_[wi])
                    wi += 1
            side = merged + w_[wi:]

            for g in range(NG):
                scan_group(i, seg, g, side, nxt)
                if nxt is not None:
                    conv_taps(nxt, 2 * g)
                    if g == 5:
                        for p in range(4):
                            z_load_wq(p)
                    elif g == 6:
                        if i + 2 < nseg:
                            ln_dma(segs[i + 2], 0, f"M{pre}_ln")
                            ln_dma(segs[i + 2], 1, f"M{pre}_ln")
                        for p in range(4):
                            z_pair(nxt, p, via_wq=True)
                        for p in range(4, 8):
                            z_load_wq(p)
                    elif g == 7:
                        for p in range(4, 8):
                            z_pair(nxt, p, via_wq=True)
            if nxt is not None:
                conv_silu(nxt, NDHI - 2)
                conv_taps(nxt, NDHI - 1)
                conv_silu(nxt, NDHI - 1)
                xdbl_mm(nxt, NDHI - 2)
                xdbl_mm(nxt, NDHI - 1)
            while side:
                side.pop(0)()
        for c in wcomb_chunks(nseg - 1, segs[-1]):
            c()
        P.close_group()

    def sweep_D():
        P.open_group()
        P.cur_tag = "D_alloc"
        lnout[0] = P.sbuf("lnout0", [128, D_MODEL], F32)
        lnout[1] = P.sbuf("lnout1", [128, D_MODEL], F32)
        w1t = P.sbuf("w1t", [128, NFC, NHID], BF16)
        w2t = P.sbuf("w2t", [128, NHID // 128, D_MODEL], BF16)
        xq1 = [P.sbuf("xq1_0", [128, 512], BF16)] * 2
        xq2 = [P.sbuf(f"xq2_{i}", [128, 512], BF16) for i in range(2)]
        i32q = P.sbuf("i32q", [128, 512], I32)
        rnd_a = P.sbuf("rnd_a", [128, 512], F32)
        rnd_b = P.sbuf("rnd_b", [128, 512], F32)

        def emit_round(src_fn, src_bufs, out_buf, out_fn, n):
            assert n <= 512
            P.op("v", lambda e, s: e.tensor_copy(i32q.h[:, 0:n], src_fn()),
                 r=list(src_bufs), w=[i32q])
            P.op("v", lambda e, s: e.tensor_single_scalar(
                rnd_a.h[:, 0:n], i32q.h[:, 0:n], -1.0, ALU.add),
                r=[i32q], w=[rnd_a])
            P.op("v", lambda e, s: e.tensor_sub(
                rnd_b.h[:, 0:n], src_fn(), rnd_a.h[:, 0:n]),
                r=list(src_bufs) + [rnd_a], w=[rnd_b])
            P.op("v", lambda e, s: e.scalar_tensor_tensor(
                rnd_a.h[:, 0:n], rnd_b.h[:, 0:n], 1.5, rnd_a.h[:, 0:n],
                ALU.is_ge, ALU.add), r=[rnd_a, rnd_b], w=[rnd_a])
            P.op("v", lambda e, s: e.scalar_tensor_tensor(
                out_fn(), rnd_b.h[:, 0:n], 0.5, rnd_a.h[:, 0:n],
                ALU.is_gt, ALU.add), r=[rnd_a, rnd_b], w=[out_buf])

        def ternarize(stages, dst, dram_buf, nrow, ncol, s_dst_col, tag):
            P.cur_tag = tag
            nchk = nrow // 128
            scols = stages[0].h.ap().shape[1]
            P.op("g", lambda e, s: e.memset(colsB.h[:, 0:1], 0.0), w=[colsB])
            i = 0
            for c in range(nchk):
                for j0 in range(0, ncol, scols):
                    j1 = min(j0 + scols, ncol)
                    stage = stages[i % 2]
                    i += 1
                    P.dma("y", stage,
                          lambda stage=stage, j0=j0, j1=j1: stage.h[:, 0:j1 - j0],
                          dram_buf, lambda c=c, j0=j0, j1=j1: dram_buf.h.ap()
                          .rearrange("(a p) n -> p a n", p=128)[:, c, j0:j1])
                    P.op("v", lambda e, s, stage=stage, j0=j0, j1=j1: e.tensor_reduce(
                        colsB.h[:, 1:2], stage.h[:, 0:j1 - j0], axis=AX.X,
                        op=ALU.add, apply_absolute_value=True),
                        r=[stage], w=[colsB])
                    P.op("v", lambda e, s: e.tensor_add(
                        colsB.h[:, 0:1], colsB.h[:, 0:1], colsB.h[:, 1:2]),
                        r=[colsB], w=[colsB])
            P.op("t", lambda e, s: e.matmul(pD.h[0:1, 0:1], colsB.h[:, 0:1],
                                            ones_col.h[:], start=True, stop=True),
                 r=[colsB, ones_col], w=[pD])
            P.op("v", lambda e, s: e.tensor_copy(colsB.h[0:1, 2:3], pD.h[0:1, 0:1]),
                 r=[pD], w=[colsB])
            P.op("t", lambda e, s: e.matmul(pD.h[:, 0:1], onesf.h[:],
                                            colsB.h[0:1, 2:3], start=True, stop=True),
                 r=[onesf, colsB], w=[pD])
            nel = float(nrow) * ncol
            P.op("v", lambda e, s, nel=nel: e.tensor_single_scalar(
                colsB.h[:, 3:4], pD.h[:, 0:1], 0.5 / nel, ALU.mult),
                r=[pD], w=[colsB])
            P.op("v", lambda e, s: e.tensor_single_scalar(
                colsB.h[:, 4:5], colsB.h[:, 3:4], -1.0, ALU.mult), r=[colsB], w=[colsB])
            i = 0
            for c in range(nchk):
                for j0 in range(0, ncol, scols):
                    j1 = min(j0 + scols, ncol)
                    stage = stages[i % 2]
                    i += 1
                    P.dma("y", stage,
                          lambda stage=stage, j0=j0, j1=j1: stage.h[:, 0:j1 - j0],
                          dram_buf, lambda c=c, j0=j0, j1=j1: dram_buf.h.ap()
                          .rearrange("(a p) n -> p a n", p=128)[:, c, j0:j1])
                    for q0 in range(0, j1 - j0, 512):
                        qn = min(512, j1 - j0 - q0)
                        P.op("v", lambda e, s, stage=stage, q0=q0, qn=qn: e.tensor_scalar(
                            rnd_a.h[:, 0:qn], stage.h[:, q0:q0 + qn],
                            colsB.h[:, 3:4], None, ALU.is_ge),
                            r=[stage, colsB], w=[rnd_a])
                        P.op("v", lambda e, s, stage=stage, q0=q0, qn=qn: e.tensor_scalar(
                            rnd_b.h[:, 0:qn], stage.h[:, q0:q0 + qn],
                            colsB.h[:, 4:5], None, ALU.is_le),
                            r=[stage, colsB], w=[rnd_b])
                        P.op("v", lambda e, s, c=c, j0=j0, q0=q0, qn=qn: e.tensor_sub(
                            dst.h[:, c, j0 + q0:j0 + q0 + qn],
                            rnd_a.h[:, 0:qn], rnd_b.h[:, 0:qn]),
                            r=[rnd_a, rnd_b], w=[dst])
            P.op("v", lambda e, s, s_dst_col=s_dst_col: e.tensor_single_scalar(
                cols.h[:, s_dst_col:s_dst_col + 1], colsB.h[:, 3:4], 2.0, ALU.mult),
                r=[colsB], w=[cols])

        P.open_group()
        wstgD = [P.sbuf(f"wstgD{i}", [128, 2048], F32, dma_target=True)
                 for i in range(2)]
        ternarize(wstgD, w1t, w1T_d, D_MODEL, NHID, 9, "D_w1")
        ternarize(wstgD, w2t, w2T_d, NHID, D_MODEL, 10, "D_w2")
        P.close_group()
        xq1T = [P.sbuf(f"xq1T{i}", [128, NFC, 128], BF16) for i in range(2)]
        xq2T = [P.sbuf(f"xq2T{i}", [128, NHID // 128, 128], BF16) for i in range(2)]
        hbuf = P.sbuf("hbuf", [128, NHID], F32)
        out_t = [P.sbuf(f"out_t{i}", [128, 512], F32, dma_target=True)
                 for i in range(4)]
        qv = rnd_b
        gcol = [P.sbuf(f"gcol{i}", [128, 1], F32) for i in range(2)]
        ocol = [P.sbuf(f"ocol{i}", [128, 1], F32) for i in range(2)]
        outw = []
        for wi in range(4):
            b = Buf(f"outw{wi}", out_d.h, sem=f"outw{wi}", is_dram=True)
            P.bufs.append(b)
            outw.append(b)

        NT_ALL = T // 128

        def stage_A(t):
            P.cur_tag = f"D_A{t}"
            dp = t % 2
            x1T = xq1T[dp]
            emit_ln_tile(0, t, "D_ln", skip_dma=True)
            if t + 2 < NT_ALL:
                ln_dma(0, t + 2, "D_ln")
            lo = lnout[dp]
            P.op("v", lambda e, s, lo=lo: e.reduce_max(
                cols.h[:, 43:44], lo.h[:], axis=AX.X,
                apply_absolute_value=True), r=[lo], w=[cols])
            P.op("v", lambda e, s: e.reciprocal(cols.h[:, 44:45], cols.h[:, 43:44]),
                 r=[cols], w=[cols])
            P.op("v", lambda e, s: e.tensor_single_scalar(
                cols.h[:, 44:45], cols.h[:, 44:45], 127.0, ALU.mult),
                r=[cols], w=[cols])
            P.op("v", lambda e, s: e.reciprocal(cols.h[:, 45:46], cols.h[:, 44:45]),
                 r=[cols], w=[cols])
            for qi in range(2):
                q0 = qi * 512
                xq1b = xq1[qi]
                P.op("v", lambda e, s, lo=lo, q0=q0: e.tensor_scalar(
                    qv.h[:], lo.h[:, q0:q0 + 512], cols.h[:, 44:45],
                    None, ALU.mult), r=[lo, cols], w=[qv])
                emit_round(lambda: qv.h[:], [qv], xq1b,
                           lambda xq1b=xq1b: xq1b.h[:], 512)
                for j in range(4):
                    fc = qi * 4 + j
                    P.op("t", lambda e, s, xq1b=xq1b, j=j: e.transpose(
                        pT.h[:], xq1b.h[:, j * 128:(j + 1) * 128], ident.h[:]),
                        r=[xq1b, ident], w=[pT])
                    P.op("v", lambda e, s, x1T=x1T, fc=fc: e.tensor_copy(
                        x1T.h[:, fc, :], pT.h[:]), r=[pT], w=[x1T])
            gc = gcol[dp]
            P.op("v", lambda e, s, gc=gc: e.tensor_mul(
                gc.h[:], cols.h[:, 9:10], cols.h[:, 45:46]),
                r=[cols], w=[gc])

        def stage_L(t):
            P.cur_tag = f"D_L{t}"
            for h in range(2):
                ot = out_t[2 * h + t % 2]
                mq = mid_q[2 * h + t % 2]
                P.dma("y", ot, lambda ot=ot: ot.h[:],
                      mq, lambda t=t, mq=mq: mq.h.ap()
                      [t * 128:(t + 1) * 128, :])

        def stage_B(t):
            P.cur_tag = f"D_B{t}"
            x1T = xq1T[t % 2]
            for hc in range(NHID // 512):
                ps = [pA, pB][hc % 2]
                for fc in range(NFC):
                    P.op("t", lambda e, s, ps=ps, x1T=x1T, hc=hc, fc=fc: e.matmul(
                        ps.h[:, 0:512], x1T.h[:, fc, :],
                        w1t.h[:, fc, hc * 512:(hc + 1) * 512],
                        start=(fc == 0), stop=(fc == NFC - 1)),
                        r=[x1T, w1t], w=[ps])
                gc = gcol[t % 2]
                P.op("s", lambda e, s, ps=ps, hc=hc, gc=gc: e.activation(
                    hbuf.h[:, hc * 512:(hc + 1) * 512], ps.h[:, 0:512],
                    AF.Gelu, bias=0.0, scale=gc.h[:]),
                    r=[ps, gc], w=[hbuf])

        def stage_C(t):
            P.cur_tag = f"D_C{t}"
            x2T = xq2T[t % 2]
            P.op("v", lambda e, s: e.reduce_max(
                cols.h[:, 43:44], hbuf.h[:], axis=AX.X,
                apply_absolute_value=True), r=[hbuf], w=[cols])
            P.op("v", lambda e, s: e.tensor_single_scalar(
                cols.h[:, 43:44], cols.h[:, 43:44], EPS_Q, ALU.max),
                r=[cols], w=[cols])
            P.op("v", lambda e, s: e.reciprocal(cols.h[:, 44:45], cols.h[:, 43:44]),
                 r=[cols], w=[cols])
            P.op("v", lambda e, s: e.tensor_single_scalar(
                cols.h[:, 44:45], cols.h[:, 44:45], 127.0, ALU.mult),
                r=[cols], w=[cols])
            P.op("v", lambda e, s: e.reciprocal(cols.h[:, 45:46], cols.h[:, 44:45]),
                 r=[cols], w=[cols])
            for qi in range(8):
                q0 = qi * 512
                xq2b = xq2[qi % 2]
                P.op("v", lambda e, s, q0=q0: e.tensor_scalar(
                    qv.h[:], hbuf.h[:, q0:q0 + 512],
                    cols.h[:, 44:45], None, ALU.mult),
                    r=[hbuf, cols], w=[qv])
                emit_round(lambda: qv.h[:], [qv], xq2b,
                           lambda xq2b=xq2b: xq2b.h[:], 512)
                for j in range(4):
                    k = qi * 4 + j
                    P.op("t", lambda e, s, xq2b=xq2b, j=j: e.transpose(
                        pT.h[:], xq2b.h[:, j * 128:(j + 1) * 128], ident.h[:]),
                        r=[xq2b, ident], w=[pT])
                    P.op("v", lambda e, s, x2T=x2T, k=k: e.tensor_copy(
                        x2T.h[:, k, :], pT.h[:]), r=[pT], w=[x2T])
            oc = ocol[t % 2]
            P.op("v", lambda e, s, oc=oc: e.tensor_mul(
                oc.h[:], cols.h[:, 10:11], cols.h[:, 45:46]),
                r=[cols], w=[oc])

        def stage_Dm(t):
            P.cur_tag = f"D_D{t}"
            x2T = xq2T[t % 2]
            for half in range(2):
                pc = [pC1, pC2][half]
                for k in range(NHID // 128):
                    P.op("t", lambda e, s, pc=pc, x2T=x2T, k=k, half=half: e.matmul(
                        pc.h[:], x2T.h[:, k, :],
                        w2t.h[:, k, half * 512:(half + 1) * 512],
                        start=(k == 0), stop=(k == NHID // 128 - 1)),
                        r=[x2T, w2t], w=[pc])

        def stage_E(t):
            P.cur_tag = f"D_E{t}"
            for half in range(2):
                pc = [pC1, pC2][half]
                ot = out_t[2 * half + t % 2]
                oc = ocol[t % 2]
                P.op("v", lambda e, s, pc=pc, ot=ot, oc=oc: e.scalar_tensor_tensor(
                    ot.h[:], pc.h[:], oc.h[:], ot.h[:],
                    ALU.mult, ALU.add),
                    r=[pc, oc, ot], w=[ot])
                ow = outw[half * 2 + t % 2]
                P.dma("gy"[half], ow,
                      lambda t=t, half=half: out_d.h.ap()
                      [t * 128:(t + 1) * 128, half * 512:(half + 1) * 512],
                      ot, lambda ot=ot: ot.h[:])

        ln_dma(0, 0, "D_ln")
        ln_dma(0, 1, "D_ln")
        stage_A(0)
        if NT_ALL > 1:
            stage_A(1)
        stage_B(0)
        if NT_ALL > 2:
            stage_A(2)
        stage_C(0)
        if NT_ALL > 1:
            stage_B(1)
        stage_L(0)
        for t in range(NT_ALL):
            stage_Dm(t)
            if t + 1 < NT_ALL:
                stage_L(t + 1)
            if t + 3 < NT_ALL:
                stage_A(t + 3)
            if t + 1 < NT_ALL:
                stage_C(t + 1)
            if t + 2 < NT_ALL:
                stage_B(t + 2)
            stage_E(t)
        P.close_group()

    sweep_ABC("fw")
    sweep_ABC("bw")
    sweep_D()

    P.cur_tag = "final"
    P.barrier_on("y", [out_d])
    P.finalize()
    return nc



def kernel(L0_attn_x, L0_attn_fw_Win, L0_attn_fw_convw, L0_attn_fw_convb,
           L0_attn_fw_Wx, L0_attn_fw_Wdt, L0_attn_fw_bdt, L0_attn_fw_Alog,
           L0_attn_fw_D, L0_attn_fw_Wout, L0_attn_bw_Win, L0_attn_bw_convw,
           L0_attn_bw_convb, L0_attn_bw_Wx, L0_attn_bw_Wdt, L0_attn_bw_bdt,
           L0_attn_bw_Alog, L0_attn_bw_D, L0_attn_bw_Wout, L0_attn_fuse_W,
           L0_attn_fuse_b, L0_ffn_w1, L0_ffn_w2):
    x = np.ascontiguousarray(np.asarray(L0_attn_x, dtype=np.float32))
    B, T, D = x.shape
    nc = build_program(T=T)

    def tr(a):
        return np.ascontiguousarray(np.asarray(a, dtype=np.float32).T)

    common = {
        "fw_WinT": tr(L0_attn_fw_Win), "bw_WinT": tr(L0_attn_bw_Win),
        "fw_WxT": tr(L0_attn_fw_Wx), "bw_WxT": tr(L0_attn_bw_Wx),
        "fw_WdtT": tr(L0_attn_fw_Wdt), "bw_WdtT": tr(L0_attn_bw_Wdt),
        "fw_Wout_nat": np.ascontiguousarray(np.asarray(L0_attn_fw_Wout, np.float32)),
        "bw_Wout_nat": np.ascontiguousarray(np.asarray(L0_attn_bw_Wout, np.float32)),
        "fw_convw": np.ascontiguousarray(np.asarray(L0_attn_fw_convw, np.float32).reshape(D_INNER, D_CONV)),
        "bw_convw": np.ascontiguousarray(np.asarray(L0_attn_bw_convw, np.float32).reshape(D_INNER, D_CONV)),
        "fw_convb": np.ascontiguousarray(np.asarray(L0_attn_fw_convb, np.float32)),
        "bw_convb": np.ascontiguousarray(np.asarray(L0_attn_bw_convb, np.float32)),
        "fw_bdt": np.ascontiguousarray(np.asarray(L0_attn_fw_bdt, np.float32)),
        "bw_bdt": np.ascontiguousarray(np.asarray(L0_attn_bw_bdt, np.float32)),
        "fw_Alog": np.ascontiguousarray(np.asarray(L0_attn_fw_Alog, np.float32)),
        "bw_Alog": np.ascontiguousarray(np.asarray(L0_attn_bw_Alog, np.float32)),
        "fw_D": np.ascontiguousarray(np.asarray(L0_attn_fw_D, np.float32)),
        "bw_D": np.ascontiguousarray(np.asarray(L0_attn_bw_D, np.float32)),
        "fuseT": tr(L0_attn_fuse_W),
        "fuse_b": np.ascontiguousarray(np.asarray(L0_attn_fuse_b, np.float32)),
        "w1T": tr(L0_ffn_w1),
        "w2T": tr(L0_ffn_w2),
    }
    in_maps = [dict(common, x=np.ascontiguousarray(x[b])) for b in range(B)]
    res = run_bass_kernel_spmd(nc, in_maps, core_ids=list(range(B)))
    return np.stack([r["out"] for r in res.results], axis=0)
